# version 11
# baseline (speedup 1.0000x reference)
import math
import numpy as np
import ml_dtypes
from contextlib import ExitStack

import concourse.bass as bass
import concourse.mybir as mybir
from concourse.bass_utils import run_bass_kernel_spmd

F32 = mybir.dt.float32
BF16 = mybir.dt.bfloat16
I32 = mybir.dt.int32
AF = mybir.ActivationFunctionType
ALU = mybir.AluOpType
AX = mybir.AxisListType

D = 2048
DCN = 16
QL = 1536
QLC = 12
KL = 512
KLC = 4
NH = 16
FF = 5632
FFC = 44
TB = 512
SCALE = float(np.float32(1.0) / np.float32(np.sqrt(np.float32(192.0))))
EPS_LN = 1e-5
EPS_RMS_ATTN = 1e-6
EPS_RMS_FFN = 1e-5
INV127 = 1.0 / 127.0

ENG_SEM = {"t": "pe", "v": "vs", "s": "ss", "g": "gs"}


class Plan:

    def __init__(self, nc):
        self.nc = nc
        self.pend = []
        self.sems = {}
        self.cnt = {}
        self.wm = {k: {} for k in "tvsgq"}
        for name in ("pe", "vs", "ss", "gs", "cc"):
            self._sem(name)

    def _sem(self, name):
        if name not in self.sems:
            self.sems[name] = self.nc.alloc_semaphore(name)
            self.cnt[name] = 0
        return self.sems[name]

    def _waits(self, eng, deps):
        need = {}
        for d in deps:
            if d is None:
                continue
            s, v = d
            if self.wm[eng].get(s, 0) >= v:
                continue
            if s not in ENG_SEM.values() and s != "cc":
                assert v == self.cnt[s], (
                    f"unsound DMA wait: {s} at {v}, latest {self.cnt[s]}")
            if need.get(s, 0) < v:
                need[s] = v
        items = sorted(need.items())
        attach = None
        if items:
            attach = items[-1]
            for s, v in items[:-1]:
                sem = self.sems[s]
                self.pend.append((eng, lambda e, sem=sem, v=v: e.wait_ge(sem, v)))
                self.wm[eng][s] = v
            self.wm[eng][attach[0]] = attach[1]
        return attach

    def op(self, eng, fn, deps=(), inc="auto", inc_val=1):
        attach = self._waits(eng, deps)
        sem_name = ENG_SEM.get(eng) if inc == "auto" else inc
        tok = None
        if sem_name:
            self._sem(sem_name)
            self.cnt[sem_name] += inc_val
            tok = (sem_name, self.cnt[sem_name])
        a_sem = self.sems[attach[0]] if attach else None
        a_val = attach[1] if attach else 0
        m_sem = self.sems[sem_name] if sem_name else None

        def run(e, fn=fn, a_sem=a_sem, a_val=a_val, m_sem=m_sem, iv=inc_val):
            inst = fn(e)
            if a_sem is not None:
                inst._wait_ge(a_sem, a_val)
            if m_sem is not None:
                inst.then_inc(m_sem, iv)
        self.pend.append((eng, run))
        return tok

    def latest(self, stream):
        if stream not in self.cnt or self.cnt[stream] == 0:
            return None
        return (stream, self.cnt[stream])

    def dma(self, eng, stream, out, in_, deps=()):
        self._sem(stream)
        return self.op(eng, lambda e: e.dma_start(out=out, in_=in_), deps,
                       inc=stream, inc_val=16)

    def barrier(self):
        toks = [(s, self.cnt[s]) for s in self.sems if self.cnt[s] > 0]
        for eng in "tvsgq":
            need = {}
            for s, v in toks:
                if s == ENG_SEM.get(eng):
                    continue
                if self.wm[eng].get(s, 0) >= v:
                    continue
                need[s] = v
            for s, v in sorted(need.items()):
                sem = self.sems[s]
                self.pend.append((eng, lambda e, sem=sem, v=v: e.wait_ge(sem, v)))
                self.wm[eng][s] = v

    def flush(self):
        if not self.pend:
            return
        pend, self.pend = self.pend, []
        with self.nc.Block() as block:
            for key, meth in (("t", block.tensor), ("v", block.vector),
                              ("s", block.scalar), ("g", block.gpsimd),
                              ("q", block.sync)):
                ops = [f for (k, f) in pend if k == key]
                if ops:
                    meth(lambda e, ops=ops: [f(e) for f in ops])


def build_nc(T=4096, n_cores=8):
    NTB = T // TB
    MYT = T // 2
    MYB = MYT // TB
    nc = bass.Bass("TRN2", target_bir_lowering=False)

    xT_h = nc.dram_tensor("xT", [DCN, 128, T], F32, kind="ExternalInput")
    xn_h = nc.dram_tensor("x_nat", [T, D], F32, kind="ExternalInput")
    xo_h = nc.dram_tensor("x_own", [MYT, D], F32, kind="ExternalInput")
    xoT_h = nc.dram_tensor("x_ownT", [DCN, 128, MYT], F32, kind="ExternalInput")
    dmask_h = nc.dram_tensor("dmask", [8, 128, 512], BF16, kind="ExternalInput")
    qaw_h = nc.dram_tensor("qaw", [128, DCN, QL], F32, kind="ExternalInput")
    qbw_h = nc.dram_tensor("qbw", [128, QLC, 3072], F32, kind="ExternalInput")
    kvaw_h = nc.dram_tensor("kvaw", [128, DCN, QL], F32, kind="ExternalInput")
    kvbw_h = nc.dram_tensor("kvbw", [128, KLC, 4096], F32, kind="ExternalInput")
    ow_h = nc.dram_tensor("ow", [DCN, 128, DCN, 128], F32, kind="ExternalInput")
    w1_h = nc.dram_tensor("w1w", [FFC, 128, DCN, 128], F32, kind="ExternalInput")
    w3_h = nc.dram_tensor("w3w", [FFC, 128, DCN, 128], F32, kind="ExternalInput")
    w2_h = nc.dram_tensor("w2w", [DCN, 128, FFC, 128], F32, kind="ExternalInput")
    gq_h = nc.dram_tensor("gq", [QLC, 128], F32, kind="ExternalInput")
    gkv_h = nc.dram_tensor("gkv", [KLC, 128], F32, kind="ExternalInput")
    gf_h = nc.dram_tensor("gf", [FFC, 128], F32, kind="ExternalInput")

    y_h = nc.dram_tensor("y", [D, MYT], F32, kind="ExternalOutput")

    qT_s = nc.dram_tensor("qT_s", [3072, MYT], BF16)
    kTn_s = nc.dram_tensor("kTn_s", [2048, T], BF16)
    kTr_s = nc.dram_tensor("kTr_s", [1024, T], BF16)
    v_s = nc.dram_tensor("v_s", [NH, 128, T // 128, 128], BF16)
    attnT_s = nc.dram_tensor("attnT_s", [2048, MYT], BF16)
    y1_s = nc.dram_tensor("y1_s", [DCN, 128, MYT], F32)
    gw_s = nc.dram_tensor("gw_s", [FFC, 128, MYT], BF16)

    eye_f_h = nc.inline_tensor(np.eye(128, dtype=np.float32), "eye_f")
    eye_b_h = nc.inline_tensor(np.eye(128).astype(ml_dtypes.bfloat16), "eye_b")
    onescol_h = nc.inline_tensor(np.ones((128, 1)).astype(ml_dtypes.bfloat16), "onescol")
    onesrow_h = nc.inline_tensor(np.ones((1, 128), dtype=np.float32), "onesrow")

    p = Plan(nc)

    with ExitStack() as PERS:
        ep = PERS.enter_context
        ident_f = ep(nc.sbuf_tensor("ident_f", [128, 128], F32))
        ident_b = ep(nc.sbuf_tensor("ident_b", [128, 128], BF16))
        ones_st = ep(nc.sbuf_tensor("ones_st", [128, 1], BF16))
        ones_bc = ep(nc.sbuf_tensor("ones_bc", [1, 128], F32))
        mask_sb = ep(nc.sbuf_tensor("mask_sb", [128, 8 * 512], BF16))
        gq_sb = ep(nc.sbuf_tensor("gq_sb", [128, QLC], F32))
        gkv_sb = ep(nc.sbuf_tensor("gkv_sb", [128, KLC], F32))
        gf_sb = ep(nc.sbuf_tensor("gf_sb", [128, FFC], F32))

        p.dma("q", "ini", ident_f[:], eye_f_h.ap())
        p.dma("q", "ini", ident_b[:], eye_b_h.ap())
        p.dma("q", "ini", ones_st[:], onescol_h.ap())
        p.dma("q", "ini", ones_bc[:], onesrow_h.ap())
        p.dma("q", "ini", mask_sb[:].rearrange("p (m c) -> p m c", m=8),
              dmask_h.ap().rearrange("m p c -> p m c"))

        with ExitStack() as P0:
            e0 = P0.enter_context
            tp_ps = e0(nc.psum_tensor("tp0_ps", [128, 128], F32))
            gtmp = e0(nc.sbuf_tensor("gtmp", [FFC, 128], F32))
            p.barrier()
            gtoks = []
            prev = None
            for gh, gsb, n in ((gq_h, gq_sb, QLC), (gkv_h, gkv_sb, KLC),
                               (gf_h, gf_sb, FFC)):
                tok = p.dma("q", f"gld{n}", gtmp[0:n, :], gh.ap(), deps=[prev])
                tt = p.op("t", lambda e, n=n: e.transpose(
                    tp_ps[:, 0:n], gtmp[0:n, :], ident_f[0:n, 0:n]),
                    deps=[tok, prev])
                cp = p.op("v", lambda e, gsb=gsb, n=n: e.tensor_copy(
                    gsb[:, 0:n], tp_ps[:, 0:n]), deps=[tt])
                gtoks.append(cp)
                prev = cp
            g_q, g_kv, g_f = gtoks
            p.barrier()
            p.flush()

        with ExitStack() as P1AO:
            eo = P1AO.enter_context
            qlatN = eo(nc.sbuf_tensor("qlatN", [128, QLC * MYT], BF16))
            qn_tok = [[None] * QLC for _ in range(MYB)]

            with ExitStack() as P1A:
                e1 = P1A.enter_context
                tp_ps = e1(nc.psum_tensor("tp1_ps", [128, 128], F32))
                bc_ps = e1(nc.psum_tensor("bc1_ps", [128, 512], F32))
                mm_ps = [e1(nc.psum_tensor(f"mm1_ps{i}", [128, 512], F32)) for i in range(2)]
                stat_ps = e1(nc.psum_tensor("st1_ps", [1, 512], F32))
                qaw_sb = e1(nc.sbuf_tensor("qaw_sb", [128, DCN * QL], BF16))
                xn_t = [e1(nc.sbuf_tensor(f"xn_t{i}", [128, D], F32)) for i in range(2)]
                x2_t = e1(nc.sbuf_tensor("x2_t", [128, D], F32))
                st_sc = e1(nc.sbuf_tensor("st_sc", [128, 8], F32))
                rows = e1(nc.sbuf_tensor("rows", [1, 1024], F32))
                mu_bc = e1(nc.sbuf_tensor("mu_bc", [128, 512], F32))
                rs_bc = e1(nc.sbuf_tensor("rs_bc", [128, 512], F32))
                xt_c = [e1(nc.sbuf_tensor(f"xt_c{i}", [128, 512], F32)) for i in range(3)]
                tmp_c = e1(nc.sbuf_tensor("tmp_c", [128, 512], F32))
                lnT = [e1(nc.sbuf_tensor("lnT0", [128, DCN * 512], BF16))] * 2
                qlat = e1(nc.sbuf_tensor("qlat", [128, QLC * 512], F32))
                sq_c = [e1(nc.sbuf_tensor(f"sq_c{i}", [128, 512], BF16)) for i in range(2)]
                qvar = e1(nc.sbuf_tensor("qvar", [1, 512], F32))
                qrs = e1(nc.sbuf_tensor("qrs", [1, 512], F32))
                rsq_bc = e1(nc.sbuf_tensor("rsq_bc", [128, 512], F32))

                p.dma("g", "w1a", qaw_sb[:].rearrange("p (c f) -> p c f", c=DCN),
                      qaw_h.ap())
                t_w1a = p.latest("w1a")

                state = dict(xn_war=[None, None], x2_war=None, xt_war=[None] * 3,
                             tp_war=g_f, bc_war=None, mu_war=None, rows_war=None)
                ln_war = [None, None]
                ln_tok = [[None] * DCN, [None] * DCN]
                mm_war = [None, None]
                stat_war = [None, None]

                def ln_block(tb, par, c0, src2d_h, srcT_h, tag, st):
                    for ti in range(4):
                        tpar = ti & 1
                        gt = 4 * tb + ti
                        ld = p.dma("q", f"xn{tpar}{tag}", xn_t[tpar][:],
                                   src2d_h.ap()[gt * 128:(gt + 1) * 128, :],
                                   deps=(st["xn_war"][tpar] or []))
                        s1 = p.op("v", lambda e, b=tpar, ti=ti: e.reduce_sum(
                            st_sc[:, ti:ti + 1], xn_t[b][:], AX.X), deps=[ld])
                        x2 = p.op("g", lambda e, b=tpar: e.tensor_tensor(
                            x2_t[:], xn_t[b][:], xn_t[b][:], ALU.mult),
                            deps=[ld, st["x2_war"]])
                        s2 = p.op("v", lambda e: e.reduce_sum(
                            st_sc[:, 4:5], x2_t[:], AX.X), deps=[x2])
                        st["x2_war"] = s2
                        st["xn_war"][tpar] = [s1, x2]
                        mu = p.op("v", lambda e, ti=ti: e.tensor_scalar_mul(
                            st_sc[:, ti:ti + 1], st_sc[:, ti:ti + 1], 1.0 / D),
                            deps=[s1])
                        m2 = p.op("v", lambda e, ti=ti: e.tensor_tensor(
                            st_sc[:, 6:7], st_sc[:, ti:ti + 1], st_sc[:, ti:ti + 1],
                            ALU.mult), deps=[mu])
                        var = p.op("v", lambda e: e.tensor_scalar(
                            st_sc[:, 4:5], st_sc[:, 4:5], 1.0 / D, EPS_LN,
                            ALU.mult, ALU.add), deps=[s2])
                        var2 = p.op("v", lambda e: e.tensor_tensor(
                            st_sc[:, 5:6], st_sc[:, 4:5], st_sc[:, 6:7], ALU.subtract),
                            deps=[var, m2])
                        lg = p.op("s", lambda e: e.activation(
                            st_sc[:, 7:8], st_sc[:, 5:6], AF.Ln), deps=[var2])
                        rstd = p.op("s", lambda e: e.activation(
                            st_sc[:, 5:6], st_sc[:, 7:8], AF.Exp, scale=-0.5),
                            deps=[lg])
                        tt1 = p.op("t", lambda e, ti=ti: e.transpose(
                            tp_ps[0:1, :], st_sc[:, ti:ti + 1], ident_f[:]),
                            deps=[mu, m2, st["tp_war"]])
                        cp1 = p.op("v", lambda e, ti=ti: e.tensor_copy(
                            rows[0:1, ti * 128:(ti + 1) * 128], tp_ps[0:1, :]),
                            deps=[tt1, st["rows_war"]])
                        tt2 = p.op("t", lambda e: e.transpose(
                            tp_ps[0:1, :], st_sc[:, 5:6], ident_f[:]),
                            deps=[rstd, cp1])
                        cp2 = p.op("v", lambda e, ti=ti: e.tensor_copy(
                            rows[0:1, 512 + ti * 128:512 + (ti + 1) * 128],
                            tp_ps[0:1, :]), deps=[tt2])
                        st["tp_war"] = cp2
                    b1 = p.op("t", lambda e: e.matmul(bc_ps[:], ones_bc[:],
                                                      rows[0:1, 0:512], start=True,
                                                      stop=True, skip_group_check=True),
                              deps=[st["tp_war"], st["bc_war"]])
                    mub = p.op("v", lambda e: e.tensor_copy(mu_bc[:], bc_ps[:]),
                               deps=[b1, st["mu_war"]])
                    b2 = p.op("t", lambda e: e.matmul(bc_ps[:], ones_bc[:],
                                                      rows[0:1, 512:1024], start=True,
                                                      stop=True, skip_group_check=True),
                              deps=[mub])
                    rsb = p.op("v", lambda e: e.tensor_copy(rs_bc[:], bc_ps[:]),
                               deps=[b2, st["mu_war"]])
                    st["bc_war"] = rsb
                    st["rows_war"] = b2
                    t2 = st.get("tmp_war")
                    for dc in range(DCN):
                        cr = dc % 3
                        xt = p.dma("q", f"xt{cr}{tag}", xt_c[cr][:],
                                   srcT_h.ap()[dc, :, c0:c0 + TB],
                                   deps=[st["xt_war"][cr]])
                        t1 = p.op("v", lambda e, cr=cr: e.tensor_tensor(
                            tmp_c[:], xt_c[cr][:], mu_bc[:], ALU.subtract),
                            deps=[xt, mub, t2])
                        st["xt_war"][cr] = t1
                        t2 = p.op("v", lambda e, dc=dc, par=par: e.tensor_tensor(
                            lnT[par][:, dc * 512:(dc + 1) * 512], tmp_c[:], rs_bc[:],
                            ALU.mult), deps=[t1, rsb, ln_war[par]])
                        ln_tok[par][dc] = t2
                    st["mu_war"] = t2
                    st["tmp_war"] = t2

                for tb in range(MYB):
                    par = tb & 1
                    c0 = tb * TB
                    ln_block(tb, par, c0, xo_h, xoT_h, "a", state)
                    qlat_tok = [None] * QLC
                    sq_war = stat_war
                    last_stop = None
                    stop = None
                    for fc in range(QLC):
                        bank = fc & 1
                        for dc in range(DCN):
                            dd = [ln_tok[par][dc], t_w1a]
                            if dc == 0:
                                dd.append(mm_war[bank])
                            stop = p.op("t", lambda e, fc=fc, dc=dc, bank=bank, par=par:
                                        e.matmul(mm_ps[bank][:],
                                                 qaw_sb[:, dc * QL + fc * 128:dc * QL + (fc + 1) * 128],
                                                 lnT[par][:, dc * 512:(dc + 1) * 512],
                                                 start=(dc == 0), stop=(dc == DCN - 1),
                                                 skip_group_check=True), deps=dd)
                        ev = p.op("v", lambda e, fc=fc, bank=bank: e.tensor_copy(
                            qlat[:, fc * 512:(fc + 1) * 512], mm_ps[bank][:]),
                            deps=[stop])
                        qlat_tok[fc] = ev
                        mm_war[bank] = ev
                        sp = fc & 1
                        sqt = p.op("v", lambda e, fc=fc, sp=sp: e.tensor_tensor(
                            sq_c[sp][:], qlat[:, fc * 512:(fc + 1) * 512],
                            qlat[:, fc * 512:(fc + 1) * 512], ALU.mult),
                            deps=[ev, sq_war[sp]])
                        last_stop = p.op("t", lambda e, fc=fc, sp=sp: e.matmul(
                            stat_ps[:], ones_st[:], sq_c[sp][:],
                            start=(fc == 0), stop=(fc == QLC - 1),
                            skip_group_check=True), deps=[sqt])
                        sq_war[sp] = last_stop
                    stat_war = sq_war
                    qv = p.op("v", lambda e: e.tensor_scalar(
                        qvar[:], stat_ps[:], 1.0 / QL, EPS_RMS_ATTN, ALU.mult,
                        ALU.add), deps=[last_stop])
                    ql1 = p.op("s", lambda e: e.activation(qrs[:], qvar[:], AF.Ln),
                               deps=[qv])
                    ql2 = p.op("s", lambda e: e.activation(qrs[:], qrs[:], AF.Exp,
                                                           scale=-0.5), deps=[ql1])
                    b3 = p.op("t", lambda e: e.matmul(bc_ps[:], ones_bc[:], qrs[:],
                                                      start=True, stop=True,
                                                      skip_group_check=True),
                              deps=[ql2, state["bc_war"]])
                    rqb = p.op("v", lambda e: e.tensor_copy(rsq_bc[:], bc_ps[:]),
                               deps=[b3])
                    state["bc_war"] = rqb
                    for fc in range(QLC):
                        qn_tok[tb][fc] = p.op("v", lambda e, fc=fc, tb=tb, c0=c0:
                                              e.scalar_tensor_tensor(
                                                  qlatN[:, fc * MYT + c0:fc * MYT + c0 + 512],
                                                  qlat[:, fc * 512:(fc + 1) * 512],
                                                  gq_sb[:, fc:fc + 1], rsq_bc[:],
                                                  ALU.mult, ALU.mult),
                                              deps=[qlat_tok[fc], rqb, g_q])
                    ln_war[par] = stop
                p.barrier()
                p.flush()

            with ExitStack() as P1A2:
                e1 = P1A2.enter_context
                mm_ps = [e1(nc.psum_tensor(f"mm1b_ps{i}", [128, 512], F32))
                         for i in range(2)]
                qbw_sb = e1(nc.sbuf_tensor("qbw_sb", [128, QLC * 3072], BF16))
                qstage = [e1(nc.sbuf_tensor(f"qstage{i}", [128, 512], BF16))
                          for i in range(2)]
                p.dma("g", "w1a2", qbw_sb[:].rearrange("p (c f) -> p c f", c=QLC),
                      qbw_h.ap())
                t_w1a2 = p.latest("w1a2")
                qs_war = [None, None]
                mm_war = [None, None]
                for tb in range(MYB):
                    c0 = tb * TB
                    for fc2 in range(24):
                        bank = fc2 & 1
                        stop = None
                        for kc in range(QLC):
                            dd = [qn_tok[tb][kc], t_w1a2]
                            if kc == 0:
                                dd.append(mm_war[bank])
                            stop = p.op("t", lambda e, fc2=fc2, kc=kc, bank=bank, c0=c0:
                                        e.matmul(mm_ps[bank][:],
                                                 qbw_sb[:, kc * 3072 + fc2 * 128:kc * 3072 + (fc2 + 1) * 128],
                                                 qlatN[:, kc * MYT + c0:kc * MYT + c0 + 512],
                                                 start=(kc == 0), stop=(kc == QLC - 1),
                                                 skip_group_check=True), deps=dd)
                        ev = p.op("v", lambda e, bank=bank: e.tensor_scalar_mul(
                            qstage[bank][:], mm_ps[bank][:], SCALE),
                            deps=[stop, qs_war[bank]])
                        mm_war[bank] = ev
                        qs_war[bank] = p.dma("q", f"qst{bank}",
                                             qT_s.ap()[fc2 * 128:(fc2 + 1) * 128,
                                                       c0:c0 + TB],
                                             qstage[bank][:], deps=[ev])
                p.barrier()
                p.flush()

        with ExitStack() as P1B:
            e1 = P1B.enter_context
            tp_ps = e1(nc.psum_tensor("tp2_ps", [128, 128], F32))
            bc_ps = e1(nc.psum_tensor("bc2_ps", [128, 512], F32))
            mm_ps = [e1(nc.psum_tensor(f"mm2_ps{i}", [128, 512], F32)) for i in range(2)]
            v_ps = [e1(nc.psum_tensor(f"v_ps{i}", [128, 512], F32)) for i in range(2)]
            stat_ps = e1(nc.psum_tensor("st2_ps", [1, 512], F32))
            kvaw_sb = e1(nc.sbuf_tensor("kvaw_sb", [128, DCN * QL], BF16))
            kvbw_sb = e1(nc.sbuf_tensor("kvbw_sb", [128, KLC * 4096], BF16))
            xn_t = [e1(nc.sbuf_tensor(f"xnb_t{i}", [128, D], F32)) for i in range(2)]
            x2_t = e1(nc.sbuf_tensor("x2b_t", [128, D], F32))
            st_sc = e1(nc.sbuf_tensor("stb_sc", [128, 8], F32))
            rows = e1(nc.sbuf_tensor("rowsb", [1, 1024], F32))
            mu_bc = e1(nc.sbuf_tensor("mub_bc", [128, 512], F32))
            rs_bc = e1(nc.sbuf_tensor("rsb_bc", [128, 512], F32))
            xt_c = [e1(nc.sbuf_tensor(f"xtb_c{i}", [128, 512], F32)) for i in range(3)]
            tmp_c = e1(nc.sbuf_tensor("tmpb_c", [128, 512], F32))
            lnT = [e1(nc.sbuf_tensor(f"lnTb{i}", [128, DCN * 512], BF16)) for i in range(2)]
            klat = e1(nc.sbuf_tensor("klat", [128, KLC * 512], F32))
            sq_c = [e1(nc.sbuf_tensor(f"sqb_c{i}", [128, 512], BF16)) for i in range(2)]
            kvar = e1(nc.sbuf_tensor("kvar", [1, 512], F32))
            krs = e1(nc.sbuf_tensor("krs", [1, 512], F32))
            rsk_bc = e1(nc.sbuf_tensor("rsk_bc", [128, 512], F32))
            klatN = e1(nc.sbuf_tensor("klatN", [128, KLC * 512], BF16))
            kstage = [e1(nc.sbuf_tensor(f"kstage{i}", [128, 512], BF16)) for i in range(2)]
            vstage = [e1(nc.sbuf_tensor(f"vstage{i}", [128, 2048], BF16)) for i in range(2)]

            p.dma("g", "w1b", kvaw_sb[:].rearrange("p (c f) -> p c f", c=DCN),
                  kvaw_h.ap())
            t_w1b = p.dma("g", "w1b", kvbw_sb[:].rearrange("p (c f) -> p c f", c=KLC),
                          kvbw_h.ap())

            state = dict(xn_war=[None, None], x2_war=None, xt_war=[None] * 3,
                         tp_war=None, bc_war=None, mu_war=None, rows_war=None)
            ln_war = [None, None]
            ln_tok = [[None] * DCN, [None] * DCN]
            mm_war = [None, None]
            kst_war = [None, None]
            vst_war = [None, None]
            vps_war = [None, None]
            stat_war = [None, None]

            def ln_block_b(tb, par, c0, st):
                for ti in range(4):
                    tpar = ti & 1
                    gt = 4 * tb + ti
                    ld = p.dma("q", f"xnb{tpar}", xn_t[tpar][:],
                               xn_h.ap()[gt * 128:(gt + 1) * 128, :],
                               deps=(st["xn_war"][tpar] or []))
                    s1 = p.op("v", lambda e, b=tpar, ti=ti: e.reduce_sum(
                        st_sc[:, ti:ti + 1], xn_t[b][:], AX.X), deps=[ld])
                    x2 = p.op("g", lambda e, b=tpar: e.tensor_tensor(
                        x2_t[:], xn_t[b][:], xn_t[b][:], ALU.mult),
                        deps=[ld, st["x2_war"]])
                    s2 = p.op("v", lambda e: e.reduce_sum(
                        st_sc[:, 4:5], x2_t[:], AX.X), deps=[x2])
                    st["x2_war"] = s2
                    st["xn_war"][tpar] = [s1, x2]
                    mu = p.op("v", lambda e, ti=ti: e.tensor_scalar_mul(
                        st_sc[:, ti:ti + 1], st_sc[:, ti:ti + 1], 1.0 / D), deps=[s1])
                    m2 = p.op("v", lambda e, ti=ti: e.tensor_tensor(
                        st_sc[:, 6:7], st_sc[:, ti:ti + 1], st_sc[:, ti:ti + 1],
                        ALU.mult), deps=[mu])
                    var = p.op("v", lambda e: e.tensor_scalar(
                        st_sc[:, 4:5], st_sc[:, 4:5], 1.0 / D, EPS_LN,
                        ALU.mult, ALU.add), deps=[s2])
                    var2 = p.op("v", lambda e: e.tensor_tensor(
                        st_sc[:, 5:6], st_sc[:, 4:5], st_sc[:, 6:7], ALU.subtract),
                        deps=[var, m2])
                    lg = p.op("s", lambda e: e.activation(
                        st_sc[:, 7:8], st_sc[:, 5:6], AF.Ln), deps=[var2])
                    rstd = p.op("s", lambda e: e.activation(
                        st_sc[:, 5:6], st_sc[:, 7:8], AF.Exp, scale=-0.5), deps=[lg])
                    tt1 = p.op("t", lambda e, ti=ti: e.transpose(
                        tp_ps[0:1, :], st_sc[:, ti:ti + 1], ident_f[:]),
                        deps=[mu, m2, st["tp_war"]])
                    cp1 = p.op("v", lambda e, ti=ti: e.tensor_copy(
                        rows[0:1, ti * 128:(ti + 1) * 128], tp_ps[0:1, :]),
                        deps=[tt1, st["rows_war"]])
                    tt2 = p.op("t", lambda e: e.transpose(
                        tp_ps[0:1, :], st_sc[:, 5:6], ident_f[:]), deps=[rstd, cp1])
                    cp2 = p.op("v", lambda e, ti=ti: e.tensor_copy(
                        rows[0:1, 512 + ti * 128:512 + (ti + 1) * 128],
                        tp_ps[0:1, :]), deps=[tt2])
                    st["tp_war"] = cp2
                b1 = p.op("t", lambda e: e.matmul(bc_ps[:], ones_bc[:],
                                                  rows[0:1, 0:512], start=True,
                                                  stop=True, skip_group_check=True),
                          deps=[st["tp_war"], st["bc_war"]])
                mub = p.op("v", lambda e: e.tensor_copy(mu_bc[:], bc_ps[:]),
                           deps=[b1, st["mu_war"]])
                b2 = p.op("t", lambda e: e.matmul(bc_ps[:], ones_bc[:],
                                                  rows[0:1, 512:1024], start=True,
                                                  stop=True, skip_group_check=True),
                          deps=[mub])
                rsb = p.op("v", lambda e: e.tensor_copy(rs_bc[:], bc_ps[:]),
                           deps=[b2, st["mu_war"]])
                st["bc_war"] = rsb
                st["rows_war"] = b2
                t2 = st.get("tmp_war")
                for dc in range(DCN):
                    cr = dc % 3
                    xt = p.dma("q", f"xtb{cr}", xt_c[cr][:],
                               xT_h.ap()[dc, :, c0:c0 + TB], deps=[st["xt_war"][cr]])
                    t1 = p.op("v", lambda e, cr=cr: e.tensor_tensor(
                        tmp_c[:], xt_c[cr][:], mu_bc[:], ALU.subtract),
                        deps=[xt, mub, t2])
                    st["xt_war"][cr] = t1
                    t2 = p.op("v", lambda e, dc=dc, par=par: e.tensor_tensor(
                        lnT[par][:, dc * 512:(dc + 1) * 512], tmp_c[:], rs_bc[:],
                        ALU.mult), deps=[t1, rsb, ln_war[par]])
                    ln_tok[par][dc] = t2
                st["mu_war"] = t2
                st["tmp_war"] = t2

            for tb in range(NTB):
                par = tb & 1
                c0 = tb * TB
                ln_block_b(tb, par, c0, state)
                klat_tok = [None] * KLC
                sq_war = stat_war
                last_stop = None
                last_mm = None
                for fc in range(QLC):
                    bank = fc & 1
                    stop = None
                    for dc in range(DCN):
                        dd = [ln_tok[par][dc], t_w1b]
                        if dc == 0:
                            dd.append(mm_war[bank])
                        stop = p.op("t", lambda e, fc=fc, dc=dc, bank=bank, par=par:
                                    e.matmul(mm_ps[bank][:],
                                             kvaw_sb[:, dc * QL + fc * 128:dc * QL + (fc + 1) * 128],
                                             lnT[par][:, dc * 512:(dc + 1) * 512],
                                             start=(dc == 0), stop=(dc == DCN - 1),
                                             skip_group_check=True), deps=dd)
                    last_mm = stop
                    if fc < KLC:
                        ev = p.op("v", lambda e, fc=fc, bank=bank: e.tensor_copy(
                            klat[:, fc * 512:(fc + 1) * 512], mm_ps[bank][:]),
                            deps=[stop])
                        klat_tok[fc] = ev
                        mm_war[bank] = ev
                        sp = fc & 1
                        sqt = p.op("v", lambda e, fc=fc, sp=sp: e.tensor_tensor(
                            sq_c[sp][:], klat[:, fc * 512:(fc + 1) * 512],
                            klat[:, fc * 512:(fc + 1) * 512], ALU.mult),
                            deps=[ev, sq_war[sp]])
                        last_stop = p.op("t", lambda e, fc=fc, sp=sp: e.matmul(
                            stat_ps[:], ones_st[:], sq_c[sp][:],
                            start=(fc == 0), stop=(fc == KLC - 1),
                            skip_group_check=True), deps=[sqt])
                        sq_war[sp] = last_stop
                    else:
                        sp = fc & 1
                        ev = p.op("v", lambda e, bank=bank, sp=sp: e.tensor_copy(
                            kstage[sp][:], mm_ps[bank][:]), deps=[stop, kst_war[sp]])
                        mm_war[bank] = ev
                        kst_war[sp] = p.dma("q", f"kst{sp}",
                                            kTr_s.ap()[(fc - KLC) * 128:(fc - 3) * 128,
                                                       c0:c0 + TB],
                                            kstage[sp][:], deps=[ev])
                stat_war = sq_war
                kv1 = p.op("v", lambda e: e.tensor_scalar(
                    kvar[:], stat_ps[:], 1.0 / KL, EPS_RMS_ATTN, ALU.mult, ALU.add),
                    deps=[last_stop])
                kl1 = p.op("s", lambda e: e.activation(krs[:], kvar[:], AF.Ln), deps=[kv1])
                kl2 = p.op("s", lambda e: e.activation(krs[:], krs[:], AF.Exp,
                                                       scale=-0.5), deps=[kl1])
                b4 = p.op("t", lambda e: e.matmul(bc_ps[:], ones_bc[:], krs[:],
                                                  start=True, stop=True,
                                                  skip_group_check=True),
                          deps=[kl2, state["bc_war"]])
                rkb = p.op("v", lambda e: e.tensor_copy(rsk_bc[:], bc_ps[:]), deps=[b4])
                state["bc_war"] = rkb
                kn_tok = [None] * KLC
                for fc in range(KLC):
                    kn_tok[fc] = p.op("v", lambda e, fc=fc: e.scalar_tensor_tensor(
                        klatN[:, fc * 512:(fc + 1) * 512],
                        klat[:, fc * 512:(fc + 1) * 512],
                        gkv_sb[:, fc:fc + 1], rsk_bc[:], ALU.mult, ALU.mult),
                        deps=[klat_tok[fc], rkb, g_kv])
                for fc2 in range(NH):
                    bank = fc2 & 1
                    stop = None
                    for kc in range(KLC):
                        dd = [kn_tok[kc], t_w1b]
                        if kc == 0:
                            dd.append(mm_war[bank])
                        stop = p.op("t", lambda e, fc2=fc2, kc=kc, bank=bank:
                                    e.matmul(mm_ps[bank][:],
                                             kvbw_sb[:, kc * 4096 + fc2 * 128:kc * 4096 + (fc2 + 1) * 128],
                                             klatN[:, kc * 512:(kc + 1) * 512],
                                             start=(kc == 0), stop=(kc == KLC - 1),
                                             skip_group_check=True), deps=dd)
                    sp = fc2 & 1
                    ev = p.op("v", lambda e, sp=sp, bank=bank: e.tensor_copy(
                        kstage[sp][:], mm_ps[bank][:]), deps=[stop, kst_war[sp]])
                    mm_war[bank] = ev
                    kst_war[sp] = p.dma("q", f"kst{sp}",
                                        kTn_s.ap()[fc2 * 128:(fc2 + 1) * 128, c0:c0 + TB],
                                        kstage[sp][:], deps=[ev])
                for ts in range(4):
                    vp = ts & 1
                    gchunk = 4 * tb + ts
                    evs = []
                    for half in range(4):
                        stop = None
                        for kc in range(KLC):
                            dd = [kn_tok[kc]]
                            if kc == 0:
                                dd.append(vps_war[vp])
                            stop = p.op("t", lambda e, ts=ts, kc=kc, half=half, vp=vp:
                                        e.matmul(v_ps[vp][:],
                                                 klatN[:, kc * 512 + ts * 128:kc * 512 + (ts + 1) * 128],
                                                 kvbw_sb[:, kc * 4096 + 2048 + half * 512:
                                                         kc * 4096 + 2048 + (half + 1) * 512],
                                                 start=(kc == 0), stop=(kc == KLC - 1),
                                                 skip_group_check=True), deps=dd)
                        ev = p.op("v", lambda e, vp=vp, half=half: e.tensor_copy(
                            vstage[vp][:, half * 512:(half + 1) * 512], v_ps[vp][:]),
                            deps=([stop, vst_war[vp]] if half == 0 else [stop]))
                        vps_war[vp] = ev
                        evs.append(ev)
                    vst_war[vp] = p.dma("q", f"vst{vp}",
                                        v_s.ap()[:, :, gchunk, :].rearrange("h p d -> p h d"),
                                        vstage[vp][:].rearrange("p (h d) -> p h d", h=NH),
                                        deps=evs)
                ln_war[par] = last_mm
            p.barrier()
            p.flush()

        with ExitStack() as P2:
            e2 = P2.enter_context
            sc_ps = [e2(nc.psum_tensor(f"sc_ps{i}", [128, 512], F32)) for i in range(2)]
            av_ps = e2(nc.psum_tensor("av_ps", [128, 512], F32))
            r_ps = e2(nc.psum_tensor("r_ps", [1, 512], F32))
            s2_ps = e2(nc.psum_tensor("s2_ps", [1, 512], F32))
            bc_ps = e2(nc.psum_tensor("bc3_ps", [128, 512], F32))
            kTn = [e2(nc.sbuf_tensor(f"kTn{i}", [128, T], BF16)) for i in range(2)]
            kTr = [e2(nc.sbuf_tensor(f"kTr{i}", [64, T], BF16)) for i in range(2)]
            vh = [e2(nc.sbuf_tensor(f"vh{i}", [128, T], BF16)) for i in range(2)]
            qTn = [e2(nc.sbuf_tensor(f"qTn{i}", [128, MYT], BF16)) for i in range(2)]
            qTr = [e2(nc.sbuf_tensor(f"qTr{i}", [64, MYT], BF16)) for i in range(2)]
            sqn = e2(nc.sbuf_tensor("sqn", [128, T], BF16))
            sqr = e2(nc.sbuf_tensor("sqr", [64, T], BF16))
            k2row = e2(nc.sbuf_tensor("k2row", [1, T], F32))
            q2row = e2(nc.sbuf_tensor("q2row", [1, MYT], F32))
            k2m = e2(nc.sbuf_tensor("k2m", [1, 1], F32))
            q2m = e2(nc.sbuf_tensor("q2m", [1, MYB], F32))
            ncg = e2(nc.sbuf_tensor("ncg", [1, MYB], F32))
            negc_bc = e2(nc.sbuf_tensor("negc_bc", [128, 1], F32))
            probs = [e2(nc.sbuf_tensor(f"probs{i}", [128, 512], BF16)) for i in range(4)]
            rrow = e2(nc.sbuf_tensor("rrow", [1, 512], F32))
            rinv = e2(nc.sbuf_tensor("rinv", [1, 512], F32))
            rinv_bc = e2(nc.sbuf_tensor("rinv_bc", [128, 512], F32))
            astage = [e2(nc.sbuf_tensor(f"astage{i}", [128, 512], BF16)) for i in range(2)]

            ld = [[None] * 5, [None] * 5]
            buf_war = [[None] * 5, [None] * 5]

            def load_head(h):
                par = h & 1
                ld[par][0] = p.dma("q", f"hld{par}", kTn[par][:],
                                   kTn_s.ap()[h * 128:(h + 1) * 128, :],
                                   deps=[buf_war[par][0]])
                ld[par][1] = p.dma("s", f"hlds{par}", kTr[par][:],
                                   kTr_s.ap()[h * 64:(h + 1) * 64, :],
                                   deps=[buf_war[par][1]])
                ld[par][2] = p.dma("q", f"hld{par}", vh[par][:],
                                   v_s.ap()[h].rearrange("p c d -> p (c d)"),
                                   deps=[buf_war[par][2]])
                ld[par][3] = p.dma("s", f"hlds{par}", qTn[par][:],
                                   qT_s.ap()[h * 128:(h + 1) * 128, :],
                                   deps=[buf_war[par][3]])
                ld[par][4] = p.dma("s", f"hlds{par}", qTr[par][:],
                                   qT_s.ap()[2048 + h * 64:2048 + (h + 1) * 64, :],
                                   deps=[buf_war[par][4]])
                ld[par][0] = ld[par][2]
                ld[par][1] = ld[par][4]
                ld[par][3] = ld[par][4]

            load_head(0)
            sq_war = None
            ps2_war = None
            ncg_war = None
            negc_war = None
            probs_war = [None] * 4
            sc_war = [None, None]
            av_war = None
            r_war = None
            bc_war = None
            ast_war = [None, None]
            for h in range(NH):
                par = h & 1
                if h + 1 < NH:
                    load_head(h + 1)
                sn = p.op("v", lambda e, par=par: e.tensor_tensor(
                    sqn[:], kTn[par][:], kTn[par][:], ALU.mult),
                    deps=[ld[par][0], sq_war])
                sr = p.op("v", lambda e, par=par: e.tensor_tensor(
                    sqr[:], kTr[par][:], kTr[par][:], ALU.mult),
                    deps=[ld[par][1], sq_war])
                kc_toks = []
                for c8 in range(T // 512):
                    m1 = p.op("t", lambda e, c8=c8: e.matmul(
                        s2_ps[:], ones_st[:], sqn[:, c8 * 512:(c8 + 1) * 512],
                        start=True, stop=False, skip_group_check=True),
                        deps=[sn, ps2_war])
                    m2 = p.op("t", lambda e, c8=c8: e.matmul(
                        s2_ps[:], ones_st[0:64, :], sqr[:, c8 * 512:(c8 + 1) * 512],
                        start=False, stop=True, skip_group_check=True), deps=[m1, sr])
                    cp = p.op("v", lambda e, c8=c8: e.tensor_copy(
                        k2row[:, c8 * 512:(c8 + 1) * 512], s2_ps[:]), deps=[m2])
                    ps2_war = cp
                    kc_toks.append(cp)
                k2t = p.op("v", lambda e: e.reduce_max(k2m[:], k2row[:], AX.X),
                           deps=kc_toks)
                sn2 = p.op("v", lambda e, par=par: e.tensor_tensor(
                    sqn[:, 0:MYT], qTn[par][:], qTn[par][:], ALU.mult),
                    deps=[ld[par][3]] + kc_toks)
                sr2 = p.op("v", lambda e, par=par: e.tensor_tensor(
                    sqr[:, 0:MYT], qTr[par][:], qTr[par][:], ALU.mult),
                    deps=[ld[par][4]] + kc_toks)
                qc_toks = []
                for c8 in range(MYT // 512):
                    m1 = p.op("t", lambda e, c8=c8: e.matmul(
                        s2_ps[:], ones_st[:], sqn[:, c8 * 512:(c8 + 1) * 512],
                        start=True, stop=False, skip_group_check=True),
                        deps=[sn2, ps2_war])
                    m2 = p.op("t", lambda e, c8=c8: e.matmul(
                        s2_ps[:], ones_st[0:64, :], sqr[:, c8 * 512:(c8 + 1) * 512],
                        start=False, stop=True, skip_group_check=True), deps=[m1, sr2])
                    cp = p.op("v", lambda e, c8=c8: e.tensor_copy(
                        q2row[:, c8 * 512:(c8 + 1) * 512], s2_ps[:]), deps=[m2])
                    ps2_war = cp
                    qc_toks.append(cp)
                sq_war = sr2
                qm = p.op("v", lambda e: e.reduce_max(
                    q2m[:], q2row[:].rearrange("o (q f) -> o q f", q=MYB), AX.X),
                    deps=qc_toks)
                pr = p.op("v", lambda e: e.tensor_single_scalar(
                    ncg[:], q2m[:], k2m[:], ALU.mult), deps=[qm, k2t, ncg_war])
                cl = p.op("s", lambda e: e.activation(ncg[:], ncg[:], AF.Ln), deps=[pr])
                ce = p.op("s", lambda e: e.activation(ncg[:], ncg[:], AF.Exp, scale=0.5),
                          deps=[cl])
                ngc = p.op("v", lambda e: e.tensor_scalar(
                    ncg[:], ncg[:], -1.0, -1.0, ALU.mult, ALU.add), deps=[ce])
                for Qi in range(MYB):
                    ncch = 8 * (Qi + 1)
                    qc0 = Qi * 512
                    bq = p.op("t", lambda e, Qi=Qi: e.matmul(
                        bc_ps[:, 0:1], ones_bc[:], ncg[0:1, Qi:Qi + 1],
                        start=True, stop=True, skip_group_check=True),
                        deps=[ngc, bc_war])
                    ncg_war = bq
                    nbc = p.op("v", lambda e: e.tensor_copy(negc_bc[:], bc_ps[:, 0:1]),
                               deps=[bq, negc_war])
                    bc_war = nbc
                    av_stop = None
                    r_stop = None
                    ex = None
                    mm1 = mm2 = None
                    pend_av = None
                    for ci in range(ncch + 1):
                      if ci < ncch:
                        bank = ci & 1
                        slot = ci & 3
                        mm1 = p.op("t", lambda e, ci=ci, bank=bank, par=par, qc0=qc0:
                                   e.matmul(sc_ps[bank][:],
                                            kTn[par][:, ci * 128:(ci + 1) * 128],
                                            qTn[par][:, qc0:qc0 + 512],
                                            start=True, stop=False,
                                            skip_group_check=True),
                                   deps=[ld[par][0], ld[par][3], sc_war[bank]])
                        mm2 = p.op("t", lambda e, ci=ci, bank=bank, par=par, qc0=qc0:
                                   e.matmul(sc_ps[bank][:],
                                            kTr[par][:, ci * 128:(ci + 1) * 128],
                                            qTr[par][:, qc0:qc0 + 512],
                                            start=False, stop=True,
                                            skip_group_check=True),
                                   deps=[ld[par][1], ld[par][4]])
                        ex = p.op("s", lambda e, bank=bank, slot=slot: e.activation(
                            probs[slot][:], sc_ps[bank][:], AF.Exp,
                            bias=negc_bc[:], scale=1.0),
                            deps=[mm2, nbc, probs_war[slot]])
                        sc_war[bank] = ex
                        ptok = ex
                        if ci >= ncch - 8:
                            off = ci - (ncch - 8)
                            ptok = p.op("v", lambda e, slot=slot, off=off: e.tensor_tensor(
                                probs[slot][:], probs[slot][:],
                                mask_sb[:, off * 512:(off + 1) * 512], ALU.mult),
                                deps=[ex])
                      if pend_av is not None:
                        cj, slot_j, ptok_j = pend_av
                        avd = [ptok_j, ld[par][2]]
                        if cj == 0:
                            avd.append(av_war)
                        av_stop = p.op("t", lambda e, cj=cj, slot=slot_j, par=par, n=ncch:
                                       e.matmul(av_ps[:],
                                                vh[par][:, cj * 128:(cj + 1) * 128],
                                                probs[slot][:],
                                                start=(cj == 0), stop=(cj == n - 1),
                                                skip_group_check=True), deps=avd)
                        rd = [ptok_j]
                        if cj == 0:
                            rd.append(r_war)
                        r_stop = p.op("t", lambda e, cj=cj, slot=slot_j, n=ncch:
                                      e.matmul(r_ps[:], ones_st[:], probs[slot][:],
                                               start=(cj == 0), stop=(cj == n - 1),
                                               skip_group_check=True), deps=rd)
                        probs_war[slot_j] = r_stop
                      if ci < ncch:
                        pend_av = (ci, slot, ptok)
                    rr = p.op("v", lambda e: e.tensor_copy(rrow[:], r_ps[:]),
                              deps=[r_stop])
                    r_war = rr
                    ri = p.op("v", lambda e: e.reciprocal(rinv[:], rrow[:]), deps=[rr])
                    bq2 = p.op("t", lambda e: e.matmul(
                        bc_ps[:], ones_bc[:], rinv[:], start=True, stop=True,
                        skip_group_check=True), deps=[ri, nbc])
                    rbc = p.op("v", lambda e: e.tensor_copy(rinv_bc[:], bc_ps[:]),
                               deps=[bq2, av_war])
                    bc_war = rbc
                    apar = Qi & 1
                    ast = p.op("v", lambda e, apar=apar: e.tensor_tensor(
                        astage[apar][:], av_ps[:], rinv_bc[:], ALU.mult),
                        deps=[av_stop, rbc, ast_war[apar]])
                    av_war = ast
                    ast_war[apar] = p.dma("q", f"ast{apar}",
                                          attnT_s.ap()[h * 128:(h + 1) * 128,
                                                       qc0:qc0 + 512],
                                          astage[apar][:], deps=[ast])
                    negc_war = ex
                buf_war[par] = [sc_war[0], sc_war[1], av_stop, mm1, mm2]
            p.barrier()
            p.flush()

        with ExitStack() as SB:
            eb = SB.enter_context
            xqT = eb(nc.sbuf_tensor("xqT", [128, DCN * MYT], BF16))
            xq_tok = [[None] * DCN for _ in range(MYB)]
            cf_tok = [None] * MYB

            SB2 = ExitStack()
            eb2 = SB2.enter_context
            si3_bc = eb2(nc.sbuf_tensor("si3_bc", [128, MYB * 512], BF16))

            with ExitStack() as P3:
                e3 = P3.enter_context
                o_ps = [e3(nc.psum_tensor(f"o_ps{i}", [128, 512], F32)) for i in range(2)]
                stm_ps = e3(nc.psum_tensor("stm_ps", [1, 512], F32))
                st2_ps = e3(nc.psum_tensor("st2b_ps", [1, 512], F32))
                tp_ps = e3(nc.psum_tensor("tp3_ps", [128, 128], F32))
                bc_ps = e3(nc.psum_tensor("bc4_ps", [128, 512], F32))
                at_sb = e3(nc.sbuf_tensor("at_sb", [128, DCN * 512], BF16))
                ow_sb = [e3(nc.sbuf_tensor(f"ow_sb{i}", [128, DCN * 128], BF16))
                         for i in range(2)]
                xm_c = [e3(nc.sbuf_tensor(f"xm_c{i}", [128, 512], F32)) for i in range(3)]
                y1blk = e3(nc.sbuf_tensor("y1blk", [128, DCN * 512], F32))
                y1b = [e3(nc.sbuf_tensor(f"y1b{i}", [128, 512], BF16)) for i in range(2)]
                sqy = [e3(nc.sbuf_tensor(f"sqy{i}", [128, 512], BF16)) for i in range(2)]
                mx = e3(nc.sbuf_tensor("mx", [128, 512], F32))
                mn = e3(nc.sbuf_tensor("mn", [128, 512], F32))
                tr64 = e3(nc.sbuf_tensor("tr64", [64, 512], F32))
                tr32 = e3(nc.sbuf_tensor("tr32", [32, 512], F32))
                toks = e3(nc.sbuf_tensor("toks", [128, 24], F32))
                rows3 = e3(nc.sbuf_tensor("rows3", [1, 2048], F32))
                mu2_bc = e3(nc.sbuf_tensor("mu2_bc", [128, 512], F32))
                rs2_bc = e3(nc.sbuf_tensor("rs2_bc", [128, 512], F32))
                s_bc = e3(nc.sbuf_tensor("s_bc", [128, 512], F32))
                si_bc = e3(nc.sbuf_tensor("si_bc", [128, 512], F32))
                si3f = e3(nc.sbuf_tensor("si3f", [128, 512], F32))
                tq = [e3(nc.sbuf_tensor(f"tq{i}", [128, 512], F32)) for i in range(2)]
                yq = [e3(nc.sbuf_tensor(f"yq{i}", [128, 512], F32)) for i in range(2)]
                i32_t = [e3(nc.sbuf_tensor(f"i32_t{i}", [128, 512], I32)) for i in range(2)]
                r0_t = [e3(nc.sbuf_tensor(f"r0_t{i}", [128, 512], F32)) for i in range(2)]
                d_t = [e3(nc.sbuf_tensor(f"d_t{i}", [128, 512], F32)) for i in range(2)]
                a_t = [e3(nc.sbuf_tensor(f"a_t{i}", [128, 512], F32)) for i in range(2)]
                b_t = [e3(nc.sbuf_tensor(f"b_t{i}", [128, 512], F32)) for i in range(2)]

                at_tok = None
                at_war = None
                ow_tok = [None, None]
                ow_war = [None, None]
                xm_war = [None] * 3
                ops_war = [None, None]
                stm_war = None
                st2_war = None
                y1b_war = [None, None]
                sqy_war = [None, None]
                y1_war = [None] * DCN
                y1blk_war = [None] * DCN
                tp_war = None
                bc_war = None
                mxtree_war = None
                si3f_war = None
                chain_war = [None, None]
                tq_war = [None, None]
                yq_war3 = [None, None]
                i32_war3 = [None, None]
                r0_war3 = [None, None]
                d_war3 = [None, None]
                a_war3 = [None, None]
                b_war3 = [None, None]
                for blk in range(MYB):
                    gc0 = blk * 512
                    for fc in range(DCN):
                        at_tok = p.dma("s", "atld", at_sb[:, fc * 512:(fc + 1) * 512],
                                       attnT_s.ap()[fc * 128:(fc + 1) * 128,
                                                    gc0:gc0 + 512],
                                       deps=([at_war] if fc == 0 else []))
                    mxi = p.op("v", lambda e: e.memset(mx[:], -3.0e38),
                               deps=[mxtree_war])
                    mni = p.op("v", lambda e: e.memset(mn[:], 3.0e38),
                               deps=[mxtree_war])
                    stop = None
                    for dc in range(DCN):
                        opar = dc & 1
                        ow_tok[opar] = p.dma("g", f"owl{opar}",
                                             ow_sb[opar][:].rearrange(
                                                 "p (c f) -> p c f", c=DCN),
                                             ow_h.ap()[dc], deps=[ow_war[opar]])
                        cr = dc % 3
                        xm = p.dma("q", f"xm{cr}", xm_c[cr][:],
                                   xoT_h.ap()[dc, :, gc0:gc0 + 512],
                                   deps=[xm_war[cr]])
                        for fc in range(DCN):
                            dd = [at_tok, ow_tok[opar]]
                            if fc == 0:
                                dd.append(ops_war[opar])
                            stop = p.op("t", lambda e, fc=fc, opar=opar: e.matmul(
                                o_ps[opar][:],
                                ow_sb[opar][:, fc * 128:(fc + 1) * 128],
                                at_sb[:, fc * 512:(fc + 1) * 512],
                                start=(fc == 0), stop=(fc == DCN - 1),
                                skip_group_check=True), deps=dd)
                        ow_war[opar] = stop
                        y1c = p.op("v", lambda e, dc=dc, opar=opar, cr=cr: e.tensor_tensor(
                            y1blk[:, dc * 512:(dc + 1) * 512], o_ps[opar][:],
                            xm_c[cr][:], ALU.add),
                            deps=[stop, xm, y1blk_war[dc], p.latest("y1st")])
                        ops_war[opar] = y1c
                        xm_war[cr] = y1c
                        mxi = p.op("v", lambda e, dc=dc: e.tensor_tensor(
                            mx[:], mx[:], y1blk[:, dc * 512:(dc + 1) * 512], ALU.max),
                            deps=[y1c, mxi])
                        mni = p.op("v", lambda e, dc=dc: e.tensor_tensor(
                            mn[:], mn[:], y1blk[:, dc * 512:(dc + 1) * 512], ALU.min),
                            deps=[y1c, mni])
                        sp = dc & 1
                        yb = p.op("s", lambda e, dc=dc, sp=sp: e.copy(
                            y1b[sp][:], y1blk[:, dc * 512:(dc + 1) * 512]),
                            deps=[y1c] + (y1b_war[sp] or []))
                        sy = p.op("g", lambda e, sp=sp: e.tensor_tensor(
                            sqy[sp][:], y1b[sp][:], y1b[sp][:], ALU.mult),
                            deps=[yb, sqy_war[sp]])
                        stm = p.op("t", lambda e, dc=dc, sp=sp: e.matmul(
                            stm_ps[:], ones_st[:], y1b[sp][:],
                            start=(dc == 0), stop=(dc == DCN - 1),
                            skip_group_check=True),
                            deps=[yb] + ([stm_war] if dc == 0 else []))
                        st2 = p.op("t", lambda e, dc=dc, sp=sp: e.matmul(
                            st2_ps[:], ones_st[:], sqy[sp][:],
                            start=(dc == 0), stop=(dc == DCN - 1),
                            skip_group_check=True),
                            deps=[sy] + ([st2_war] if dc == 0 else []))
                        y1b_war[sp] = [stm, sy]
                        sqy_war[sp] = st2
                        y1_war[dc] = p.dma("q", "y1st", y1_s.ap()[dc, :, gc0:gc0 + 512],
                                           y1blk[:, dc * 512:(dc + 1) * 512],
                                           deps=[y1c])
                    at_war = stop
                    muv = p.op("v", lambda e: e.tensor_scalar_mul(
                        rows3[0:1, 0:512], stm_ps[:], 1.0 / D), deps=[stm])
                    stm_war = muv
                    ex2 = p.op("v", lambda e: e.tensor_scalar_mul(
                        rows3[0:1, 512:1024], st2_ps[:], 1.0 / D), deps=[st2])
                    st2_war = ex2
                    ms2 = p.op("v", lambda e: e.tensor_tensor(
                        rows3[0:1, 1024:1536], rows3[0:1, 0:512], rows3[0:1, 0:512],
                        ALU.mult), deps=[muv])
                    vr = p.op("v", lambda e: e.scalar_tensor_tensor(
                        rows3[0:1, 512:1024], rows3[0:1, 512:1024], EPS_LN,
                        rows3[0:1, 1024:1536], ALU.add, ALU.subtract),
                        deps=[ex2, ms2])
                    l1 = p.op("s", lambda e: e.activation(
                        rows3[0:1, 512:1024], rows3[0:1, 512:1024], AF.Ln), deps=[vr])
                    l2 = p.op("s", lambda e: e.activation(
                        rows3[0:1, 512:1024], rows3[0:1, 512:1024], AF.Exp,
                        scale=-0.5), deps=[l1])
                    for k in range(4):
                        t1 = p.op("t", lambda e, k=k: e.transpose(
                            tp_ps[:, 0:1], rows3[0:1, k * 128:(k + 1) * 128],
                            ident_f[0:1, 0:1]), deps=[muv, tp_war])
                        c1 = p.op("v", lambda e, k=k: e.tensor_copy(
                            toks[:, k:k + 1], tp_ps[:, 0:1]), deps=[t1])
                        t2 = p.op("t", lambda e, k=k: e.transpose(
                            tp_ps[:, 0:1],
                            rows3[0:1, 512 + k * 128:512 + (k + 1) * 128],
                            ident_f[0:1, 0:1]), deps=[l2, c1])
                        c2 = p.op("v", lambda e, k=k: e.tensor_copy(
                            toks[:, 4 + k:5 + k], tp_ps[:, 0:1]), deps=[t2])
                        tp_war = c2
                    t64c = p.op("v", lambda e: e.tensor_copy(
                        tr64[:], mx[64:128, :]), deps=[mxi])
                    t64 = p.op("v", lambda e: e.tensor_tensor(
                        tr64[:], tr64[:], mx[0:64, :], ALU.max), deps=[t64c])
                    t32c = p.op("v", lambda e: e.tensor_copy(
                        tr32[:], tr64[32:64, :]), deps=[t64])
                    t32 = p.op("v", lambda e: e.tensor_tensor(
                        tr32[:], tr32[:], tr64[0:32, :], ALU.max), deps=[t32c])
                    for k in range(4):
                        tt = p.op("t", lambda e, k=k: e.transpose(
                            tp_ps[:, 0:32], tr32[:, k * 128:(k + 1) * 128],
                            ident_f[0:32, 0:32]), deps=[t32, tp_war])
                        rm = p.op("v", lambda e, k=k: e.reduce_max(
                            toks[:, 8 + k:9 + k], tp_ps[:, 0:32], AX.X), deps=[tt])
                        tp_war = rm
                    t64nc = p.op("v", lambda e: e.tensor_copy(
                        tr64[:], mn[64:128, :]), deps=[mni, t32])
                    t64n = p.op("v", lambda e: e.tensor_tensor(
                        tr64[:], tr64[:], mn[0:64, :], ALU.min), deps=[t64nc])
                    t32nc = p.op("v", lambda e: e.tensor_copy(
                        tr32[:], tr64[32:64, :]), deps=[t64n])
                    t32n = p.op("v", lambda e: e.tensor_tensor(
                        tr32[:], tr32[:], tr64[0:32, :], ALU.min), deps=[t32nc])
                    for k in range(4):
                        tt = p.op("t", lambda e, k=k: e.transpose(
                            tp_ps[:, 0:32], tr32[:, k * 128:(k + 1) * 128],
                            ident_f[0:32, 0:32]), deps=[t32n, tp_war])
                        rm = p.op("v", lambda e, k=k: e.tensor_reduce(
                            toks[:, 12 + k:13 + k], tp_ps[:, 0:32], AX.X, ALU.min,
                            negate=True), deps=[tt])
                        tp_war = rm
                    mxtree_war = tp_war
                    v1 = p.op("v", lambda e: e.tensor_tensor(
                        toks[:, 16:20], toks[:, 8:12], toks[:, 0:4], ALU.subtract),
                        deps=[tp_war])
                    v2 = p.op("v", lambda e: e.tensor_tensor(
                        toks[:, 20:24], toks[:, 12:16], toks[:, 0:4], ALU.add),
                        deps=[v1])
                    v3 = p.op("v", lambda e: e.tensor_tensor(
                        toks[:, 16:20], toks[:, 16:20], toks[:, 20:24], ALU.max),
                        deps=[v2])
                    v4 = p.op("v", lambda e: e.tensor_tensor(
                        toks[:, 16:20], toks[:, 16:20], toks[:, 4:8], ALU.mult),
                        deps=[v3])
                    v5 = p.op("v", lambda e: e.tensor_scalar_max(
                        toks[:, 16:20], toks[:, 16:20], 1e-5), deps=[v4])
                    v6 = p.op("v", lambda e: e.reciprocal(
                        toks[:, 20:24], toks[:, 16:20]), deps=[v5])
                    v7 = p.op("v", lambda e: e.tensor_scalar_mul(
                        toks[:, 20:24], toks[:, 20:24], 127.0), deps=[v6])
                    v8 = p.op("v", lambda e: e.tensor_scalar_mul(
                        toks[:, 16:20], toks[:, 16:20], INV127), deps=[v5, v7])
                    for k in range(4):
                        t1 = p.op("t", lambda e, k=k: e.transpose(
                            tp_ps[0:1, :], toks[:, 20 + k:21 + k], ident_f[:]),
                            deps=[v7, tp_war])
                        c1 = p.op("v", lambda e, k=k: e.tensor_copy(
                            rows3[0:1, 1024 + k * 128:1024 + (k + 1) * 128],
                            tp_ps[0:1, :]), deps=[t1])
                        t2 = p.op("t", lambda e, k=k: e.transpose(
                            tp_ps[0:1, :], toks[:, 16 + k:17 + k], ident_f[:]),
                            deps=[v8, c1])
                        c2 = p.op("v", lambda e, k=k: e.tensor_copy(
                            rows3[0:1, 1536 + k * 128:1536 + (k + 1) * 128],
                            tp_ps[0:1, :]), deps=[t2])
                        tp_war = c2
                    toks_done = tp_war
                    bt = p.op("t", lambda e: e.matmul(
                        bc_ps[:], ones_bc[:], rows3[0:1, 0:512], start=True, stop=True,
                        skip_group_check=True), deps=[muv, bc_war])
                    bmu = p.op("v", lambda e: e.tensor_copy(mu2_bc[:], bc_ps[:]),
                               deps=[bt, chain_war[0], chain_war[1]])
                    bt = p.op("t", lambda e: e.matmul(
                        bc_ps[:], ones_bc[:], rows3[0:1, 512:1024], start=True,
                        stop=True, skip_group_check=True), deps=[l2, bmu])
                    brs = p.op("v", lambda e: e.tensor_copy(rs2_bc[:], bc_ps[:]),
                               deps=[bt, chain_war[1], chain_war[0]])
                    bt = p.op("t", lambda e: e.matmul(
                        bc_ps[:], ones_bc[:], rows3[0:1, 1024:1536], start=True,
                        stop=True, skip_group_check=True), deps=[toks_done, brs])
                    bs = p.op("v", lambda e: e.tensor_copy(s_bc[:], bc_ps[:]),
                              deps=[bt])
                    bt = p.op("t", lambda e: e.matmul(
                        bc_ps[:], ones_bc[:], rows3[0:1, 1536:2048], start=True,
                        stop=True, skip_group_check=True), deps=[toks_done, bs])
                    bsi = p.op("v", lambda e: e.tensor_copy(si_bc[:], bc_ps[:]),
                               deps=[bt])
                    s3a = p.op("v", lambda e, blk=blk: e.tensor_tensor(
                        si3f[:], si_bc[:], si_bc[:], ALU.mult), deps=[bsi, si3f_war])
                    s3b = p.op("v", lambda e, blk=blk: e.tensor_tensor(
                        si3_bc[:, blk * 512:(blk + 1) * 512], si3f[:], si_bc[:],
                        ALU.mult), deps=[s3a])
                    si3f_war = s3b
                    bc_war = bsi
                    for dc0 in range(0, DCN, 2):
                        pair = (dc0, dc0 + 1)
                        hh = {}
                        for dc in pair:
                            cp = dc & 1
                            hh[(dc, "q1")] = p.op(
                                "v", lambda e, dc=dc, cp=cp: e.tensor_tensor(
                                    tq[cp][:], y1blk[:, dc * 512:(dc + 1) * 512],
                                    mu2_bc[:], ALU.subtract),
                                deps=[bmu, tq_war[cp]])
                            y1blk_war[dc] = hh[(dc, "q1")]
                        for dc in pair:
                            cp = dc & 1
                            hh[(dc, "q2")] = p.op("g", lambda e, cp=cp: e.tensor_tensor(
                                tq[cp][:], tq[cp][:], rs2_bc[:], ALU.mult),
                                deps=[hh[(dc, "q1")], brs])
                        for dc in pair:
                            cp = dc & 1
                            hh[(dc, "q3")] = p.op("v", lambda e, cp=cp: e.tensor_tensor(
                                yq[cp][:], tq[cp][:], s_bc[:], ALU.mult),
                                deps=[hh[(dc, "q2")], bs, yq_war3[cp]])
                            tq_war[cp] = hh[(dc, "q3")]
                        for dc in pair:
                            cp = dc & 1
                            c1 = p.op("s", lambda e, cp=cp: e.copy(
                                i32_t[cp][:], yq[cp][:]),
                                deps=[hh[(dc, "q3")], i32_war3[cp]])
                            c2 = p.op("s", lambda e, cp=cp: e.copy(
                                r0_t[cp][:], i32_t[cp][:]), deps=[c1, r0_war3[cp]])
                            i32_war3[cp] = c2
                            hh[(dc, "c2")] = c2
                        for dc in pair:
                            cp = dc & 1
                            hh[(dc, "dd")] = p.op("g", lambda e, cp=cp: e.tensor_tensor(
                                d_t[cp][:], yq[cp][:], r0_t[cp][:], ALU.subtract),
                                deps=[hh[(dc, "c2")], d_war3[cp]])
                        for dc in pair:
                            cp = dc & 1
                            ab = p.op("s", lambda e, cp=cp: e.activation(
                                yq[cp][:], d_t[cp][:], AF.Abs), deps=[hh[(dc, "dd")]])
                            hh[(dc, "sg")] = p.op("s", lambda e, cp=cp: e.activation(
                                d_t[cp][:], d_t[cp][:], AF.Sign), deps=[ab])
                            hh[(dc, "ab")] = ab
                        cmh = {}
                        for dc in pair:
                            cp = dc & 1
                            cmh[dc] = p.op("v", lambda e, cp=cp: e.tensor_single_scalar(
                                a_t[cp][:], yq[cp][:], 0.5, ALU.is_gt),
                                deps=[hh[(dc, "ab")], a_war3[cp]])
                        for thr in (1.5, 2.5, 3.5):
                            for dc in pair:
                                cp = dc & 1
                                cmh[dc] = p.op("v", lambda e, cp=cp, thr=thr:
                                               e.scalar_tensor_tensor(
                                                   a_t[cp][:], yq[cp][:], thr,
                                                   a_t[cp][:], ALU.is_gt, ALU.add),
                                               deps=[cmh[dc]])
                        for dc in pair:
                            cp = dc & 1
                            hh[(dc, "co")] = p.op("g", lambda e, cp=cp: e.tensor_tensor(
                                b_t[cp][:], a_t[cp][:], d_t[cp][:], ALU.mult),
                                deps=[cmh[dc], hh[(dc, "sg")], b_war3[cp]])
                            yq_war3[cp] = cmh[dc]
                            a_war3[cp] = hh[(dc, "co")]
                            d_war3[cp] = hh[(dc, "co")]
                        for dc in pair:
                            cp = dc & 1
                            xq = p.op("v", lambda e, dc=dc, cp=cp, gc0=gc0:
                                      e.tensor_tensor(
                                          xqT[:, dc * MYT + gc0:dc * MYT + gc0 + 512],
                                          b_t[cp][:], r0_t[cp][:], ALU.add),
                                      deps=[hh[(dc, "co")]])
                            xq_tok[blk][dc] = xq
                            b_war3[cp] = xq
                            r0_war3[cp] = xq
                            chain_war[cp] = xq
                p.barrier()
                p.flush()

            gw_store = [[None] * MYB for _ in range(FFC)]
            cf_bc = eb2(nc.sbuf_tensor("cf_bc", [128, MYB * 512], BF16))
            sif_bc = eb2(nc.sbuf_tensor("sif_bc", [128, MYB * 512], BF16))
            mxf = eb2(nc.sbuf_tensor("mxf", [128, MYB * 512], BF16))
            mnf = eb2(nc.sbuf_tensor("mnf", [128, MYB * 512], BF16))
            statf_row = eb2(nc.sbuf_tensor("statf_row", [1, MYB * 512], BF16))
            with ExitStack() as P4:
                e4 = P4.enter_context
                f_ps = [e4(nc.psum_tensor(f"f_ps{i}", [128, 512], F32)) for i in range(4)]
                stf_ps = e4(nc.psum_tensor("stf_ps", [1, MYB * 512], F32))
                w1_sb = [e4(nc.sbuf_tensor(f"w1_sb{i}", [128, DCN * 128], BF16))
                         for i in range(2)]
                w3_sb = [e4(nc.sbuf_tensor(f"w3_sb{i}", [128, DCN * 128], BF16))
                         for i in range(2)]
                w3l_sb = [e4(nc.sbuf_tensor(f"w3l_sb{i}", [128, DCN * 128], BF16))
                          for i in range(2)]
                w3f_sb = [e4(nc.sbuf_tensor(f"w3f_sb{i}", [128, DCN * 128], F32))
                          for i in range(2)]
                u_t = [e4(nc.sbuf_tensor(f"u_t{i}", [128, 512], F32)) for i in range(2)]
                u2_t = [e4(nc.sbuf_tensor(f"u2_t{i}", [128, 512], F32)) for i in range(2)]
                g_t = [e4(nc.sbuf_tensor(f"g_t{i}", [128, 512], F32)) for i in range(2)]
                gw_t = [e4(nc.sbuf_tensor(f"gw_t{i}", [128, 512], BF16)) for i in range(2)]
                sqf = [e4(nc.sbuf_tensor(f"sqf{i}", [128, 512], BF16)) for i in range(2)]

                w1_tok = [None, None]
                w3_tok = [None, None]
                w3l_tok = [None, None]
                w3f_war = [None, None]
                w_war = [None, None]
                fps_war = [None] * 4
                u_war = [None, None]
                g_war = [None, None]
                gsq_war = [None, None]
                gw_war = [None, None]
                sq_war = [None, None]
                mx_tok = p.op("v", lambda e: e.memset(mxf[:], -3.0e38), deps=[])
                mn_tok = p.op("v", lambda e: e.memset(mnf[:], 3.0e38), deps=[])
                stf_stop = [None] * MYB
                pend_stat = []
                for fc in range(FFC):
                    wpar = fc & 1
                    w1_tok[wpar] = p.dma("g", f"wld{wpar}",
                                         w1_sb[wpar][:].rearrange("p (c f) -> p c f", c=DCN),
                                         w1_h.ap()[fc], deps=[w_war[wpar]])
                    w3_tok[wpar] = p.dma("g", f"wld{wpar}",
                                         w3_sb[wpar][:].rearrange("p (c f) -> p c f", c=DCN),
                                         w3_h.ap()[fc], deps=[])
                    wf = p.dma("q", f"wf{wpar}",
                               w3f_sb[wpar][:].rearrange("p (c f) -> p c f", c=DCN),
                               w3_h.ap()[fc], deps=[w3f_war[wpar]])
                    w3l_tok[wpar] = p.op("v", lambda e, wpar=wpar: e.tensor_tensor(
                        w3l_sb[wpar][:], w3f_sb[wpar][:], w3_sb[wpar][:],
                        ALU.subtract), deps=[wf, w3_tok[wpar], w_war[wpar]])
                    w3f_war[wpar] = w3l_tok[wpar]
                    for blk in range(MYB):
                        bp = blk & 1
                        p1b, p3b = bp, 2 + bp
                        stop1 = None
                        stop3 = None
                        for dc in range(DCN):
                            dd1 = [xq_tok[blk][dc], w3_tok[wpar]]
                            if dc == 0:
                                dd1.append(fps_war[p1b])
                            stop1 = p.op("t", lambda e, dc=dc, blk=blk, wpar=wpar, p1b=p1b:
                                         e.matmul(f_ps[p1b][:],
                                                  w1_sb[wpar][:, dc * 128:(dc + 1) * 128],
                                                  xqT[:, dc * MYT + blk * 512:
                                                      dc * MYT + (blk + 1) * 512],
                                                  start=(dc == 0), stop=(dc == DCN - 1),
                                                  skip_group_check=True), deps=dd1)
                        for dcl in range(2 * DCN):
                            dc, lo = dcl % DCN, dcl >= DCN
                            dd3 = [xq_tok[blk][dc],
                                   w3l_tok[wpar] if lo else w3_tok[wpar]]
                            if dcl == 0:
                                dd3.append(fps_war[p3b])
                            stop3 = p.op("t", lambda e, dc=dc, dcl=dcl, lo=lo, blk=blk,
                                         wpar=wpar, p3b=p3b:
                                         e.matmul(f_ps[p3b][:],
                                                  (w3l_sb if lo else w3_sb)[wpar][:, dc * 128:(dc + 1) * 128],
                                                  xqT[:, dc * MYT + blk * 512:
                                                      dc * MYT + (blk + 1) * 512],
                                                  start=(dcl == 0), stop=(dcl == 2 * DCN - 1),
                                                  skip_group_check=True), deps=dd3)
                        sp = (fc * MYB + blk) & 1
                        u = p.op("s", lambda e, sp=sp, p1b=p1b: e.activation(
                            u_t[sp][:], f_ps[p1b][:], AF.Relu), deps=[stop1, u_war[sp]])
                        u2 = p.op("v", lambda e, sp=sp: e.tensor_tensor(
                            u2_t[sp][:], u_t[sp][:], u_t[sp][:], ALU.mult), deps=[u])
                        u_war[sp] = u2
                        g = p.op("v", lambda e, sp=sp, p3b=p3b: e.tensor_tensor(
                            g_t[sp][:], u2_t[sp][:], f_ps[p3b][:], ALU.mult),
                            deps=[u2, stop3, g_war[sp], gsq_war[sp]])
                        g = p.op("v", lambda e, sp=sp, blk=blk: e.tensor_tensor(
                            g_t[sp][:], g_t[sp][:],
                            si3_bc[:, blk * 512:(blk + 1) * 512], ALU.mult),
                            deps=[g])
                        fps_war[p1b] = u
                        fps_war[p3b] = g
                        gw = p.op("v", lambda e, sp=sp, fc=fc: e.tensor_scalar_mul(
                            gw_t[sp][:], g_t[sp][:], gf_sb[:, fc:fc + 1]),
                            deps=[g, g_f, gw_war[sp], mn_tok])
                        g_war[sp] = gw
                        sq = p.op("g", lambda e, sp=sp: e.tensor_tensor(
                            sqf[sp][:], g_t[sp][:], g_t[sp][:], ALU.mult),
                            deps=[g, sq_war[sp]])
                        gsq_war[sp] = sq
                        pend_stat.append((fc, blk, sp, sq))
                        if len(pend_stat) > 1:
                            fc2, blk2, sp2, sq2 = pend_stat.pop(0)
                            stf_stop[blk2] = p.op(
                                "t", lambda e, fc2=fc2, blk2=blk2, sp2=sp2:
                                e.matmul(stf_ps[0:1, blk2 * 512:(blk2 + 1) * 512],
                                         ones_st[:], sqf[sp2][:],
                                         start=(fc2 == 0), stop=(fc2 == FFC - 1),
                                         skip_group_check=True), deps=[sq2])
                            sq_war[sp2] = stf_stop[blk2]
                        mx_tok = p.op("v", lambda e, blk=blk, sp=sp: e.tensor_tensor(
                            mxf[:, blk * 512:(blk + 1) * 512],
                            mxf[:, blk * 512:(blk + 1) * 512], gw_t[sp][:], ALU.max),
                            deps=[gw, mx_tok])
                        mn_tok = p.op("v", lambda e, blk=blk, sp=sp: e.tensor_tensor(
                            mnf[:, blk * 512:(blk + 1) * 512],
                            mnf[:, blk * 512:(blk + 1) * 512], gw_t[sp][:], ALU.min),
                            deps=[gw, mx_tok])
                        gw_store[fc][blk] = p.dma("q", f"gst{sp}",
                                                  gw_s.ap()[fc, :, blk * 512:(blk + 1) * 512],
                                                  gw_t[sp][:], deps=[gw])
                        gw_war[sp] = gw_store[fc][blk]
                    w_war[wpar] = stop3
                while pend_stat:
                    fc2, blk2, sp2, sq2 = pend_stat.pop(0)
                    stf_stop[blk2] = p.op(
                        "t", lambda e, fc2=fc2, blk2=blk2, sp2=sp2:
                        e.matmul(stf_ps[0:1, blk2 * 512:(blk2 + 1) * 512],
                                 ones_st[:], sqf[sp2][:],
                                 start=(fc2 == 0), stop=(fc2 == FFC - 1),
                                 skip_group_check=True), deps=[sq2])
                    sq_war[sp2] = stf_stop[blk2]
                sfr = p.op("v", lambda e: e.tensor_scalar(
                    statf_row[:], stf_ps[:], 1.0 / FF, EPS_RMS_FFN, ALU.mult, ALU.add),
                    deps=stf_stop)
                p.barrier()
                p.flush()

            with ExitStack() as P4T:
                e5 = P4T.enter_context
                tp_ps = e5(nc.psum_tensor("tp4_ps", [128, 128], F32))
                tpb_ps = e5(nc.psum_tensor("tpb_ps", [128, 128], BF16))
                bc_ps = e5(nc.psum_tensor("bc5_ps", [128, 512], F32))
                amf = e5(nc.sbuf_tensor("amf", [128, 512], BF16))
                trb64 = e5(nc.sbuf_tensor("trb64", [64, 512], BF16))
                trb32 = e5(nc.sbuf_tensor("trb32", [32, 512], BF16))
                toks4 = e5(nc.sbuf_tensor("toks4", [128, 8], F32))
                rows4 = e5(nc.sbuf_tensor("rows4", [1, 1536], F32))
                rsf = p.op("s", lambda e: e.activation(statf_row[:], statf_row[:],
                                                       AF.Ln), deps=[sfr])
                rsf2 = p.op("s", lambda e: e.activation(statf_row[:], statf_row[:],
                                                        AF.Exp, scale=-0.5), deps=[rsf])
                tp_war = None
                bc_war = None
                am_war = None
                for blk in range(MYB):
                    ng = p.op("v", lambda e, blk=blk: e.tensor_scalar_mul(
                        amf[:], mnf[:, blk * 512:(blk + 1) * 512], -1.0),
                        deps=[mn_tok, am_war])
                    am = p.op("v", lambda e, blk=blk: e.tensor_tensor(
                        amf[:], amf[:], mxf[:, blk * 512:(blk + 1) * 512], ALU.max),
                        deps=[ng, mx_tok])
                    t64c = p.op("v", lambda e: e.tensor_copy(
                        trb64[:], amf[64:128, :]), deps=[am])
                    t64 = p.op("v", lambda e: e.tensor_tensor(
                        trb64[:], trb64[:], amf[0:64, :], ALU.max), deps=[t64c])
                    t32c = p.op("v", lambda e: e.tensor_copy(
                        trb32[:], trb64[32:64, :]), deps=[t64])
                    t32 = p.op("v", lambda e: e.tensor_tensor(
                        trb32[:], trb32[:], trb64[0:32, :], ALU.max), deps=[t32c])
                    am_war = t64
                    ktoks = []
                    for k in range(4):
                        tt = p.op("t", lambda e, k=k: e.transpose(
                            tpb_ps[:, 0:32], trb32[:, k * 128:(k + 1) * 128],
                            ident_b[0:32, 0:32]), deps=[t32, tp_war])
                        rm = p.op("v", lambda e, k=k: e.reduce_max(
                            toks4[:, 0:1], tpb_ps[:, 0:32], AX.X), deps=[tt])
                        Mc = p.op("v", lambda e: e.tensor_scalar_max(
                            toks4[:, 0:1], toks4[:, 0:1], 1e-5), deps=[rm])
                        R = p.op("v", lambda e: e.reciprocal(
                            toks4[:, 1:2], toks4[:, 0:1]), deps=[Mc])
                        sv = p.op("v", lambda e: e.tensor_scalar_mul(
                            toks4[:, 2:3], toks4[:, 1:2], 127.0), deps=[R])
                        siv = p.op("v", lambda e: e.tensor_scalar_mul(
                            toks4[:, 3:4], toks4[:, 0:1], INV127), deps=[Mc, sv])
                        ts1 = p.op("t", lambda e: e.transpose(
                            tp_ps[0:1, :], toks4[:, 2:3], ident_f[:]),
                            deps=[sv, tt])
                        cs1 = p.op("v", lambda e, k=k: e.tensor_copy(
                            rows4[0:1, k * 128:(k + 1) * 128], tp_ps[0:1, :]),
                            deps=[ts1])
                        ts2 = p.op("t", lambda e: e.transpose(
                            tp_ps[0:1, :], toks4[:, 3:4], ident_f[:]),
                            deps=[siv, cs1])
                        cs2 = p.op("v", lambda e, k=k: e.tensor_copy(
                            rows4[0:1, 512 + k * 128:512 + (k + 1) * 128],
                            tp_ps[0:1, :]), deps=[ts2])
                        tp_war = cs2
                        ktoks.append(cs2)
                    cr = p.op("v", lambda e, blk=blk: e.tensor_tensor(
                        rows4[0:1, 1024:1536], rows4[0:1, 512:1024],
                        statf_row[0:1, blk * 512:(blk + 1) * 512], ALU.mult),
                        deps=ktoks + [rsf2])
                    b1 = p.op("t", lambda e: e.matmul(
                        bc_ps[:], ones_bc[:], rows4[0:1, 0:512], start=True,
                        stop=True, skip_group_check=True), deps=ktoks + [bc_war])
                    c1 = p.op("v", lambda e, blk=blk: e.tensor_copy(
                        cf_bc[:, blk * 512:(blk + 1) * 512], bc_ps[:]), deps=[b1])
                    b2 = p.op("t", lambda e: e.matmul(
                        bc_ps[:], ones_bc[:], rows4[0:1, 1024:1536], start=True,
                        stop=True, skip_group_check=True), deps=[c1, cr])
                    c2 = p.op("v", lambda e, blk=blk: e.tensor_copy(
                        sif_bc[:, blk * 512:(blk + 1) * 512], bc_ps[:]), deps=[b2])
                    bc_war = c2
                    cf_tok[blk] = c2
                p.barrier()
                p.flush()

            with ExitStack() as P5:
                e6 = P5.enter_context
                o5_ps = [e6(nc.psum_tensor(f"o5_ps{i}", [128, 512], F32)) for i in range(2)]
                iqT = e6(nc.sbuf_tensor("iqT", [128, FFC * 512], BF16))
                w2_sb = [e6(nc.sbuf_tensor(f"w2_sb{i}", [128, FFC * 128], BF16))
                         for i in range(2)]
                NCH = 4
                gw_c = [e6(nc.sbuf_tensor(f"gw_c{i}", [128, 512], BF16)) for i in range(NCH)]
                yq5 = [e6(nc.sbuf_tensor(f"yq5_{i}", [128, 512], F32)) for i in range(NCH)]
                i32_5 = [e6(nc.sbuf_tensor(f"i32_5_{i}", [128, 512], I32)) for i in range(2)]
                r0n_5 = [e6(nc.sbuf_tensor(f"r0n_5_{i}", [128, 512], BF16)) for i in range(NCH)]
                sg_5 = [e6(nc.sbuf_tensor(f"sg_5_{i}", [128, 512], BF16)) for i in range(NCH)]
                cb_5 = [e6(nc.sbuf_tensor(f"cb_5_{i}", [128, 512], BF16)) for i in range(NCH)]
                mb_5 = [e6(nc.sbuf_tensor(f"mb_5_{i}", [128, 512], F32)) for i in range(NCH)]
                y1c = [e6(nc.sbuf_tensor(f"y1c{i}", [128, 512], F32)) for i in range(2)]

                gw_ld = [None] * NCH
                q1_last = [None] * NCH
                iq_tok = [None] * FFC
                iq_guard = [None] * FFC
                w2_tok = [None, None]
                w2_war = [None, None]
                o5_war = [None, None]
                y1_war5 = [None, None]
                y2_war = [None, None]
                i32_war = [None, None]
                yq_war = [None] * NCH
                mb_war = [None] * NCH
                sg_war = [None] * NCH
                cb_war = [None] * NCH
                r0_war = [None] * NCH
                for blk in range(MYB):
                    gc0 = blk * 512
                    def head5(kc0, blk):
                        gc0b = blk * 512
                        pair = (kc0, kc0 + 1)
                        hh = {}
                        for kc in pair:
                            gp = kc & 3
                            if blk == 0 and kc < 2:
                                gw_ld[gp] = p.dma("q", f"gwld{gp}", gw_c[gp][:],
                                                  gw_s.ap()[kc, :, gc0b:gc0b + 512],
                                                  deps=[gw_store[kc][blk]])
                        for kc in pair:
                            gp = kc & 3
                            hh[(kc, "q1")] = p.op(
                                "v", lambda e, gp=gp, blk=blk: e.tensor_tensor(
                                    yq5[gp][:], gw_c[gp][:],
                                    cf_bc[:, blk * 512:(blk + 1) * 512], ALU.mult),
                                deps=[gw_ld[gp], cf_tok[blk], yq_war[gp]])
                            q1_last[gp] = hh[(kc, "q1")]
                        for kc in pair:
                            nkc, nblk = kc + 2, blk
                            if nkc >= FFC:
                                nkc, nblk = kc + 2 - FFC, blk + 1
                            gpn = nkc & 3
                            if nblk < MYB:
                                gw_ld[gpn] = p.dma("q", f"gwld{gpn}", gw_c[gpn][:],
                                                   gw_s.ap()[nkc, :, nblk * 512:nblk * 512 + 512],
                                                   deps=[q1_last[gpn], gw_store[nkc][nblk]])
                        for kc in pair:
                            gp = kc & 3
                            ip = kc & 1
                            c1 = p.op("s", lambda e, gp=gp, ip=ip: e.copy(
                                i32_5[ip][:], yq5[gp][:]),
                                deps=[hh[(kc, "q1")], i32_war[ip]])
                            c2 = p.op("s", lambda e, gp=gp, ip=ip: e.activation(
                                r0n_5[gp][:], i32_5[ip][:], AF.Copy, scale=-1.0),
                                deps=[c1, r0_war[gp]])
                            i32_war[ip] = c2
                            hh[(kc, "c2")] = c2
                        for kc in pair:
                            gp = kc & 3
                            hh[(kc, "dd")] = p.op("g", lambda e, gp=gp: e.tensor_tensor(
                                mb_5[gp][:], yq5[gp][:], r0n_5[gp][:], ALU.add),
                                deps=[hh[(kc, "c2")], mb_war[gp]])
                        for kc in pair:
                            gp = kc & 3
                            sg = p.op("s", lambda e, gp=gp: e.activation(
                                sg_5[gp][:], mb_5[gp][:], AF.Sign),
                                deps=[hh[(kc, "dd")], sg_war[gp]])
                            hh[(kc, "sg")] = sg
                            hh[(kc, "ab")] = p.op("s", lambda e, gp=gp: e.activation(
                                mb_5[gp][:], mb_5[gp][:], AF.Abs), deps=[sg])
                        lvh = {kc: hh[(kc, "ab")] for kc in pair}
                        for li, w in enumerate((128.0, 64.0, 32.0, 16.0, 8.0, 4.0)):
                            for kc in pair:
                                gp = kc & 3
                                ddl = [lvh[kc]]
                                if li == 0:
                                    ddl.append(cb_war[gp])
                                lvh[kc] = p.op("v", lambda e, gp=gp, w=w:
                                               e.tensor_scalar(cb_5[gp][:], mb_5[gp][:],
                                                               w, w, ALU.is_gt,
                                                               ALU.mult), deps=ddl)
                            for kc in pair:
                                gp = kc & 3
                                lvh[kc] = p.op("g" if li % 2 == 0 else "v",
                                               lambda e, gp=gp:
                                               e.tensor_tensor(mb_5[gp][:], mb_5[gp][:],
                                                               cb_5[gp][:],
                                                               ALU.subtract),
                                               deps=[lvh[kc]])
                        for kc in pair:
                            hh[(kc, "lv")] = lvh[kc]
                        return hh

                    def tail5(kc0, blk, hh):
                        pair = (kc0, kc0 + 1)
                        cmh = {}
                        for kc in pair:
                            gp = kc & 3
                            cmh[kc] = p.op("v", lambda e, gp=gp: e.tensor_single_scalar(
                                cb_5[gp][:], mb_5[gp][:], 0.5, ALU.is_gt),
                                deps=[hh[(kc, "lv")], cb_war[gp]])
                        for thr in (1.5, 2.5, 3.5):
                            for kc in pair:
                                gp = kc & 3
                                cmh[kc] = p.op("v", lambda e, gp=gp, thr=thr:
                                               e.scalar_tensor_tensor(
                                                   cb_5[gp][:], mb_5[gp][:], thr,
                                                   cb_5[gp][:], ALU.is_gt, ALU.add),
                                               deps=[cmh[kc]])
                        for kc in pair:
                            gp = kc & 3
                            fr = p.op("g", lambda e, gp=gp: e.tensor_tensor(
                                mb_5[gp][:], mb_5[gp][:], cb_5[gp][:], ALU.subtract),
                                deps=[cmh[kc]])
                            cb_war[gp] = fr
                            cmh[kc] = fr
                        for kc in pair:
                            gp = kc & 3
                            sfr = p.op("g", lambda e, gp=gp: e.tensor_tensor(
                                mb_5[gp][:], mb_5[gp][:], sg_5[gp][:], ALU.mult),
                                deps=[cmh[kc]])
                            sg_war[gp] = sfr
                            cmh[kc] = sfr
                        for kc in pair:
                            gp = kc & 3
                            rfin = p.op("v", lambda e, gp=gp: e.tensor_tensor(
                                mb_5[gp][:], yq5[gp][:], mb_5[gp][:], ALU.subtract),
                                deps=[cmh[kc]])
                            yq_war[gp] = rfin
                            iq = p.op("v", lambda e, kc=kc, blk=blk, gp=gp:
                                      e.tensor_tensor(
                                          iqT[:, kc * 512:(kc + 1) * 512], mb_5[gp][:],
                                          sif_bc[:, blk * 512:(blk + 1) * 512],
                                          ALU.mult),
                                      deps=[rfin, cf_tok[blk], iq_guard[kc]])
                            iq_tok[kc] = iq
                            mb_war[gp] = iq
                            r0_war[gp] = hh[(kc, "dd")]

                    prev = None
                    for kc0 in range(0, FFC, 2):
                        hh = head5(kc0, blk)
                        if prev is not None:
                            pk, ph = prev
                            tail5(pk, blk, ph)
                        prev = (kc0, hh)
                    pk, ph = prev
                    tail5(pk, blk, ph)
                    if blk == 0:
                        for dc in (0, 1):
                            w2_tok[dc] = p.dma("g", f"w2ld{dc}",
                                               w2_sb[dc][:].rearrange("p (c f) -> p c f", c=FFC),
                                               w2_h.ap()[dc], deps=[])
                    for dc in range(DCN):
                        wp = dc & 1
                        ypar = dc & 1
                        yld = p.dma("q", f"y1ld{ypar}", y1c[ypar][:],
                                    y1_s.ap()[dc, :, gc0:gc0 + 512],
                                    deps=[y1_war5[ypar]])
                        stop = None
                        for kc in range(FFC):
                            dd2 = [iq_tok[kc], w2_tok[wp]]
                            if kc == 0:
                                dd2.append(o5_war[wp])
                            stop = p.op("t", lambda e, kc=kc, wp=wp: e.matmul(
                                o5_ps[wp][:],
                                w2_sb[wp][:, kc * 128:(kc + 1) * 128],
                                iqT[:, kc * 512:(kc + 1) * 512],
                                start=(kc == 0), stop=(kc == FFC - 1),
                                skip_group_check=True), deps=dd2)
                            if dc == DCN - 1:
                                iq_guard[kc] = stop
                        ndc = dc + 2
                        if not (blk == MYB - 1 and ndc >= DCN):
                            nb = blk if ndc < DCN else blk + 1
                            w2_tok[wp] = p.dma("g", f"w2ld{wp}",
                                               w2_sb[wp][:].rearrange("p (c f) -> p c f", c=FFC),
                                               w2_h.ap()[ndc % DCN], deps=[stop])
                        y2 = p.op("v", lambda e, wp=wp, ypar=ypar: e.tensor_tensor(
                            y1c[ypar][:], o5_ps[wp][:], y1c[ypar][:], ALU.add),
                            deps=[stop, yld, y2_war[ypar]])
                        o5_war[wp] = y2
                        y2_war[ypar] = p.dma("q", f"yst{ypar}",
                                             y_h.ap()[dc * 128:(dc + 1) * 128,
                                                      gc0:gc0 + 512],
                                             y1c[ypar][:], deps=[y2])
                        y1_war5[ypar] = y2_war[ypar]
                p.barrier()
                p.flush()
            SB2.close()

        p.barrier()
        p.op("q", lambda e: e.nop(), deps=[], inc=None)
        p.flush()

    return nc


def _own_idx(T, j):
    MYT = T // 2
    return np.concatenate([np.arange((2 * Z + j) * 512, (2 * Z + j + 1) * 512)
                           for Z in range(MYT // 512)])


def _prep_core(inputs, b, j, T):
    x = np.asarray(inputs["L0_attn_hidden_states"][b])
    MYT = T // 2
    idx = _own_idx(T, j)
    x_own = np.ascontiguousarray(x[idx])
    xT = np.ascontiguousarray(x.T).reshape(DCN, 128, T)
    x_ownT = np.ascontiguousarray(x_own.T).reshape(DCN, 128, MYT)

    jj = np.arange(512)[None, :]
    ii = np.arange(128)[:, None]
    trils = [(jj >= ii + off) for off in (0, 128, 256, 384)]
    if j == 0:
        tiles = trils + [np.zeros((128, 512), bool)] * 4
    else:
        tiles = [np.ones((128, 512), bool)] * 4 + trils
    dmask = np.stack([t.astype(ml_dtypes.bfloat16) for t in tiles])

    qb = np.asarray(inputs["L0_attn_q_b_w"])
    cols = [c for h in range(NH) for c in range(h * 192, h * 192 + 128)] + \
           [c for h in range(NH) for c in range(h * 192 + 128, h * 192 + 192)]
    qbw = np.ascontiguousarray(qb[:, cols])
    kva = np.ascontiguousarray(np.asarray(inputs["L0_attn_kv_a_w"]))
    kvb = np.asarray(inputs["L0_attn_kv_b_w"])
    colsb = [c for h in range(NH) for c in range(h * 256, h * 256 + 128)] + \
            [c for h in range(NH) for c in range(h * 256 + 128, h * 256 + 256)]
    kvbw = np.ascontiguousarray(kvb[:, colsb])

    def tile_in(w, n_in_c):
        d_in, f = w.shape
        return np.ascontiguousarray(w.reshape(n_in_c, 128, f).transpose(1, 0, 2))

    qaw = tile_in(np.asarray(inputs["L0_attn_q_a_w"]), DCN)
    qbw_t = tile_in(qbw, QLC)
    kvaw_t = tile_in(kva, DCN)
    kvbw_t = tile_in(kvbw, KLC)

    ow = np.ascontiguousarray(
        np.asarray(inputs["L0_attn_o_w"]).reshape(DCN, 128, DCN, 128)
        .transpose(2, 1, 0, 3))
    w13 = np.asarray(inputs["L0_ffn_w13"])
    w1w = np.ascontiguousarray(
        w13[:FF].T.reshape(DCN, 128, FFC, 128).transpose(2, 1, 0, 3))
    w3w = np.ascontiguousarray(
        w13[FF:].T.reshape(DCN, 128, FFC, 128).transpose(2, 1, 0, 3))
    w2w = np.ascontiguousarray(
        np.asarray(inputs["L0_ffn_w2"]).T.reshape(FFC, 128, DCN, 128)
        .transpose(2, 1, 0, 3))

    gq = np.ascontiguousarray(np.asarray(inputs["L0_attn_q_a_ln_w"]).reshape(QLC, 128))
    gkv = np.ascontiguousarray(np.asarray(inputs["L0_attn_kv_a_ln_w"]).reshape(KLC, 128))
    gf = np.ascontiguousarray(np.asarray(inputs["L0_ffn_norm_w"]).reshape(FFC, 128))

    return {
        "xT": xT, "x_nat": np.ascontiguousarray(x), "x_own": x_own,
        "x_ownT": x_ownT, "dmask": dmask,
        "qaw": qaw, "qbw": qbw_t, "kvaw": kvaw_t, "kvbw": kvbw_t,
        "ow": ow, "w1w": w1w, "w3w": w3w, "w2w": w2w,
        "gq": gq, "gkv": gkv, "gf": gf,
    }


def kernel(**inputs):
    T = int(np.asarray(inputs["L0_attn_hidden_states"]).shape[1])
    B = int(np.asarray(inputs["L0_attn_hidden_states"]).shape[0])
    n_cores = 8
    nc = build_nc(T=T, n_cores=n_cores)
    in_maps = []
    for c in range(n_cores):
        b, j = c // 2, c % 2
        in_maps.append(_prep_core(inputs, b, j, T))
    res = run_bass_kernel_spmd(nc, in_maps, core_ids=list(range(n_cores)))
    out = np.empty((B, T, D), dtype=np.float32)
    for c in range(n_cores):
        b, j = c // 2, c % 2
        out[b, _own_idx(T, j), :] = res.results[c]["y"].T
    return out


# version 12
# speedup vs baseline: 1.0285x; 1.0285x over previous
import math
import numpy as np
import ml_dtypes
from contextlib import ExitStack

import concourse.bass as bass
import concourse.mybir as mybir
from concourse.bass_utils import run_bass_kernel_spmd

F32 = mybir.dt.float32
BF16 = mybir.dt.bfloat16
I32 = mybir.dt.int32
AF = mybir.ActivationFunctionType
ALU = mybir.AluOpType
AX = mybir.AxisListType

D = 2048
DCN = 16
QL = 1536
QLC = 12
KL = 512
KLC = 4
NH = 16
FF = 5632
FFC = 44
TB = 512
SCALE = float(np.float32(1.0) / np.float32(np.sqrt(np.float32(192.0))))
EPS_LN = 1e-5
EPS_RMS_ATTN = 1e-6
EPS_RMS_FFN = 1e-5
INV127 = 1.0 / 127.0

ENG_SEM = {"t": "pe", "v": "vs", "s": "ss", "g": "gs"}


class Plan:

    def __init__(self, nc):
        self.nc = nc
        self.pend = []
        self.sems = {}
        self.cnt = {}
        self.wm = {k: {} for k in "tvsgq"}
        for name in ("pe", "vs", "ss", "gs", "cc"):
            self._sem(name)

    def _sem(self, name):
        if name not in self.sems:
            self.sems[name] = self.nc.alloc_semaphore(name)
            self.cnt[name] = 0
        return self.sems[name]

    def _waits(self, eng, deps):
        need = {}
        for d in deps:
            if d is None:
                continue
            s, v = d
            if self.wm[eng].get(s, 0) >= v:
                continue
            if s not in ENG_SEM.values() and s != "cc":
                assert v == self.cnt[s], (
                    f"unsound DMA wait: {s} at {v}, latest {self.cnt[s]}")
            if need.get(s, 0) < v:
                need[s] = v
        items = sorted(need.items())
        attach = None
        if items:
            attach = items[-1]
            for s, v in items[:-1]:
                sem = self.sems[s]
                self.pend.append((eng, lambda e, sem=sem, v=v: e.wait_ge(sem, v)))
                self.wm[eng][s] = v
            self.wm[eng][attach[0]] = attach[1]
        return attach

    def op(self, eng, fn, deps=(), inc="auto", inc_val=1):
        attach = self._waits(eng, deps)
        sem_name = ENG_SEM.get(eng) if inc == "auto" else inc
        tok = None
        if sem_name:
            self._sem(sem_name)
            self.cnt[sem_name] += inc_val
            tok = (sem_name, self.cnt[sem_name])
        a_sem = self.sems[attach[0]] if attach else None
        a_val = attach[1] if attach else 0
        m_sem = self.sems[sem_name] if sem_name else None

        def run(e, fn=fn, a_sem=a_sem, a_val=a_val, m_sem=m_sem, iv=inc_val):
            inst = fn(e)
            if a_sem is not None:
                inst._wait_ge(a_sem, a_val)
            if m_sem is not None:
                inst.then_inc(m_sem, iv)
        self.pend.append((eng, run))
        return tok

    def latest(self, stream):
        if stream not in self.cnt or self.cnt[stream] == 0:
            return None
        return (stream, self.cnt[stream])

    def dma(self, eng, stream, out, in_, deps=()):
        self._sem(stream)
        return self.op(eng, lambda e: e.dma_start(out=out, in_=in_), deps,
                       inc=stream, inc_val=16)

    def barrier(self):
        toks = [(s, self.cnt[s]) for s in self.sems if self.cnt[s] > 0]
        for eng in "tvsgq":
            need = {}
            for s, v in toks:
                if s == ENG_SEM.get(eng):
                    continue
                if self.wm[eng].get(s, 0) >= v:
                    continue
                need[s] = v
            for s, v in sorted(need.items()):
                sem = self.sems[s]
                self.pend.append((eng, lambda e, sem=sem, v=v: e.wait_ge(sem, v)))
                self.wm[eng][s] = v

    def flush(self):
        if not self.pend:
            return
        pend, self.pend = self.pend, []
        with self.nc.Block() as block:
            for key, meth in (("t", block.tensor), ("v", block.vector),
                              ("s", block.scalar), ("g", block.gpsimd),
                              ("q", block.sync)):
                ops = [f for (k, f) in pend if k == key]
                if ops:
                    meth(lambda e, ops=ops: [f(e) for f in ops])


def build_nc(T=4096, n_cores=8):
    NTB = T // TB
    MYT = T // 2
    MYB = MYT // TB
    nc = bass.Bass("TRN2", target_bir_lowering=False)

    xT_h = nc.dram_tensor("xT", [DCN, 128, T], F32, kind="ExternalInput")
    xn_h = nc.dram_tensor("x_nat", [T, D], F32, kind="ExternalInput")
    xo_h = nc.dram_tensor("x_own", [MYT, D], F32, kind="ExternalInput")
    xoT_h = nc.dram_tensor("x_ownT", [DCN, 128, MYT], F32, kind="ExternalInput")
    dmask_h = nc.dram_tensor("dmask", [8, 128, 512], BF16, kind="ExternalInput")
    qaw_h = nc.dram_tensor("qaw", [128, DCN, QL], F32, kind="ExternalInput")
    qbw_h = nc.dram_tensor("qbw", [128, QLC, 3072], F32, kind="ExternalInput")
    kvaw_h = nc.dram_tensor("kvaw", [128, DCN, QL], F32, kind="ExternalInput")
    kvbw_h = nc.dram_tensor("kvbw", [128, KLC, 4096], F32, kind="ExternalInput")
    ow_h = nc.dram_tensor("ow", [DCN, 128, DCN, 128], F32, kind="ExternalInput")
    w1_h = nc.dram_tensor("w1w", [FFC, 128, DCN, 128], F32, kind="ExternalInput")
    w3_h = nc.dram_tensor("w3w", [FFC, 128, DCN, 128], F32, kind="ExternalInput")
    w2_h = nc.dram_tensor("w2w", [DCN, 128, FFC, 128], F32, kind="ExternalInput")
    gq_h = nc.dram_tensor("gq", [QLC, 128], F32, kind="ExternalInput")
    gkv_h = nc.dram_tensor("gkv", [KLC, 128], F32, kind="ExternalInput")
    gf_h = nc.dram_tensor("gf", [FFC, 128], F32, kind="ExternalInput")

    y_h = nc.dram_tensor("y", [D, MYT], F32, kind="ExternalOutput")

    qT_s = nc.dram_tensor("qT_s", [3072, MYT], BF16)
    kTn_s = nc.dram_tensor("kTn_s", [2048, T], BF16)
    kTr_s = nc.dram_tensor("kTr_s", [1024, T], BF16)
    v_s = nc.dram_tensor("v_s", [NH, 128, T // 128, 128], BF16)
    attnT_s = nc.dram_tensor("attnT_s", [2048, MYT], BF16)
    y1_s = nc.dram_tensor("y1_s", [DCN, 128, MYT], F32)
    gw_s = nc.dram_tensor("gw_s", [FFC, 128, MYT], BF16)

    eye_f_h = nc.inline_tensor(np.eye(128, dtype=np.float32), "eye_f")
    eye_b_h = nc.inline_tensor(np.eye(128).astype(ml_dtypes.bfloat16), "eye_b")
    onescol_h = nc.inline_tensor(np.ones((128, 1)).astype(ml_dtypes.bfloat16), "onescol")
    onesrow_h = nc.inline_tensor(np.ones((1, 128), dtype=np.float32), "onesrow")

    p = Plan(nc)

    with ExitStack() as PERS:
        ep = PERS.enter_context
        ident_f = ep(nc.sbuf_tensor("ident_f", [128, 128], F32))
        ident_b = ep(nc.sbuf_tensor("ident_b", [128, 128], BF16))
        ones_st = ep(nc.sbuf_tensor("ones_st", [128, 1], BF16))
        ones_bc = ep(nc.sbuf_tensor("ones_bc", [1, 128], F32))
        mask_sb = ep(nc.sbuf_tensor("mask_sb", [128, 8 * 512], BF16))
        gq_sb = ep(nc.sbuf_tensor("gq_sb", [128, QLC], F32))
        gkv_sb = ep(nc.sbuf_tensor("gkv_sb", [128, KLC], F32))
        gf_sb = ep(nc.sbuf_tensor("gf_sb", [128, FFC], F32))

        p.dma("q", "ini", ident_f[:], eye_f_h.ap())
        p.dma("q", "ini", ident_b[:], eye_b_h.ap())
        p.dma("q", "ini", ones_st[:], onescol_h.ap())
        p.dma("q", "ini", ones_bc[:], onesrow_h.ap())
        p.dma("q", "ini", mask_sb[:].rearrange("p (m c) -> p m c", m=8),
              dmask_h.ap().rearrange("m p c -> p m c"))

        with ExitStack() as P0:
            e0 = P0.enter_context
            tp_ps = e0(nc.psum_tensor("tp0_ps", [128, 128], F32))
            gtmp = e0(nc.sbuf_tensor("gtmp", [FFC, 128], F32))
            p.barrier()
            gtoks = []
            prev = None
            for gh, gsb, n in ((gq_h, gq_sb, QLC), (gkv_h, gkv_sb, KLC),
                               (gf_h, gf_sb, FFC)):
                tok = p.dma("q", f"gld{n}", gtmp[0:n, :], gh.ap(), deps=[prev])
                tt = p.op("t", lambda e, n=n: e.transpose(
                    tp_ps[:, 0:n], gtmp[0:n, :], ident_f[0:n, 0:n]),
                    deps=[tok, prev])
                cp = p.op("v", lambda e, gsb=gsb, n=n: e.tensor_copy(
                    gsb[:, 0:n], tp_ps[:, 0:n]), deps=[tt])
                gtoks.append(cp)
                prev = cp
            g_q, g_kv, g_f = gtoks
            p.barrier()
            p.flush()

        with ExitStack() as P1AO:
            eo = P1AO.enter_context
            qlatN = eo(nc.sbuf_tensor("qlatN", [128, QLC * MYT], BF16))
            qn_tok = [[None] * QLC for _ in range(MYB)]

            with ExitStack() as P1A:
                e1 = P1A.enter_context
                tp_ps = e1(nc.psum_tensor("tp1_ps", [128, 128], F32))
                bc_ps = e1(nc.psum_tensor("bc1_ps", [128, 512], F32))
                mm_ps = [e1(nc.psum_tensor(f"mm1_ps{i}", [128, 512], F32)) for i in range(2)]
                stat_ps = e1(nc.psum_tensor("st1_ps", [1, 512], F32))
                qaw_sb = e1(nc.sbuf_tensor("qaw_sb", [128, DCN * QL], BF16))
                xn_t = [e1(nc.sbuf_tensor(f"xn_t{i}", [128, D], F32)) for i in range(2)]
                x2_t = e1(nc.sbuf_tensor("x2_t", [128, D], F32))
                st_sc = e1(nc.sbuf_tensor("st_sc", [128, 8], F32))
                rows = e1(nc.sbuf_tensor("rows", [1, 1024], F32))
                mu_bc = e1(nc.sbuf_tensor("mu_bc", [128, 512], F32))
                rs_bc = e1(nc.sbuf_tensor("rs_bc", [128, 512], F32))
                xt_c = [e1(nc.sbuf_tensor(f"xt_c{i}", [128, 512], F32)) for i in range(3)]
                tmp_c = e1(nc.sbuf_tensor("tmp_c", [128, 512], F32))
                lnT = [e1(nc.sbuf_tensor("lnT0", [128, DCN * 512], BF16))] * 2
                qlat = e1(nc.sbuf_tensor("qlat", [128, QLC * 512], F32))
                sq_c = [e1(nc.sbuf_tensor(f"sq_c{i}", [128, 512], BF16)) for i in range(2)]
                qvar = e1(nc.sbuf_tensor("qvar", [1, 512], F32))
                qrs = e1(nc.sbuf_tensor("qrs", [1, 512], F32))
                rsq_bc = e1(nc.sbuf_tensor("rsq_bc", [128, 512], F32))

                p.dma("g", "w1a", qaw_sb[:].rearrange("p (c f) -> p c f", c=DCN),
                      qaw_h.ap())
                t_w1a = p.latest("w1a")

                state = dict(xn_war=[None, None], x2_war=None, xt_war=[None] * 3,
                             tp_war=g_f, bc_war=None, mu_war=None, rows_war=None)
                ln_war = [None, None]
                ln_tok = [[None] * DCN, [None] * DCN]
                mm_war = [None, None]
                stat_war = [None, None]

                def ln_block(tb, par, c0, src2d_h, srcT_h, tag, st):
                    for ti in range(4):
                        tpar = ti & 1
                        gt = 4 * tb + ti
                        ld = p.dma("q", f"xn{tpar}{tag}", xn_t[tpar][:],
                                   src2d_h.ap()[gt * 128:(gt + 1) * 128, :],
                                   deps=(st["xn_war"][tpar] or []))
                        s1 = p.op("v", lambda e, b=tpar, ti=ti: e.reduce_sum(
                            st_sc[:, ti:ti + 1], xn_t[b][:], AX.X), deps=[ld])
                        x2 = p.op("g", lambda e, b=tpar: e.tensor_tensor(
                            x2_t[:], xn_t[b][:], xn_t[b][:], ALU.mult),
                            deps=[ld, st["x2_war"]])
                        s2 = p.op("v", lambda e: e.reduce_sum(
                            st_sc[:, 4:5], x2_t[:], AX.X), deps=[x2])
                        st["x2_war"] = s2
                        st["xn_war"][tpar] = [s1, x2]
                        mu = p.op("v", lambda e, ti=ti: e.tensor_scalar_mul(
                            st_sc[:, ti:ti + 1], st_sc[:, ti:ti + 1], 1.0 / D),
                            deps=[s1])
                        m2 = p.op("v", lambda e, ti=ti: e.tensor_tensor(
                            st_sc[:, 6:7], st_sc[:, ti:ti + 1], st_sc[:, ti:ti + 1],
                            ALU.mult), deps=[mu])
                        var = p.op("v", lambda e: e.tensor_scalar(
                            st_sc[:, 4:5], st_sc[:, 4:5], 1.0 / D, EPS_LN,
                            ALU.mult, ALU.add), deps=[s2])
                        var2 = p.op("v", lambda e: e.tensor_tensor(
                            st_sc[:, 5:6], st_sc[:, 4:5], st_sc[:, 6:7], ALU.subtract),
                            deps=[var, m2])
                        lg = p.op("s", lambda e: e.activation(
                            st_sc[:, 7:8], st_sc[:, 5:6], AF.Ln), deps=[var2])
                        rstd = p.op("s", lambda e: e.activation(
                            st_sc[:, 5:6], st_sc[:, 7:8], AF.Exp, scale=-0.5),
                            deps=[lg])
                        tt1 = p.op("t", lambda e, ti=ti: e.transpose(
                            tp_ps[0:1, :], st_sc[:, ti:ti + 1], ident_f[:]),
                            deps=[mu, m2, st["tp_war"]])
                        cp1 = p.op("v", lambda e, ti=ti: e.tensor_copy(
                            rows[0:1, ti * 128:(ti + 1) * 128], tp_ps[0:1, :]),
                            deps=[tt1, st["rows_war"]])
                        tt2 = p.op("t", lambda e: e.transpose(
                            tp_ps[0:1, :], st_sc[:, 5:6], ident_f[:]),
                            deps=[rstd, cp1])
                        cp2 = p.op("v", lambda e, ti=ti: e.tensor_copy(
                            rows[0:1, 512 + ti * 128:512 + (ti + 1) * 128],
                            tp_ps[0:1, :]), deps=[tt2])
                        st["tp_war"] = cp2
                    b1 = p.op("t", lambda e: e.matmul(bc_ps[:], ones_bc[:],
                                                      rows[0:1, 0:512], start=True,
                                                      stop=True, skip_group_check=True),
                              deps=[st["tp_war"], st["bc_war"]])
                    mub = p.op("v", lambda e: e.tensor_copy(mu_bc[:], bc_ps[:]),
                               deps=[b1, st["mu_war"]])
                    b2 = p.op("t", lambda e: e.matmul(bc_ps[:], ones_bc[:],
                                                      rows[0:1, 512:1024], start=True,
                                                      stop=True, skip_group_check=True),
                              deps=[mub])
                    rsb = p.op("v", lambda e: e.tensor_copy(rs_bc[:], bc_ps[:]),
                               deps=[b2, st["mu_war"]])
                    st["bc_war"] = rsb
                    st["rows_war"] = b2
                    t2 = st.get("tmp_war")
                    for dc in range(DCN):
                        cr = dc % 3
                        xt = p.dma("q", f"xt{cr}{tag}", xt_c[cr][:],
                                   srcT_h.ap()[dc, :, c0:c0 + TB],
                                   deps=[st["xt_war"][cr]])
                        t1 = p.op("v", lambda e, cr=cr: e.tensor_tensor(
                            tmp_c[:], xt_c[cr][:], mu_bc[:], ALU.subtract),
                            deps=[xt, mub, t2])
                        st["xt_war"][cr] = t1
                        t2 = p.op("v", lambda e, dc=dc, par=par: e.tensor_tensor(
                            lnT[par][:, dc * 512:(dc + 1) * 512], tmp_c[:], rs_bc[:],
                            ALU.mult), deps=[t1, rsb, ln_war[par]])
                        ln_tok[par][dc] = t2
                    st["mu_war"] = t2
                    st["tmp_war"] = t2

                for tb in range(MYB):
                    par = tb & 1
                    c0 = tb * TB
                    ln_block(tb, par, c0, xo_h, xoT_h, "a", state)
                    qlat_tok = [None] * QLC
                    sq_war = stat_war
                    last_stop = None
                    stop = None
                    for fc in range(QLC):
                        bank = fc & 1
                        for dc in range(DCN):
                            dd = [ln_tok[par][dc], t_w1a]
                            if dc == 0:
                                dd.append(mm_war[bank])
                            stop = p.op("t", lambda e, fc=fc, dc=dc, bank=bank, par=par:
                                        e.matmul(mm_ps[bank][:],
                                                 qaw_sb[:, dc * QL + fc * 128:dc * QL + (fc + 1) * 128],
                                                 lnT[par][:, dc * 512:(dc + 1) * 512],
                                                 start=(dc == 0), stop=(dc == DCN - 1),
                                                 skip_group_check=True), deps=dd)
                        ev = p.op("v", lambda e, fc=fc, bank=bank: e.tensor_copy(
                            qlat[:, fc * 512:(fc + 1) * 512], mm_ps[bank][:]),
                            deps=[stop])
                        qlat_tok[fc] = ev
                        mm_war[bank] = ev
                        sp = fc & 1
                        sqt = p.op("v", lambda e, fc=fc, sp=sp: e.tensor_tensor(
                            sq_c[sp][:], qlat[:, fc * 512:(fc + 1) * 512],
                            qlat[:, fc * 512:(fc + 1) * 512], ALU.mult),
                            deps=[ev, sq_war[sp]])
                        last_stop = p.op("t", lambda e, fc=fc, sp=sp: e.matmul(
                            stat_ps[:], ones_st[:], sq_c[sp][:],
                            start=(fc == 0), stop=(fc == QLC - 1),
                            skip_group_check=True), deps=[sqt])
                        sq_war[sp] = last_stop
                    stat_war = sq_war
                    qv = p.op("v", lambda e: e.tensor_scalar(
                        qvar[:], stat_ps[:], 1.0 / QL, EPS_RMS_ATTN, ALU.mult,
                        ALU.add), deps=[last_stop])
                    ql1 = p.op("s", lambda e: e.activation(qrs[:], qvar[:], AF.Ln),
                               deps=[qv])
                    ql2 = p.op("s", lambda e: e.activation(qrs[:], qrs[:], AF.Exp,
                                                           scale=-0.5), deps=[ql1])
                    b3 = p.op("t", lambda e: e.matmul(bc_ps[:], ones_bc[:], qrs[:],
                                                      start=True, stop=True,
                                                      skip_group_check=True),
                              deps=[ql2, state["bc_war"]])
                    rqb = p.op("v", lambda e: e.tensor_copy(rsq_bc[:], bc_ps[:]),
                               deps=[b3])
                    state["bc_war"] = rqb
                    for fc in range(QLC):
                        qn_tok[tb][fc] = p.op("v", lambda e, fc=fc, tb=tb, c0=c0:
                                              e.scalar_tensor_tensor(
                                                  qlatN[:, fc * MYT + c0:fc * MYT + c0 + 512],
                                                  qlat[:, fc * 512:(fc + 1) * 512],
                                                  gq_sb[:, fc:fc + 1], rsq_bc[:],
                                                  ALU.mult, ALU.mult),
                                              deps=[qlat_tok[fc], rqb, g_q])
                    ln_war[par] = stop
                p.barrier()
                p.flush()

            with ExitStack() as P1A2:
                e1 = P1A2.enter_context
                mm_ps = [e1(nc.psum_tensor(f"mm1b_ps{i}", [128, 512], F32))
                         for i in range(2)]
                qbw_sb = e1(nc.sbuf_tensor("qbw_sb", [128, QLC * 3072], BF16))
                qstage = [e1(nc.sbuf_tensor(f"qstage{i}", [128, 512], BF16))
                          for i in range(2)]
                p.dma("g", "w1a2", qbw_sb[:].rearrange("p (c f) -> p c f", c=QLC),
                      qbw_h.ap())
                t_w1a2 = p.latest("w1a2")
                qs_war = [None, None]
                mm_war = [None, None]
                for tb in range(MYB):
                    c0 = tb * TB
                    for fc2 in range(24):
                        bank = fc2 & 1
                        stop = None
                        for kc in range(QLC):
                            dd = [qn_tok[tb][kc], t_w1a2]
                            if kc == 0:
                                dd.append(mm_war[bank])
                            stop = p.op("t", lambda e, fc2=fc2, kc=kc, bank=bank, c0=c0:
                                        e.matmul(mm_ps[bank][:],
                                                 qbw_sb[:, kc * 3072 + fc2 * 128:kc * 3072 + (fc2 + 1) * 128],
                                                 qlatN[:, kc * MYT + c0:kc * MYT + c0 + 512],
                                                 start=(kc == 0), stop=(kc == QLC - 1),
                                                 skip_group_check=True), deps=dd)
                        ev = p.op("v", lambda e, bank=bank: e.tensor_scalar_mul(
                            qstage[bank][:], mm_ps[bank][:], SCALE),
                            deps=[stop, qs_war[bank]])
                        mm_war[bank] = ev
                        qs_war[bank] = p.dma("q", f"qst{bank}",
                                             qT_s.ap()[fc2 * 128:(fc2 + 1) * 128,
                                                       c0:c0 + TB],
                                             qstage[bank][:], deps=[ev])
                p.barrier()
                p.flush()

        with ExitStack() as P1B:
            e1 = P1B.enter_context
            tp_ps = e1(nc.psum_tensor("tp2_ps", [128, 128], F32))
            bc_ps = e1(nc.psum_tensor("bc2_ps", [128, 512], F32))
            mm_ps = [e1(nc.psum_tensor(f"mm2_ps{i}", [128, 512], F32)) for i in range(2)]
            v_ps = [e1(nc.psum_tensor(f"v_ps{i}", [128, 512], F32)) for i in range(2)]
            stat_ps = e1(nc.psum_tensor("st2_ps", [1, 512], F32))
            kvaw_sb = e1(nc.sbuf_tensor("kvaw_sb", [128, DCN * QL], BF16))
            kvbw_sb = e1(nc.sbuf_tensor("kvbw_sb", [128, KLC * 4096], BF16))
            xn_t = [e1(nc.sbuf_tensor(f"xnb_t{i}", [128, D], F32)) for i in range(2)]
            x2_t = e1(nc.sbuf_tensor("x2b_t", [128, D], F32))
            st_sc = e1(nc.sbuf_tensor("stb_sc", [128, 8], F32))
            rows = e1(nc.sbuf_tensor("rowsb", [1, 1024], F32))
            mu_bc = e1(nc.sbuf_tensor("mub_bc", [128, 512], F32))
            rs_bc = e1(nc.sbuf_tensor("rsb_bc", [128, 512], F32))
            xt_c = [e1(nc.sbuf_tensor(f"xtb_c{i}", [128, 512], F32)) for i in range(3)]
            tmp_c = e1(nc.sbuf_tensor("tmpb_c", [128, 512], F32))
            lnT = [e1(nc.sbuf_tensor(f"lnTb{i}", [128, DCN * 512], BF16)) for i in range(2)]
            klat = e1(nc.sbuf_tensor("klat", [128, KLC * 512], F32))
            sq_c = [e1(nc.sbuf_tensor(f"sqb_c{i}", [128, 512], BF16)) for i in range(2)]
            kvar = e1(nc.sbuf_tensor("kvar", [1, 512], F32))
            krs = e1(nc.sbuf_tensor("krs", [1, 512], F32))
            rsk_bc = e1(nc.sbuf_tensor("rsk_bc", [128, 512], F32))
            klatN = e1(nc.sbuf_tensor("klatN", [128, KLC * 512], BF16))
            kstage = [e1(nc.sbuf_tensor(f"kstage{i}", [128, 512], BF16)) for i in range(2)]
            vstage = [e1(nc.sbuf_tensor(f"vstage{i}", [128, 2048], BF16)) for i in range(2)]

            p.dma("g", "w1b", kvaw_sb[:].rearrange("p (c f) -> p c f", c=DCN),
                  kvaw_h.ap())
            t_w1b = p.dma("g", "w1b", kvbw_sb[:].rearrange("p (c f) -> p c f", c=KLC),
                          kvbw_h.ap())

            state = dict(xn_war=[None, None], x2_war=None, xt_war=[None] * 3,
                         tp_war=None, bc_war=None, mu_war=None, rows_war=None)
            ln_war = [None, None]
            ln_tok = [[None] * DCN, [None] * DCN]
            mm_war = [None, None]
            kst_war = [None, None]
            vst_war = [None, None]
            vps_war = [None, None]
            stat_war = [None, None]

            def ln_block_b(tb, par, c0, st):
                for ti in range(4):
                    tpar = ti & 1
                    gt = 4 * tb + ti
                    ld = p.dma("q", f"xnb{tpar}", xn_t[tpar][:],
                               xn_h.ap()[gt * 128:(gt + 1) * 128, :],
                               deps=(st["xn_war"][tpar] or []))
                    s1 = p.op("v", lambda e, b=tpar, ti=ti: e.reduce_sum(
                        st_sc[:, ti:ti + 1], xn_t[b][:], AX.X), deps=[ld])
                    x2 = p.op("g", lambda e, b=tpar: e.tensor_tensor(
                        x2_t[:], xn_t[b][:], xn_t[b][:], ALU.mult),
                        deps=[ld, st["x2_war"]])
                    s2 = p.op("v", lambda e: e.reduce_sum(
                        st_sc[:, 4:5], x2_t[:], AX.X), deps=[x2])
                    st["x2_war"] = s2
                    st["xn_war"][tpar] = [s1, x2]
                    mu = p.op("v", lambda e, ti=ti: e.tensor_scalar_mul(
                        st_sc[:, ti:ti + 1], st_sc[:, ti:ti + 1], 1.0 / D), deps=[s1])
                    m2 = p.op("v", lambda e, ti=ti: e.tensor_tensor(
                        st_sc[:, 6:7], st_sc[:, ti:ti + 1], st_sc[:, ti:ti + 1],
                        ALU.mult), deps=[mu])
                    var = p.op("v", lambda e: e.tensor_scalar(
                        st_sc[:, 4:5], st_sc[:, 4:5], 1.0 / D, EPS_LN,
                        ALU.mult, ALU.add), deps=[s2])
                    var2 = p.op("v", lambda e: e.tensor_tensor(
                        st_sc[:, 5:6], st_sc[:, 4:5], st_sc[:, 6:7], ALU.subtract),
                        deps=[var, m2])
                    lg = p.op("s", lambda e: e.activation(
                        st_sc[:, 7:8], st_sc[:, 5:6], AF.Ln), deps=[var2])
                    rstd = p.op("s", lambda e: e.activation(
                        st_sc[:, 5:6], st_sc[:, 7:8], AF.Exp, scale=-0.5), deps=[lg])
                    tt1 = p.op("t", lambda e, ti=ti: e.transpose(
                        tp_ps[0:1, :], st_sc[:, ti:ti + 1], ident_f[:]),
                        deps=[mu, m2, st["tp_war"]])
                    cp1 = p.op("v", lambda e, ti=ti: e.tensor_copy(
                        rows[0:1, ti * 128:(ti + 1) * 128], tp_ps[0:1, :]),
                        deps=[tt1, st["rows_war"]])
                    tt2 = p.op("t", lambda e: e.transpose(
                        tp_ps[0:1, :], st_sc[:, 5:6], ident_f[:]), deps=[rstd, cp1])
                    cp2 = p.op("v", lambda e, ti=ti: e.tensor_copy(
                        rows[0:1, 512 + ti * 128:512 + (ti + 1) * 128],
                        tp_ps[0:1, :]), deps=[tt2])
                    st["tp_war"] = cp2
                b1 = p.op("t", lambda e: e.matmul(bc_ps[:], ones_bc[:],
                                                  rows[0:1, 0:512], start=True,
                                                  stop=True, skip_group_check=True),
                          deps=[st["tp_war"], st["bc_war"]])
                mub = p.op("v", lambda e: e.tensor_copy(mu_bc[:], bc_ps[:]),
                           deps=[b1, st["mu_war"]])
                b2 = p.op("t", lambda e: e.matmul(bc_ps[:], ones_bc[:],
                                                  rows[0:1, 512:1024], start=True,
                                                  stop=True, skip_group_check=True),
                          deps=[mub])
                rsb = p.op("v", lambda e: e.tensor_copy(rs_bc[:], bc_ps[:]),
                           deps=[b2, st["mu_war"]])
                st["bc_war"] = rsb
                st["rows_war"] = b2
                t2 = st.get("tmp_war")
                for dc in range(DCN):
                    cr = dc % 3
                    xt = p.dma("q", f"xtb{cr}", xt_c[cr][:],
                               xT_h.ap()[dc, :, c0:c0 + TB], deps=[st["xt_war"][cr]])
                    t1 = p.op("v", lambda e, cr=cr: e.tensor_tensor(
                        tmp_c[:], xt_c[cr][:], mu_bc[:], ALU.subtract),
                        deps=[xt, mub, t2])
                    st["xt_war"][cr] = t1
                    t2 = p.op("v", lambda e, dc=dc, par=par: e.tensor_tensor(
                        lnT[par][:, dc * 512:(dc + 1) * 512], tmp_c[:], rs_bc[:],
                        ALU.mult), deps=[t1, rsb, ln_war[par]])
                    ln_tok[par][dc] = t2
                st["mu_war"] = t2
                st["tmp_war"] = t2

            for tb in range(NTB):
                par = tb & 1
                c0 = tb * TB
                ln_block_b(tb, par, c0, state)
                klat_tok = [None] * KLC
                sq_war = stat_war
                last_stop = None
                last_mm = None
                for fc in range(QLC):
                    bank = fc & 1
                    stop = None
                    for dc in range(DCN):
                        dd = [ln_tok[par][dc], t_w1b]
                        if dc == 0:
                            dd.append(mm_war[bank])
                        stop = p.op("t", lambda e, fc=fc, dc=dc, bank=bank, par=par:
                                    e.matmul(mm_ps[bank][:],
                                             kvaw_sb[:, dc * QL + fc * 128:dc * QL + (fc + 1) * 128],
                                             lnT[par][:, dc * 512:(dc + 1) * 512],
                                             start=(dc == 0), stop=(dc == DCN - 1),
                                             skip_group_check=True), deps=dd)
                    last_mm = stop
                    if fc < KLC:
                        ev = p.op("v", lambda e, fc=fc, bank=bank: e.tensor_copy(
                            klat[:, fc * 512:(fc + 1) * 512], mm_ps[bank][:]),
                            deps=[stop])
                        klat_tok[fc] = ev
                        mm_war[bank] = ev
                        sp = fc & 1
                        sqt = p.op("v", lambda e, fc=fc, sp=sp: e.tensor_tensor(
                            sq_c[sp][:], klat[:, fc * 512:(fc + 1) * 512],
                            klat[:, fc * 512:(fc + 1) * 512], ALU.mult),
                            deps=[ev, sq_war[sp]])
                        last_stop = p.op("t", lambda e, fc=fc, sp=sp: e.matmul(
                            stat_ps[:], ones_st[:], sq_c[sp][:],
                            start=(fc == 0), stop=(fc == KLC - 1),
                            skip_group_check=True), deps=[sqt])
                        sq_war[sp] = last_stop
                    else:
                        sp = fc & 1
                        ev = p.op("v", lambda e, bank=bank, sp=sp: e.tensor_copy(
                            kstage[sp][:], mm_ps[bank][:]), deps=[stop, kst_war[sp]])
                        mm_war[bank] = ev
                        kst_war[sp] = p.dma("q", f"kst{sp}",
                                            kTr_s.ap()[(fc - KLC) * 128:(fc - 3) * 128,
                                                       c0:c0 + TB],
                                            kstage[sp][:], deps=[ev])
                stat_war = sq_war
                kv1 = p.op("v", lambda e: e.tensor_scalar(
                    kvar[:], stat_ps[:], 1.0 / KL, EPS_RMS_ATTN, ALU.mult, ALU.add),
                    deps=[last_stop])
                kl1 = p.op("s", lambda e: e.activation(krs[:], kvar[:], AF.Ln), deps=[kv1])
                kl2 = p.op("s", lambda e: e.activation(krs[:], krs[:], AF.Exp,
                                                       scale=-0.5), deps=[kl1])
                b4 = p.op("t", lambda e: e.matmul(bc_ps[:], ones_bc[:], krs[:],
                                                  start=True, stop=True,
                                                  skip_group_check=True),
                          deps=[kl2, state["bc_war"]])
                rkb = p.op("v", lambda e: e.tensor_copy(rsk_bc[:], bc_ps[:]), deps=[b4])
                state["bc_war"] = rkb
                kn_tok = [None] * KLC
                for fc in range(KLC):
                    kn_tok[fc] = p.op("v", lambda e, fc=fc: e.scalar_tensor_tensor(
                        klatN[:, fc * 512:(fc + 1) * 512],
                        klat[:, fc * 512:(fc + 1) * 512],
                        gkv_sb[:, fc:fc + 1], rsk_bc[:], ALU.mult, ALU.mult),
                        deps=[klat_tok[fc], rkb, g_kv])
                for fc2 in range(NH):
                    bank = fc2 & 1
                    stop = None
                    for kc in range(KLC):
                        dd = [kn_tok[kc], t_w1b]
                        if kc == 0:
                            dd.append(mm_war[bank])
                        stop = p.op("t", lambda e, fc2=fc2, kc=kc, bank=bank:
                                    e.matmul(mm_ps[bank][:],
                                             kvbw_sb[:, kc * 4096 + fc2 * 128:kc * 4096 + (fc2 + 1) * 128],
                                             klatN[:, kc * 512:(kc + 1) * 512],
                                             start=(kc == 0), stop=(kc == KLC - 1),
                                             skip_group_check=True), deps=dd)
                    sp = fc2 & 1
                    ev = p.op("v", lambda e, sp=sp, bank=bank: e.tensor_copy(
                        kstage[sp][:], mm_ps[bank][:]), deps=[stop, kst_war[sp]])
                    mm_war[bank] = ev
                    kst_war[sp] = p.dma("q", f"kst{sp}",
                                        kTn_s.ap()[fc2 * 128:(fc2 + 1) * 128, c0:c0 + TB],
                                        kstage[sp][:], deps=[ev])
                for ts in range(4):
                    vp = ts & 1
                    gchunk = 4 * tb + ts
                    evs = []
                    for half in range(4):
                        stop = None
                        for kc in range(KLC):
                            dd = [kn_tok[kc]]
                            if kc == 0:
                                dd.append(vps_war[vp])
                            stop = p.op("t", lambda e, ts=ts, kc=kc, half=half, vp=vp:
                                        e.matmul(v_ps[vp][:],
                                                 klatN[:, kc * 512 + ts * 128:kc * 512 + (ts + 1) * 128],
                                                 kvbw_sb[:, kc * 4096 + 2048 + half * 512:
                                                         kc * 4096 + 2048 + (half + 1) * 512],
                                                 start=(kc == 0), stop=(kc == KLC - 1),
                                                 skip_group_check=True), deps=dd)
                        ev = p.op("v", lambda e, vp=vp, half=half: e.tensor_copy(
                            vstage[vp][:, half * 512:(half + 1) * 512], v_ps[vp][:]),
                            deps=([stop, vst_war[vp]] if half == 0 else [stop]))
                        vps_war[vp] = ev
                        evs.append(ev)
                    vst_war[vp] = p.dma("q", f"vst{vp}",
                                        v_s.ap()[:, :, gchunk, :].rearrange("h p d -> p h d"),
                                        vstage[vp][:].rearrange("p (h d) -> p h d", h=NH),
                                        deps=evs)
                ln_war[par] = last_mm
            p.barrier()
            p.flush()

        with ExitStack() as P2:
            e2 = P2.enter_context
            sc_ps = [e2(nc.psum_tensor(f"sc_ps{i}", [128, 512], F32)) for i in range(2)]
            av_ps = e2(nc.psum_tensor("av_ps", [128, 512], F32))
            r_ps = e2(nc.psum_tensor("r_ps", [1, 512], F32))
            s2_ps = e2(nc.psum_tensor("s2_ps", [1, 512], F32))
            bc_ps = e2(nc.psum_tensor("bc3_ps", [128, 512], F32))
            kTn = [e2(nc.sbuf_tensor(f"kTn{i}", [128, T], BF16)) for i in range(2)]
            kTr = [e2(nc.sbuf_tensor(f"kTr{i}", [64, T], BF16)) for i in range(2)]
            vh = [e2(nc.sbuf_tensor(f"vh{i}", [128, T], BF16)) for i in range(2)]
            qTn = [e2(nc.sbuf_tensor(f"qTn{i}", [128, MYT], BF16)) for i in range(2)]
            qTr = [e2(nc.sbuf_tensor(f"qTr{i}", [64, MYT], BF16)) for i in range(2)]
            sqn = e2(nc.sbuf_tensor("sqn", [128, T], BF16))
            sqr = e2(nc.sbuf_tensor("sqr", [64, T], BF16))
            k2row = e2(nc.sbuf_tensor("k2row", [1, T], F32))
            q2row = e2(nc.sbuf_tensor("q2row", [1, MYT], F32))
            k2m = e2(nc.sbuf_tensor("k2m", [1, 1], F32))
            q2m = e2(nc.sbuf_tensor("q2m", [1, MYB], F32))
            ncg = e2(nc.sbuf_tensor("ncg", [1, MYB], F32))
            negc_bc = e2(nc.sbuf_tensor("negc_bc", [128, 1], F32))
            probs = [e2(nc.sbuf_tensor(f"probs{i}", [128, 512], BF16)) for i in range(4)]
            rrow = e2(nc.sbuf_tensor("rrow", [1, 512], F32))
            rinv = e2(nc.sbuf_tensor("rinv", [1, 512], F32))
            rinv_bc = e2(nc.sbuf_tensor("rinv_bc", [128, 512], F32))
            astage = [e2(nc.sbuf_tensor(f"astage{i}", [128, 512], BF16)) for i in range(2)]

            ld = [[None] * 5, [None] * 5]
            buf_war = [[None] * 5, [None] * 5]

            def load_head(h):
                par = h & 1
                ld[par][0] = p.dma("q", f"hld{par}", kTn[par][:],
                                   kTn_s.ap()[h * 128:(h + 1) * 128, :],
                                   deps=[buf_war[par][0]])
                ld[par][1] = p.dma("s", f"hlds{par}", kTr[par][:],
                                   kTr_s.ap()[h * 64:(h + 1) * 64, :],
                                   deps=[buf_war[par][1]])
                ld[par][2] = p.dma("q", f"hld{par}", vh[par][:],
                                   v_s.ap()[h].rearrange("p c d -> p (c d)"),
                                   deps=[buf_war[par][2]])
                ld[par][3] = p.dma("s", f"hlds{par}", qTn[par][:],
                                   qT_s.ap()[h * 128:(h + 1) * 128, :],
                                   deps=[buf_war[par][3]])
                ld[par][4] = p.dma("s", f"hlds{par}", qTr[par][:],
                                   qT_s.ap()[2048 + h * 64:2048 + (h + 1) * 64, :],
                                   deps=[buf_war[par][4]])
                ld[par][0] = ld[par][2]
                ld[par][1] = ld[par][4]
                ld[par][3] = ld[par][4]

            load_head(0)
            sq_war = None
            ps2_war = None
            ncg_war = None
            negc_war = None
            probs_war = [None] * 4
            sc_war = [None, None]
            av_war = None
            r_war = None
            bc_war = None
            ast_war = [None, None]
            for h in range(NH):
                par = h & 1
                if h + 1 < NH:
                    load_head(h + 1)
                sn = p.op("v", lambda e, par=par: e.tensor_tensor(
                    sqn[:], kTn[par][:], kTn[par][:], ALU.mult),
                    deps=[ld[par][0], sq_war])
                sr = p.op("v", lambda e, par=par: e.tensor_tensor(
                    sqr[:], kTr[par][:], kTr[par][:], ALU.mult),
                    deps=[ld[par][1], sq_war])
                kc_toks = []
                for c8 in range(T // 512):
                    m1 = p.op("t", lambda e, c8=c8: e.matmul(
                        s2_ps[:], ones_st[:], sqn[:, c8 * 512:(c8 + 1) * 512],
                        start=True, stop=False, skip_group_check=True),
                        deps=[sn, ps2_war])
                    m2 = p.op("t", lambda e, c8=c8: e.matmul(
                        s2_ps[:], ones_st[0:64, :], sqr[:, c8 * 512:(c8 + 1) * 512],
                        start=False, stop=True, skip_group_check=True), deps=[m1, sr])
                    cp = p.op("v", lambda e, c8=c8: e.tensor_copy(
                        k2row[:, c8 * 512:(c8 + 1) * 512], s2_ps[:]), deps=[m2])
                    ps2_war = cp
                    kc_toks.append(cp)
                k2t = p.op("v", lambda e: e.reduce_max(k2m[:], k2row[:], AX.X),
                           deps=kc_toks)
                sn2 = p.op("v", lambda e, par=par: e.tensor_tensor(
                    sqn[:, 0:MYT], qTn[par][:], qTn[par][:], ALU.mult),
                    deps=[ld[par][3]] + kc_toks)
                sr2 = p.op("v", lambda e, par=par: e.tensor_tensor(
                    sqr[:, 0:MYT], qTr[par][:], qTr[par][:], ALU.mult),
                    deps=[ld[par][4]] + kc_toks)
                qc_toks = []
                for c8 in range(MYT // 512):
                    m1 = p.op("t", lambda e, c8=c8: e.matmul(
                        s2_ps[:], ones_st[:], sqn[:, c8 * 512:(c8 + 1) * 512],
                        start=True, stop=False, skip_group_check=True),
                        deps=[sn2, ps2_war])
                    m2 = p.op("t", lambda e, c8=c8: e.matmul(
                        s2_ps[:], ones_st[0:64, :], sqr[:, c8 * 512:(c8 + 1) * 512],
                        start=False, stop=True, skip_group_check=True), deps=[m1, sr2])
                    cp = p.op("v", lambda e, c8=c8: e.tensor_copy(
                        q2row[:, c8 * 512:(c8 + 1) * 512], s2_ps[:]), deps=[m2])
                    ps2_war = cp
                    qc_toks.append(cp)
                sq_war = sr2
                qm = p.op("v", lambda e: e.reduce_max(
                    q2m[:], q2row[:].rearrange("o (q f) -> o q f", q=MYB), AX.X),
                    deps=qc_toks)
                pr = p.op("v", lambda e: e.tensor_single_scalar(
                    ncg[:], q2m[:], k2m[:], ALU.mult), deps=[qm, k2t, ncg_war])
                cl = p.op("s", lambda e: e.activation(ncg[:], ncg[:], AF.Ln), deps=[pr])
                ce = p.op("s", lambda e: e.activation(ncg[:], ncg[:], AF.Exp, scale=0.5),
                          deps=[cl])
                ngc = p.op("v", lambda e: e.tensor_scalar(
                    ncg[:], ncg[:], -1.0, -1.0, ALU.mult, ALU.add), deps=[ce])
                for Qi in range(MYB):
                    ncch = 8 * (Qi + 1)
                    qc0 = Qi * 512
                    bq = p.op("t", lambda e, Qi=Qi: e.matmul(
                        bc_ps[:, 0:1], ones_bc[:], ncg[0:1, Qi:Qi + 1],
                        start=True, stop=True, skip_group_check=True),
                        deps=[ngc, bc_war])
                    ncg_war = bq
                    nbc = p.op("v", lambda e: e.tensor_copy(negc_bc[:], bc_ps[:, 0:1]),
                               deps=[bq, negc_war])
                    bc_war = nbc
                    av_stop = None
                    r_stop = None
                    ex = None
                    mm1 = mm2 = None
                    pend_av = None
                    for ci in range(ncch + 1):
                      if ci < ncch:
                        bank = ci & 1
                        slot = ci & 3
                        mm1 = p.op("t", lambda e, ci=ci, bank=bank, par=par, qc0=qc0:
                                   e.matmul(sc_ps[bank][:],
                                            kTn[par][:, ci * 128:(ci + 1) * 128],
                                            qTn[par][:, qc0:qc0 + 512],
                                            start=True, stop=False,
                                            skip_group_check=True),
                                   deps=[ld[par][0], ld[par][3], sc_war[bank]])
                        mm2 = p.op("t", lambda e, ci=ci, bank=bank, par=par, qc0=qc0:
                                   e.matmul(sc_ps[bank][:],
                                            kTr[par][:, ci * 128:(ci + 1) * 128],
                                            qTr[par][:, qc0:qc0 + 512],
                                            start=False, stop=True,
                                            skip_group_check=True),
                                   deps=[ld[par][1], ld[par][4]])
                        ex = p.op("s", lambda e, bank=bank, slot=slot: e.activation(
                            probs[slot][:], sc_ps[bank][:], AF.Exp,
                            bias=negc_bc[:], scale=1.0),
                            deps=[mm2, nbc, probs_war[slot]])
                        sc_war[bank] = ex
                        ptok = ex
                        if ci >= ncch - 8:
                            off = ci - (ncch - 8)
                            ptok = p.op("v", lambda e, slot=slot, off=off: e.tensor_tensor(
                                probs[slot][:], probs[slot][:],
                                mask_sb[:, off * 512:(off + 1) * 512], ALU.mult),
                                deps=[ex])
                      if pend_av is not None:
                        cj, slot_j, ptok_j = pend_av
                        avd = [ptok_j, ld[par][2]]
                        if cj == 0:
                            avd.append(av_war)
                        av_stop = p.op("t", lambda e, cj=cj, slot=slot_j, par=par, n=ncch:
                                       e.matmul(av_ps[:],
                                                vh[par][:, cj * 128:(cj + 1) * 128],
                                                probs[slot][:],
                                                start=(cj == 0), stop=(cj == n - 1),
                                                skip_group_check=True), deps=avd)
                        rd = [ptok_j]
                        if cj == 0:
                            rd.append(r_war)
                        r_stop = p.op("t", lambda e, cj=cj, slot=slot_j, n=ncch:
                                      e.matmul(r_ps[:], ones_st[:], probs[slot][:],
                                               start=(cj == 0), stop=(cj == n - 1),
                                               skip_group_check=True), deps=rd)
                        probs_war[slot_j] = r_stop
                      if ci < ncch:
                        pend_av = (ci, slot, ptok)
                    rr = p.op("v", lambda e: e.tensor_copy(rrow[:], r_ps[:]),
                              deps=[r_stop])
                    r_war = rr
                    ri = p.op("v", lambda e: e.reciprocal(rinv[:], rrow[:]), deps=[rr])
                    bq2 = p.op("t", lambda e: e.matmul(
                        bc_ps[:], ones_bc[:], rinv[:], start=True, stop=True,
                        skip_group_check=True), deps=[ri, nbc])
                    rbc = p.op("v", lambda e: e.tensor_copy(rinv_bc[:], bc_ps[:]),
                               deps=[bq2, av_war])
                    bc_war = rbc
                    apar = Qi & 1
                    ast = p.op("v", lambda e, apar=apar: e.tensor_tensor(
                        astage[apar][:], av_ps[:], rinv_bc[:], ALU.mult),
                        deps=[av_stop, rbc, ast_war[apar]])
                    av_war = ast
                    ast_war[apar] = p.dma("q", f"ast{apar}",
                                          attnT_s.ap()[h * 128:(h + 1) * 128,
                                                       qc0:qc0 + 512],
                                          astage[apar][:], deps=[ast])
                    negc_war = ex
                buf_war[par] = [sc_war[0], sc_war[1], av_stop, mm1, mm2]
            p.barrier()
            p.flush()

        with ExitStack() as SB:
            eb = SB.enter_context
            xqT = eb(nc.sbuf_tensor("xqT", [128, DCN * MYT], BF16))
            xq_tok = [[None] * DCN for _ in range(MYB)]
            cf_tok = [None] * MYB

            SB2 = ExitStack()
            eb2 = SB2.enter_context
            si3_bc = eb2(nc.sbuf_tensor("si3_bc", [128, MYB * 512], BF16))

            with ExitStack() as P3:
                e3 = P3.enter_context
                o_ps = [e3(nc.psum_tensor(f"o_ps{i}", [128, 512], F32)) for i in range(2)]
                stm_ps = e3(nc.psum_tensor("stm_ps", [1, 512], F32))
                st2_ps = e3(nc.psum_tensor("st2b_ps", [1, 512], F32))
                tp_ps = e3(nc.psum_tensor("tp3_ps", [128, 128], F32))
                bc_ps = e3(nc.psum_tensor("bc4_ps", [128, 512], F32))
                at_sb = e3(nc.sbuf_tensor("at_sb", [128, DCN * 512], BF16))
                ow_sb = [e3(nc.sbuf_tensor(f"ow_sb{i}", [128, DCN * 128], BF16))
                         for i in range(2)]
                xm_c = [e3(nc.sbuf_tensor(f"xm_c{i}", [128, 512], F32)) for i in range(3)]
                y1blk = e3(nc.sbuf_tensor("y1blk", [128, DCN * 512], F32))
                y1b = [e3(nc.sbuf_tensor(f"y1b{i}", [128, 512], BF16)) for i in range(2)]
                sqy = [e3(nc.sbuf_tensor(f"sqy{i}", [128, 512], BF16)) for i in range(2)]
                mx = e3(nc.sbuf_tensor("mx", [128, 512], F32))
                mn = e3(nc.sbuf_tensor("mn", [128, 512], F32))
                tr64 = e3(nc.sbuf_tensor("tr64", [64, 512], F32))
                tr32 = e3(nc.sbuf_tensor("tr32", [32, 512], F32))
                toks = e3(nc.sbuf_tensor("toks", [128, 24], F32))
                rows3 = e3(nc.sbuf_tensor("rows3", [1, 2048], F32))
                mu2_bc = e3(nc.sbuf_tensor("mu2_bc", [128, 512], F32))
                rs2_bc = e3(nc.sbuf_tensor("rs2_bc", [128, 512], F32))
                s_bc = e3(nc.sbuf_tensor("s_bc", [128, 512], F32))
                si_bc = e3(nc.sbuf_tensor("si_bc", [128, 512], F32))
                si3f = e3(nc.sbuf_tensor("si3f", [128, 512], F32))
                tq = [e3(nc.sbuf_tensor(f"tq{i}", [128, 512], F32)) for i in range(2)]
                yq = [e3(nc.sbuf_tensor(f"yq{i}", [128, 512], F32)) for i in range(2)]
                i32_t = [e3(nc.sbuf_tensor(f"i32_t{i}", [128, 512], I32)) for i in range(2)]
                r0_t = [e3(nc.sbuf_tensor(f"r0_t{i}", [128, 512], F32)) for i in range(2)]
                d_t = [e3(nc.sbuf_tensor(f"d_t{i}", [128, 512], F32)) for i in range(2)]
                a_t = [e3(nc.sbuf_tensor(f"a_t{i}", [128, 512], F32)) for i in range(2)]
                b_t = [e3(nc.sbuf_tensor(f"b_t{i}", [128, 512], F32)) for i in range(2)]

                at_tok = None
                at_war = None
                ow_tok = [None, None]
                ow_war = [None, None]
                xm_war = [None] * 3
                ops_war = [None, None]
                stm_war = None
                st2_war = None
                y1b_war = [None, None]
                sqy_war = [None, None]
                y1_war = [None] * DCN
                y1blk_war = [None] * DCN
                tp_war = None
                bc_war = None
                mxtree_war = None
                si3f_war = None
                chain_war = [None, None]
                tq_war = [None, None]
                yq_war3 = [None, None]
                i32_war3 = [None, None]
                r0_war3 = [None, None]
                d_war3 = [None, None]
                a_war3 = [None, None]
                b_war3 = [None, None]
                for blk in range(MYB):
                    gc0 = blk * 512
                    for fc in range(DCN):
                        at_tok = p.dma("s", "atld", at_sb[:, fc * 512:(fc + 1) * 512],
                                       attnT_s.ap()[fc * 128:(fc + 1) * 128,
                                                    gc0:gc0 + 512],
                                       deps=([at_war] if fc == 0 else []))
                    mxi = p.op("v", lambda e: e.memset(mx[:], -3.0e38),
                               deps=[mxtree_war])
                    mni = p.op("v", lambda e: e.memset(mn[:], 3.0e38),
                               deps=[mxtree_war])
                    stop = None
                    for dc in range(DCN):
                        opar = dc & 1
                        ow_tok[opar] = p.dma("g", f"owl{opar}",
                                             ow_sb[opar][:].rearrange(
                                                 "p (c f) -> p c f", c=DCN),
                                             ow_h.ap()[dc], deps=[ow_war[opar]])
                        cr = dc % 3
                        xm = p.dma("q", f"xm{cr}", xm_c[cr][:],
                                   xoT_h.ap()[dc, :, gc0:gc0 + 512],
                                   deps=[xm_war[cr]])
                        for fc in range(DCN):
                            dd = [at_tok, ow_tok[opar]]
                            if fc == 0:
                                dd.append(ops_war[opar])
                            stop = p.op("t", lambda e, fc=fc, opar=opar: e.matmul(
                                o_ps[opar][:],
                                ow_sb[opar][:, fc * 128:(fc + 1) * 128],
                                at_sb[:, fc * 512:(fc + 1) * 512],
                                start=(fc == 0), stop=(fc == DCN - 1),
                                skip_group_check=True), deps=dd)
                        ow_war[opar] = stop
                        y1c = p.op("v", lambda e, dc=dc, opar=opar, cr=cr: e.tensor_tensor(
                            y1blk[:, dc * 512:(dc + 1) * 512], o_ps[opar][:],
                            xm_c[cr][:], ALU.add),
                            deps=[stop, xm, y1blk_war[dc], p.latest("y1st")])
                        ops_war[opar] = y1c
                        xm_war[cr] = y1c
                        mxi = p.op("v", lambda e, dc=dc: e.tensor_tensor(
                            mx[:], mx[:], y1blk[:, dc * 512:(dc + 1) * 512], ALU.max),
                            deps=[y1c, mxi])
                        mni = p.op("v", lambda e, dc=dc: e.tensor_tensor(
                            mn[:], mn[:], y1blk[:, dc * 512:(dc + 1) * 512], ALU.min),
                            deps=[y1c, mni])
                        sp = dc & 1
                        yb = p.op("s", lambda e, dc=dc, sp=sp: e.copy(
                            y1b[sp][:], y1blk[:, dc * 512:(dc + 1) * 512]),
                            deps=[y1c] + (y1b_war[sp] or []))
                        sy = p.op("g", lambda e, sp=sp: e.tensor_tensor(
                            sqy[sp][:], y1b[sp][:], y1b[sp][:], ALU.mult),
                            deps=[yb, sqy_war[sp]])
                        stm = p.op("t", lambda e, dc=dc, sp=sp: e.matmul(
                            stm_ps[:], ones_st[:], y1b[sp][:],
                            start=(dc == 0), stop=(dc == DCN - 1),
                            skip_group_check=True),
                            deps=[yb] + ([stm_war] if dc == 0 else []))
                        st2 = p.op("t", lambda e, dc=dc, sp=sp: e.matmul(
                            st2_ps[:], ones_st[:], sqy[sp][:],
                            start=(dc == 0), stop=(dc == DCN - 1),
                            skip_group_check=True),
                            deps=[sy] + ([st2_war] if dc == 0 else []))
                        y1b_war[sp] = [stm, sy]
                        sqy_war[sp] = st2
                        y1_war[dc] = p.dma("q", "y1st", y1_s.ap()[dc, :, gc0:gc0 + 512],
                                           y1blk[:, dc * 512:(dc + 1) * 512],
                                           deps=[y1c])
                    at_war = stop
                    muv = p.op("v", lambda e: e.tensor_scalar_mul(
                        rows3[0:1, 0:512], stm_ps[:], 1.0 / D), deps=[stm])
                    stm_war = muv
                    ex2 = p.op("v", lambda e: e.tensor_scalar_mul(
                        rows3[0:1, 512:1024], st2_ps[:], 1.0 / D), deps=[st2])
                    st2_war = ex2
                    ms2 = p.op("v", lambda e: e.tensor_tensor(
                        rows3[0:1, 1024:1536], rows3[0:1, 0:512], rows3[0:1, 0:512],
                        ALU.mult), deps=[muv])
                    vr = p.op("v", lambda e: e.scalar_tensor_tensor(
                        rows3[0:1, 512:1024], rows3[0:1, 512:1024], EPS_LN,
                        rows3[0:1, 1024:1536], ALU.add, ALU.subtract),
                        deps=[ex2, ms2])
                    l1 = p.op("s", lambda e: e.activation(
                        rows3[0:1, 512:1024], rows3[0:1, 512:1024], AF.Ln), deps=[vr])
                    l2 = p.op("s", lambda e: e.activation(
                        rows3[0:1, 512:1024], rows3[0:1, 512:1024], AF.Exp,
                        scale=-0.5), deps=[l1])
                    for k in range(4):
                        t1 = p.op("t", lambda e, k=k: e.transpose(
                            tp_ps[:, 0:1], rows3[0:1, k * 128:(k + 1) * 128],
                            ident_f[0:1, 0:1]), deps=[muv, tp_war])
                        c1 = p.op("v", lambda e, k=k: e.tensor_copy(
                            toks[:, k:k + 1], tp_ps[:, 0:1]), deps=[t1])
                        t2 = p.op("t", lambda e, k=k: e.transpose(
                            tp_ps[:, 0:1],
                            rows3[0:1, 512 + k * 128:512 + (k + 1) * 128],
                            ident_f[0:1, 0:1]), deps=[l2, c1])
                        c2 = p.op("v", lambda e, k=k: e.tensor_copy(
                            toks[:, 4 + k:5 + k], tp_ps[:, 0:1]), deps=[t2])
                        tp_war = c2
                    t64c = p.op("v", lambda e: e.tensor_copy(
                        tr64[:], mx[64:128, :]), deps=[mxi])
                    t64 = p.op("v", lambda e: e.tensor_tensor(
                        tr64[:], tr64[:], mx[0:64, :], ALU.max), deps=[t64c])
                    t32c = p.op("v", lambda e: e.tensor_copy(
                        tr32[:], tr64[32:64, :]), deps=[t64])
                    t32 = p.op("v", lambda e: e.tensor_tensor(
                        tr32[:], tr32[:], tr64[0:32, :], ALU.max), deps=[t32c])
                    for k in range(4):
                        tt = p.op("t", lambda e, k=k: e.transpose(
                            tp_ps[:, 0:32], tr32[:, k * 128:(k + 1) * 128],
                            ident_f[0:32, 0:32]), deps=[t32, tp_war])
                        rm = p.op("v", lambda e, k=k: e.reduce_max(
                            toks[:, 8 + k:9 + k], tp_ps[:, 0:32], AX.X), deps=[tt])
                        tp_war = rm
                    t64nc = p.op("v", lambda e: e.tensor_copy(
                        tr64[:], mn[64:128, :]), deps=[mni, t32])
                    t64n = p.op("v", lambda e: e.tensor_tensor(
                        tr64[:], tr64[:], mn[0:64, :], ALU.min), deps=[t64nc])
                    t32nc = p.op("v", lambda e: e.tensor_copy(
                        tr32[:], tr64[32:64, :]), deps=[t64n])
                    t32n = p.op("v", lambda e: e.tensor_tensor(
                        tr32[:], tr32[:], tr64[0:32, :], ALU.min), deps=[t32nc])
                    for k in range(4):
                        tt = p.op("t", lambda e, k=k: e.transpose(
                            tp_ps[:, 0:32], tr32[:, k * 128:(k + 1) * 128],
                            ident_f[0:32, 0:32]), deps=[t32n, tp_war])
                        rm = p.op("v", lambda e, k=k: e.tensor_reduce(
                            toks[:, 12 + k:13 + k], tp_ps[:, 0:32], AX.X, ALU.min,
                            negate=True), deps=[tt])
                        tp_war = rm
                    mxtree_war = tp_war
                    v1 = p.op("v", lambda e: e.tensor_tensor(
                        toks[:, 16:20], toks[:, 8:12], toks[:, 0:4], ALU.subtract),
                        deps=[tp_war])
                    v2 = p.op("v", lambda e: e.tensor_tensor(
                        toks[:, 20:24], toks[:, 12:16], toks[:, 0:4], ALU.add),
                        deps=[v1])
                    v3 = p.op("v", lambda e: e.tensor_tensor(
                        toks[:, 16:20], toks[:, 16:20], toks[:, 20:24], ALU.max),
                        deps=[v2])
                    v4 = p.op("v", lambda e: e.tensor_tensor(
                        toks[:, 16:20], toks[:, 16:20], toks[:, 4:8], ALU.mult),
                        deps=[v3])
                    v5 = p.op("v", lambda e: e.tensor_scalar_max(
                        toks[:, 16:20], toks[:, 16:20], 1e-5), deps=[v4])
                    v6 = p.op("v", lambda e: e.reciprocal(
                        toks[:, 20:24], toks[:, 16:20]), deps=[v5])
                    v7 = p.op("v", lambda e: e.tensor_scalar_mul(
                        toks[:, 20:24], toks[:, 20:24], 127.0), deps=[v6])
                    v8 = p.op("v", lambda e: e.tensor_scalar_mul(
                        toks[:, 16:20], toks[:, 16:20], INV127), deps=[v5, v7])
                    for k in range(4):
                        t1 = p.op("t", lambda e, k=k: e.transpose(
                            tp_ps[0:1, :], toks[:, 20 + k:21 + k], ident_f[:]),
                            deps=[v7, tp_war])
                        c1 = p.op("v", lambda e, k=k: e.tensor_copy(
                            rows3[0:1, 1024 + k * 128:1024 + (k + 1) * 128],
                            tp_ps[0:1, :]), deps=[t1])
                        t2 = p.op("t", lambda e, k=k: e.transpose(
                            tp_ps[0:1, :], toks[:, 16 + k:17 + k], ident_f[:]),
                            deps=[v8, c1])
                        c2 = p.op("v", lambda e, k=k: e.tensor_copy(
                            rows3[0:1, 1536 + k * 128:1536 + (k + 1) * 128],
                            tp_ps[0:1, :]), deps=[t2])
                        tp_war = c2
                    toks_done = tp_war
                    bt = p.op("t", lambda e: e.matmul(
                        bc_ps[:], ones_bc[:], rows3[0:1, 0:512], start=True, stop=True,
                        skip_group_check=True), deps=[muv, bc_war])
                    bmu = p.op("v", lambda e: e.tensor_copy(mu2_bc[:], bc_ps[:]),
                               deps=[bt, chain_war[0], chain_war[1]])
                    bt = p.op("t", lambda e: e.matmul(
                        bc_ps[:], ones_bc[:], rows3[0:1, 512:1024], start=True,
                        stop=True, skip_group_check=True), deps=[l2, bmu])
                    brs = p.op("v", lambda e: e.tensor_copy(rs2_bc[:], bc_ps[:]),
                               deps=[bt, chain_war[1], chain_war[0]])
                    bt = p.op("t", lambda e: e.matmul(
                        bc_ps[:], ones_bc[:], rows3[0:1, 1024:1536], start=True,
                        stop=True, skip_group_check=True), deps=[toks_done, brs])
                    bs = p.op("v", lambda e: e.tensor_copy(s_bc[:], bc_ps[:]),
                              deps=[bt])
                    bt = p.op("t", lambda e: e.matmul(
                        bc_ps[:], ones_bc[:], rows3[0:1, 1536:2048], start=True,
                        stop=True, skip_group_check=True), deps=[toks_done, bs])
                    bsi = p.op("v", lambda e: e.tensor_copy(si_bc[:], bc_ps[:]),
                               deps=[bt])
                    s3a = p.op("v", lambda e, blk=blk: e.tensor_tensor(
                        si3f[:], si_bc[:], si_bc[:], ALU.mult), deps=[bsi, si3f_war])
                    s3b = p.op("v", lambda e, blk=blk: e.tensor_tensor(
                        si3_bc[:, blk * 512:(blk + 1) * 512], si3f[:], si_bc[:],
                        ALU.mult), deps=[s3a])
                    si3f_war = s3b
                    bc_war = bsi
                    for dc0 in range(0, DCN, 2):
                        pair = (dc0, dc0 + 1)
                        hh = {}
                        for dc in pair:
                            cp = dc & 1
                            hh[(dc, "q1")] = p.op(
                                "v", lambda e, dc=dc, cp=cp: e.tensor_tensor(
                                    tq[cp][:], y1blk[:, dc * 512:(dc + 1) * 512],
                                    mu2_bc[:], ALU.subtract),
                                deps=[bmu, tq_war[cp]])
                            y1blk_war[dc] = hh[(dc, "q1")]
                        for dc in pair:
                            cp = dc & 1
                            hh[(dc, "q2")] = p.op("g", lambda e, cp=cp: e.tensor_tensor(
                                tq[cp][:], tq[cp][:], rs2_bc[:], ALU.mult),
                                deps=[hh[(dc, "q1")], brs])
                        for dc in pair:
                            cp = dc & 1
                            hh[(dc, "q3")] = p.op("v", lambda e, cp=cp: e.tensor_tensor(
                                yq[cp][:], tq[cp][:], s_bc[:], ALU.mult),
                                deps=[hh[(dc, "q2")], bs, yq_war3[cp]])
                            tq_war[cp] = hh[(dc, "q3")]
                        for dc in pair:
                            cp = dc & 1
                            c1 = p.op("s", lambda e, cp=cp: e.copy(
                                i32_t[cp][:], yq[cp][:]),
                                deps=[hh[(dc, "q3")], i32_war3[cp]])
                            c2 = p.op("s", lambda e, cp=cp: e.copy(
                                r0_t[cp][:], i32_t[cp][:]), deps=[c1, r0_war3[cp]])
                            i32_war3[cp] = c2
                            hh[(dc, "c2")] = c2
                        for dc in pair:
                            cp = dc & 1
                            hh[(dc, "dd")] = p.op("g", lambda e, cp=cp: e.tensor_tensor(
                                d_t[cp][:], yq[cp][:], r0_t[cp][:], ALU.subtract),
                                deps=[hh[(dc, "c2")], d_war3[cp]])
                        for dc in pair:
                            cp = dc & 1
                            ab = p.op("s", lambda e, cp=cp: e.activation(
                                yq[cp][:], d_t[cp][:], AF.Abs), deps=[hh[(dc, "dd")]])
                            hh[(dc, "sg")] = p.op("s", lambda e, cp=cp: e.activation(
                                d_t[cp][:], d_t[cp][:], AF.Sign), deps=[ab])
                            hh[(dc, "ab")] = ab
                        cmh = {}
                        for dc in pair:
                            cp = dc & 1
                            cmh[dc] = p.op("v", lambda e, cp=cp: e.tensor_single_scalar(
                                a_t[cp][:], yq[cp][:], 0.5, ALU.is_gt),
                                deps=[hh[(dc, "ab")], a_war3[cp]])
                        for thr in (1.5, 2.5, 3.5):
                            for dc in pair:
                                cp = dc & 1
                                cmh[dc] = p.op("v", lambda e, cp=cp, thr=thr:
                                               e.scalar_tensor_tensor(
                                                   a_t[cp][:], yq[cp][:], thr,
                                                   a_t[cp][:], ALU.is_gt, ALU.add),
                                               deps=[cmh[dc]])
                        for dc in pair:
                            cp = dc & 1
                            hh[(dc, "co")] = p.op("g", lambda e, cp=cp: e.tensor_tensor(
                                b_t[cp][:], a_t[cp][:], d_t[cp][:], ALU.mult),
                                deps=[cmh[dc], hh[(dc, "sg")], b_war3[cp]])
                            yq_war3[cp] = cmh[dc]
                            a_war3[cp] = hh[(dc, "co")]
                            d_war3[cp] = hh[(dc, "co")]
                        for dc in pair:
                            cp = dc & 1
                            xq = p.op("v", lambda e, dc=dc, cp=cp, gc0=gc0:
                                      e.tensor_tensor(
                                          xqT[:, dc * MYT + gc0:dc * MYT + gc0 + 512],
                                          b_t[cp][:], r0_t[cp][:], ALU.add),
                                      deps=[hh[(dc, "co")]])
                            xq_tok[blk][dc] = xq
                            b_war3[cp] = xq
                            r0_war3[cp] = xq
                            chain_war[cp] = xq
                p.barrier()
                p.flush()

            gw_store = [[None] * MYB for _ in range(FFC)]
            cf_bc = eb2(nc.sbuf_tensor("cf_bc", [128, MYB * 512], BF16))
            sif_bc = eb2(nc.sbuf_tensor("sif_bc", [128, MYB * 512], BF16))
            mxf = eb2(nc.sbuf_tensor("mxf", [128, MYB * 512], BF16))
            mnf = eb2(nc.sbuf_tensor("mnf", [128, MYB * 512], BF16))
            statf_row = eb2(nc.sbuf_tensor("statf_row", [1, MYB * 512], BF16))
            with ExitStack() as P4:
                e4 = P4.enter_context
                f_ps = [e4(nc.psum_tensor(f"f_ps{i}", [128, 512], F32)) for i in range(4)]
                stf_ps = e4(nc.psum_tensor("stf_ps", [1, MYB * 512], F32))
                w1_sb = [e4(nc.sbuf_tensor(f"w1_sb{i}", [128, DCN * 128], BF16))
                         for i in range(2)]
                w3_sb = [e4(nc.sbuf_tensor(f"w3_sb{i}", [128, DCN * 128], BF16))
                         for i in range(2)]
                w3l_sb = [e4(nc.sbuf_tensor(f"w3l_sb{i}", [128, DCN * 128], BF16))
                          for i in range(2)]
                w3f_sb = [e4(nc.sbuf_tensor(f"w3f_sb{i}", [128, DCN * 128], F32))
                          for i in range(2)]
                u_t = [e4(nc.sbuf_tensor(f"u_t{i}", [128, 512], F32)) for i in range(2)]
                u2_t = [e4(nc.sbuf_tensor(f"u2_t{i}", [128, 512], F32)) for i in range(2)]
                g_t = [e4(nc.sbuf_tensor(f"g_t{i}", [128, 512], F32)) for i in range(2)]
                gw_t = [e4(nc.sbuf_tensor(f"gw_t{i}", [128, 512], BF16)) for i in range(2)]
                sqf = [e4(nc.sbuf_tensor(f"sqf{i}", [128, 512], BF16)) for i in range(2)]

                w1_tok = [None, None]
                w3_tok = [None, None]
                w3l_tok = [None, None]
                w3f_war = [None, None]
                w_war = [None, None]
                fps_war = [None] * 4
                u_war = [None, None]
                g_war = [None, None]
                gsq_war = [None, None]
                gw_war = [None, None]
                sq_war = [None, None]
                mx_tok = p.op("v", lambda e: e.memset(mxf[:], -3.0e38), deps=[])
                mn_tok = p.op("v", lambda e: e.memset(mnf[:], 3.0e38), deps=[])
                stf_stop = [None] * MYB
                pend_stat = []
                for fc in range(FFC):
                    wpar = fc & 1
                    w1_tok[wpar] = p.dma("g", f"wld{wpar}",
                                         w1_sb[wpar][:].rearrange("p (c f) -> p c f", c=DCN),
                                         w1_h.ap()[fc], deps=[w_war[wpar]])
                    w3_tok[wpar] = p.dma("g", f"wld{wpar}",
                                         w3_sb[wpar][:].rearrange("p (c f) -> p c f", c=DCN),
                                         w3_h.ap()[fc], deps=[])
                    wf = p.dma("q", f"wf{wpar}",
                               w3f_sb[wpar][:].rearrange("p (c f) -> p c f", c=DCN),
                               w3_h.ap()[fc], deps=[w3f_war[wpar]])
                    w3l_tok[wpar] = p.op("v", lambda e, wpar=wpar: e.tensor_tensor(
                        w3l_sb[wpar][:], w3f_sb[wpar][:], w3_sb[wpar][:],
                        ALU.subtract), deps=[wf, w3_tok[wpar], w_war[wpar]])
                    w3f_war[wpar] = w3l_tok[wpar]
                    for blk in range(MYB):
                        bp = blk & 1
                        p1b, p3b = bp, 2 + bp
                        stop1 = None
                        stop3 = None
                        for dc in range(DCN):
                            dd1 = [xq_tok[blk][dc], w3_tok[wpar]]
                            if dc == 0:
                                dd1.append(fps_war[p1b])
                            stop1 = p.op("t", lambda e, dc=dc, blk=blk, wpar=wpar, p1b=p1b:
                                         e.matmul(f_ps[p1b][:],
                                                  w1_sb[wpar][:, dc * 128:(dc + 1) * 128],
                                                  xqT[:, dc * MYT + blk * 512:
                                                      dc * MYT + (blk + 1) * 512],
                                                  start=(dc == 0), stop=(dc == DCN - 1),
                                                  skip_group_check=True), deps=dd1)
                        for dcl in range(2 * DCN):
                            dc, lo = dcl % DCN, dcl >= DCN
                            dd3 = [xq_tok[blk][dc],
                                   w3l_tok[wpar] if lo else w3_tok[wpar]]
                            if dcl == 0:
                                dd3.append(fps_war[p3b])
                            stop3 = p.op("t", lambda e, dc=dc, dcl=dcl, lo=lo, blk=blk,
                                         wpar=wpar, p3b=p3b:
                                         e.matmul(f_ps[p3b][:],
                                                  (w3l_sb if lo else w3_sb)[wpar][:, dc * 128:(dc + 1) * 128],
                                                  xqT[:, dc * MYT + blk * 512:
                                                      dc * MYT + (blk + 1) * 512],
                                                  start=(dcl == 0), stop=(dcl == 2 * DCN - 1),
                                                  skip_group_check=True), deps=dd3)
                        sp = (fc * MYB + blk) & 1
                        u = p.op("s", lambda e, sp=sp, p1b=p1b: e.activation(
                            u_t[sp][:], f_ps[p1b][:], AF.Relu), deps=[stop1, u_war[sp]])
                        u2 = p.op("v", lambda e, sp=sp: e.tensor_tensor(
                            u2_t[sp][:], u_t[sp][:], u_t[sp][:], ALU.mult), deps=[u])
                        u_war[sp] = u2
                        g = p.op("v", lambda e, sp=sp, p3b=p3b: e.tensor_tensor(
                            g_t[sp][:], u2_t[sp][:], f_ps[p3b][:], ALU.mult),
                            deps=[u2, stop3, g_war[sp], gsq_war[sp]])
                        g = p.op("v", lambda e, sp=sp, blk=blk: e.tensor_tensor(
                            g_t[sp][:], g_t[sp][:],
                            si3_bc[:, blk * 512:(blk + 1) * 512], ALU.mult),
                            deps=[g])
                        fps_war[p1b] = u
                        fps_war[p3b] = g
                        gw = p.op("v", lambda e, sp=sp, fc=fc: e.tensor_scalar_mul(
                            gw_t[sp][:], g_t[sp][:], gf_sb[:, fc:fc + 1]),
                            deps=[g, g_f, gw_war[sp], mn_tok])
                        g_war[sp] = gw
                        sq = p.op("g", lambda e, sp=sp: e.tensor_tensor(
                            sqf[sp][:], g_t[sp][:], g_t[sp][:], ALU.mult),
                            deps=[g, sq_war[sp]])
                        gsq_war[sp] = sq
                        pend_stat.append((fc, blk, sp, sq))
                        if len(pend_stat) > 1:
                            fc2, blk2, sp2, sq2 = pend_stat.pop(0)
                            stf_stop[blk2] = p.op(
                                "t", lambda e, fc2=fc2, blk2=blk2, sp2=sp2:
                                e.matmul(stf_ps[0:1, blk2 * 512:(blk2 + 1) * 512],
                                         ones_st[:], sqf[sp2][:],
                                         start=(fc2 == 0), stop=(fc2 == FFC - 1),
                                         skip_group_check=True), deps=[sq2])
                            sq_war[sp2] = stf_stop[blk2]
                        mx_tok = p.op("v", lambda e, blk=blk, sp=sp: e.tensor_tensor(
                            mxf[:, blk * 512:(blk + 1) * 512],
                            mxf[:, blk * 512:(blk + 1) * 512], gw_t[sp][:], ALU.max),
                            deps=[gw, mx_tok])
                        mn_tok = p.op("v", lambda e, blk=blk, sp=sp: e.tensor_tensor(
                            mnf[:, blk * 512:(blk + 1) * 512],
                            mnf[:, blk * 512:(blk + 1) * 512], gw_t[sp][:], ALU.min),
                            deps=[gw, mx_tok])
                        gw_store[fc][blk] = p.dma("q", f"gst{sp}",
                                                  gw_s.ap()[fc, :, blk * 512:(blk + 1) * 512],
                                                  gw_t[sp][:], deps=[gw])
                        gw_war[sp] = gw_store[fc][blk]
                    w_war[wpar] = stop3
                while pend_stat:
                    fc2, blk2, sp2, sq2 = pend_stat.pop(0)
                    stf_stop[blk2] = p.op(
                        "t", lambda e, fc2=fc2, blk2=blk2, sp2=sp2:
                        e.matmul(stf_ps[0:1, blk2 * 512:(blk2 + 1) * 512],
                                 ones_st[:], sqf[sp2][:],
                                 start=(fc2 == 0), stop=(fc2 == FFC - 1),
                                 skip_group_check=True), deps=[sq2])
                    sq_war[sp2] = stf_stop[blk2]
                sfr = p.op("v", lambda e: e.tensor_scalar(
                    statf_row[:], stf_ps[:], 1.0 / FF, EPS_RMS_FFN, ALU.mult, ALU.add),
                    deps=stf_stop)
                p.barrier()
                p.flush()

            with ExitStack() as P4T:
                e5 = P4T.enter_context
                tp_ps = e5(nc.psum_tensor("tp4_ps", [128, 128], F32))
                tpb_ps = e5(nc.psum_tensor("tpb_ps", [128, 128], BF16))
                bc_ps = e5(nc.psum_tensor("bc5_ps", [128, 512], F32))
                amf = e5(nc.sbuf_tensor("amf", [128, 512], BF16))
                trb64 = e5(nc.sbuf_tensor("trb64", [64, 512], BF16))
                trb32 = e5(nc.sbuf_tensor("trb32", [32, 512], BF16))
                toks4 = e5(nc.sbuf_tensor("toks4", [128, 8], F32))
                rows4 = e5(nc.sbuf_tensor("rows4", [1, 1536], F32))
                rsf = p.op("s", lambda e: e.activation(statf_row[:], statf_row[:],
                                                       AF.Ln), deps=[sfr])
                rsf2 = p.op("s", lambda e: e.activation(statf_row[:], statf_row[:],
                                                        AF.Exp, scale=-0.5), deps=[rsf])
                tp_war = None
                bc_war = None
                am_war = None
                for blk in range(MYB):
                    ng = p.op("v", lambda e, blk=blk: e.tensor_scalar_mul(
                        amf[:], mnf[:, blk * 512:(blk + 1) * 512], -1.0),
                        deps=[mn_tok, am_war])
                    am = p.op("v", lambda e, blk=blk: e.tensor_tensor(
                        amf[:], amf[:], mxf[:, blk * 512:(blk + 1) * 512], ALU.max),
                        deps=[ng, mx_tok])
                    t64c = p.op("v", lambda e: e.tensor_copy(
                        trb64[:], amf[64:128, :]), deps=[am])
                    t64 = p.op("v", lambda e: e.tensor_tensor(
                        trb64[:], trb64[:], amf[0:64, :], ALU.max), deps=[t64c])
                    t32c = p.op("v", lambda e: e.tensor_copy(
                        trb32[:], trb64[32:64, :]), deps=[t64])
                    t32 = p.op("v", lambda e: e.tensor_tensor(
                        trb32[:], trb32[:], trb64[0:32, :], ALU.max), deps=[t32c])
                    am_war = t64
                    ktoks = []
                    for k in range(4):
                        tt = p.op("t", lambda e, k=k: e.transpose(
                            tpb_ps[:, 0:32], trb32[:, k * 128:(k + 1) * 128],
                            ident_b[0:32, 0:32]), deps=[t32, tp_war])
                        rm = p.op("v", lambda e, k=k: e.reduce_max(
                            toks4[:, 0:1], tpb_ps[:, 0:32], AX.X), deps=[tt])
                        Mc = p.op("v", lambda e: e.tensor_scalar_max(
                            toks4[:, 0:1], toks4[:, 0:1], 1e-5), deps=[rm])
                        R = p.op("v", lambda e: e.reciprocal(
                            toks4[:, 1:2], toks4[:, 0:1]), deps=[Mc])
                        sv = p.op("v", lambda e: e.tensor_scalar_mul(
                            toks4[:, 2:3], toks4[:, 1:2], 127.0), deps=[R])
                        siv = p.op("v", lambda e: e.tensor_scalar_mul(
                            toks4[:, 3:4], toks4[:, 0:1], INV127), deps=[Mc, sv])
                        ts1 = p.op("t", lambda e: e.transpose(
                            tp_ps[0:1, :], toks4[:, 2:3], ident_f[:]),
                            deps=[sv, tt])
                        cs1 = p.op("v", lambda e, k=k: e.tensor_copy(
                            rows4[0:1, k * 128:(k + 1) * 128], tp_ps[0:1, :]),
                            deps=[ts1])
                        ts2 = p.op("t", lambda e: e.transpose(
                            tp_ps[0:1, :], toks4[:, 3:4], ident_f[:]),
                            deps=[siv, cs1])
                        cs2 = p.op("v", lambda e, k=k: e.tensor_copy(
                            rows4[0:1, 512 + k * 128:512 + (k + 1) * 128],
                            tp_ps[0:1, :]), deps=[ts2])
                        tp_war = cs2
                        ktoks.append(cs2)
                    cr = p.op("v", lambda e, blk=blk: e.tensor_tensor(
                        rows4[0:1, 1024:1536], rows4[0:1, 512:1024],
                        statf_row[0:1, blk * 512:(blk + 1) * 512], ALU.mult),
                        deps=ktoks + [rsf2])
                    b1 = p.op("t", lambda e: e.matmul(
                        bc_ps[:], ones_bc[:], rows4[0:1, 0:512], start=True,
                        stop=True, skip_group_check=True), deps=ktoks + [bc_war])
                    c1 = p.op("v", lambda e, blk=blk: e.tensor_copy(
                        cf_bc[:, blk * 512:(blk + 1) * 512], bc_ps[:]), deps=[b1])
                    b2 = p.op("t", lambda e: e.matmul(
                        bc_ps[:], ones_bc[:], rows4[0:1, 1024:1536], start=True,
                        stop=True, skip_group_check=True), deps=[c1, cr])
                    c2 = p.op("v", lambda e, blk=blk: e.tensor_copy(
                        sif_bc[:, blk * 512:(blk + 1) * 512], bc_ps[:]), deps=[b2])
                    bc_war = c2
                    cf_tok[blk] = c2
                p.barrier()
                p.flush()

            with ExitStack() as P5:
                e6 = P5.enter_context
                o5_ps = [e6(nc.psum_tensor(f"o5_ps{i}", [128, 512], F32)) for i in range(2)]
                iqT = e6(nc.sbuf_tensor("iqT", [128, FFC * 512], BF16))
                w2_sb = [e6(nc.sbuf_tensor(f"w2_sb{i}", [128, FFC * 128], BF16))
                         for i in range(2)]
                NCH = 4
                gw_c = [e6(nc.sbuf_tensor(f"gw_c{i}", [128, 512], BF16)) for i in range(NCH)]
                yq5 = [e6(nc.sbuf_tensor(f"yq5_{i}", [128, 512], F32)) for i in range(NCH)]
                i32_5 = [e6(nc.sbuf_tensor(f"i32_5_{i}", [128, 512], I32)) for i in range(2)]
                r0n_5 = [e6(nc.sbuf_tensor(f"r0n_5_{i}", [128, 512], BF16)) for i in range(NCH)]
                sg_5 = [e6(nc.sbuf_tensor(f"sg_5_{i}", [128, 512], BF16)) for i in range(NCH)]
                cb_5 = [e6(nc.sbuf_tensor(f"cb_5_{i}", [128, 512], BF16)) for i in range(NCH)]
                mb_5 = [e6(nc.sbuf_tensor(f"mb_5_{i}", [128, 512], F32)) for i in range(NCH)]
                y1c = [e6(nc.sbuf_tensor(f"y1c{i}", [128, 512], F32)) for i in range(2)]

                gw_ld = [None] * NCH
                q1_last = [None] * NCH
                iq_tok = [None] * FFC
                iq_guard = [None] * FFC
                w2_tok = [None, None]
                w2_war = [None, None]
                o5_war = [None, None]
                y1_war5 = [None, None]
                y2_war = [None, None]
                i32_war = [None, None]
                yq_war = [None] * NCH
                mb_war = [None] * NCH
                sg_war = [None] * NCH
                cb_war = [None] * NCH
                r0_war = [None] * NCH
                for blk in range(MYB):
                    gc0 = blk * 512
                    def head5(kc0, blk):
                        gc0b = blk * 512
                        pair = (kc0, kc0 + 1)
                        hh = {}
                        for kc in pair:
                            gp = kc & 3
                            if blk == 0 and kc < 2:
                                gw_ld[gp] = p.dma("q", f"gwld{gp}", gw_c[gp][:],
                                                  gw_s.ap()[kc, :, gc0b:gc0b + 512],
                                                  deps=[gw_store[kc][blk]])
                        for kc in pair:
                            gp = kc & 3
                            hh[(kc, "q1")] = p.op(
                                "v", lambda e, gp=gp, blk=blk: e.tensor_tensor(
                                    yq5[gp][:], gw_c[gp][:],
                                    cf_bc[:, blk * 512:(blk + 1) * 512], ALU.mult),
                                deps=[gw_ld[gp], cf_tok[blk], yq_war[gp]])
                            q1_last[gp] = hh[(kc, "q1")]
                        for kc in pair:
                            nkc, nblk = kc + 2, blk
                            if nkc >= FFC:
                                nkc, nblk = kc + 2 - FFC, blk + 1
                            gpn = nkc & 3
                            if nblk < MYB:
                                gw_ld[gpn] = p.dma("q", f"gwld{gpn}", gw_c[gpn][:],
                                                   gw_s.ap()[nkc, :, nblk * 512:nblk * 512 + 512],
                                                   deps=[q1_last[gpn], gw_store[nkc][nblk]])
                        for kc in pair:
                            gp = kc & 3
                            ip = kc & 1
                            c1 = p.op("s", lambda e, gp=gp, ip=ip: e.copy(
                                i32_5[ip][:], yq5[gp][:]),
                                deps=[hh[(kc, "q1")], i32_war[ip]])
                            c2 = p.op("s", lambda e, gp=gp, ip=ip: e.activation(
                                r0n_5[gp][:], i32_5[ip][:], AF.Copy, scale=-1.0),
                                deps=[c1, r0_war[gp]])
                            i32_war[ip] = c2
                            hh[(kc, "c2")] = c2
                        for kc in pair:
                            gp = kc & 3
                            hh[(kc, "dd")] = p.op("g", lambda e, gp=gp: e.tensor_tensor(
                                mb_5[gp][:], yq5[gp][:], r0n_5[gp][:], ALU.add),
                                deps=[hh[(kc, "c2")], mb_war[gp]])
                        for kc in pair:
                            gp = kc & 3
                            sg = p.op("s", lambda e, gp=gp: e.activation(
                                sg_5[gp][:], mb_5[gp][:], AF.Sign),
                                deps=[hh[(kc, "dd")], sg_war[gp]])
                            hh[(kc, "sg")] = sg
                            hh[(kc, "ab")] = p.op("s", lambda e, gp=gp: e.activation(
                                mb_5[gp][:], mb_5[gp][:], AF.Abs), deps=[sg])
                        lvh = {kc: hh[(kc, "ab")] for kc in pair}
                        for li, w in enumerate((128.0, 64.0, 32.0, 16.0, 8.0, 4.0)):
                            for kc in pair:
                                gp = kc & 3
                                ddl = [lvh[kc]]
                                if li == 0:
                                    ddl.append(cb_war[gp])
                                lvh[kc] = p.op("v", lambda e, gp=gp, w=w:
                                               e.tensor_scalar(cb_5[gp][:], mb_5[gp][:],
                                                               w, w, ALU.is_gt,
                                                               ALU.mult), deps=ddl)
                            for kc in pair:
                                gp = kc & 3
                                lvh[kc] = p.op("v", lambda e, gp=gp:
                                               e.tensor_tensor(mb_5[gp][:], mb_5[gp][:],
                                                               cb_5[gp][:],
                                                               ALU.subtract),
                                               deps=[lvh[kc]])
                        for kc in pair:
                            hh[(kc, "lv")] = lvh[kc]
                        return hh

                    def tail5(kc0, blk, hh):
                        pair = (kc0, kc0 + 1)
                        cmh = {}
                        for kc in pair:
                            gp = kc & 3
                            cmh[kc] = p.op("v", lambda e, gp=gp: e.tensor_single_scalar(
                                cb_5[gp][:], mb_5[gp][:], 0.5, ALU.is_gt),
                                deps=[hh[(kc, "lv")], cb_war[gp]])
                        for thr in (1.5, 2.5, 3.5):
                            for kc in pair:
                                gp = kc & 3
                                cmh[kc] = p.op("v", lambda e, gp=gp, thr=thr:
                                               e.scalar_tensor_tensor(
                                                   cb_5[gp][:], mb_5[gp][:], thr,
                                                   cb_5[gp][:], ALU.is_gt, ALU.add),
                                               deps=[cmh[kc]])
                        for kc in pair:
                            gp = kc & 3
                            fr = p.op("g", lambda e, gp=gp: e.tensor_tensor(
                                mb_5[gp][:], mb_5[gp][:], cb_5[gp][:], ALU.subtract),
                                deps=[cmh[kc]])
                            cb_war[gp] = fr
                            cmh[kc] = fr
                        for kc in pair:
                            gp = kc & 3
                            sfr = p.op("g", lambda e, gp=gp: e.tensor_tensor(
                                mb_5[gp][:], mb_5[gp][:], sg_5[gp][:], ALU.mult),
                                deps=[cmh[kc]])
                            sg_war[gp] = sfr
                            cmh[kc] = sfr
                        for kc in pair:
                            gp = kc & 3
                            rfin = p.op("v", lambda e, gp=gp: e.tensor_tensor(
                                mb_5[gp][:], yq5[gp][:], mb_5[gp][:], ALU.subtract),
                                deps=[cmh[kc]])
                            yq_war[gp] = rfin
                            iq = p.op("v", lambda e, kc=kc, blk=blk, gp=gp:
                                      e.tensor_tensor(
                                          iqT[:, kc * 512:(kc + 1) * 512], mb_5[gp][:],
                                          sif_bc[:, blk * 512:(blk + 1) * 512],
                                          ALU.mult),
                                      deps=[rfin, cf_tok[blk], iq_guard[kc]])
                            iq_tok[kc] = iq
                            mb_war[gp] = iq
                            r0_war[gp] = hh[(kc, "dd")]

                    prev = None
                    for kc0 in range(0, FFC, 2):
                        hh = head5(kc0, blk)
                        if prev is not None:
                            pk, ph = prev
                            tail5(pk, blk, ph)
                        prev = (kc0, hh)
                    pk, ph = prev
                    tail5(pk, blk, ph)
                    if blk == 0:
                        for dc in (0, 1):
                            w2_tok[dc] = p.dma("g", f"w2ld{dc}",
                                               w2_sb[dc][:].rearrange("p (c f) -> p c f", c=FFC),
                                               w2_h.ap()[dc], deps=[])
                    for dc in range(DCN):
                        wp = dc & 1
                        ypar = dc & 1
                        yld = p.dma("q", f"y1ld{ypar}", y1c[ypar][:],
                                    y1_s.ap()[dc, :, gc0:gc0 + 512],
                                    deps=[y1_war5[ypar]])
                        stop = None
                        for kc in range(FFC):
                            dd2 = [iq_tok[kc], w2_tok[wp]]
                            if kc == 0:
                                dd2.append(o5_war[wp])
                            stop = p.op("t", lambda e, kc=kc, wp=wp: e.matmul(
                                o5_ps[wp][:],
                                w2_sb[wp][:, kc * 128:(kc + 1) * 128],
                                iqT[:, kc * 512:(kc + 1) * 512],
                                start=(kc == 0), stop=(kc == FFC - 1),
                                skip_group_check=True), deps=dd2)
                            if dc == DCN - 1:
                                iq_guard[kc] = stop
                        ndc = dc + 2
                        if not (blk == MYB - 1 and ndc >= DCN):
                            nb = blk if ndc < DCN else blk + 1
                            w2_tok[wp] = p.dma("g", f"w2ld{wp}",
                                               w2_sb[wp][:].rearrange("p (c f) -> p c f", c=FFC),
                                               w2_h.ap()[ndc % DCN], deps=[stop])
                        y2 = p.op("v", lambda e, wp=wp, ypar=ypar: e.tensor_tensor(
                            y1c[ypar][:], o5_ps[wp][:], y1c[ypar][:], ALU.add),
                            deps=[stop, yld, y2_war[ypar]])
                        o5_war[wp] = y2
                        y2_war[ypar] = p.dma("q", f"yst{ypar}",
                                             y_h.ap()[dc * 128:(dc + 1) * 128,
                                                      gc0:gc0 + 512],
                                             y1c[ypar][:], deps=[y2])
                        y1_war5[ypar] = y2_war[ypar]
                p.barrier()
                p.flush()
            SB2.close()

        p.barrier()
        p.op("q", lambda e: e.nop(), deps=[], inc=None)
        p.flush()

    return nc


def _own_idx(T, j):
    MYT = T // 2
    return np.concatenate([np.arange((2 * Z + j) * 512, (2 * Z + j + 1) * 512)
                           for Z in range(MYT // 512)])


def _prep_core(inputs, b, j, T):
    x = np.asarray(inputs["L0_attn_hidden_states"][b])
    MYT = T // 2
    idx = _own_idx(T, j)
    x_own = np.ascontiguousarray(x[idx])
    xT = np.ascontiguousarray(x.T).reshape(DCN, 128, T)
    x_ownT = np.ascontiguousarray(x_own.T).reshape(DCN, 128, MYT)

    jj = np.arange(512)[None, :]
    ii = np.arange(128)[:, None]
    trils = [(jj >= ii + off) for off in (0, 128, 256, 384)]
    if j == 0:
        tiles = trils + [np.zeros((128, 512), bool)] * 4
    else:
        tiles = [np.ones((128, 512), bool)] * 4 + trils
    dmask = np.stack([t.astype(ml_dtypes.bfloat16) for t in tiles])

    qb = np.asarray(inputs["L0_attn_q_b_w"])
    cols = [c for h in range(NH) for c in range(h * 192, h * 192 + 128)] + \
           [c for h in range(NH) for c in range(h * 192 + 128, h * 192 + 192)]
    qbw = np.ascontiguousarray(qb[:, cols])
    kva = np.ascontiguousarray(np.asarray(inputs["L0_attn_kv_a_w"]))
    kvb = np.asarray(inputs["L0_attn_kv_b_w"])
    colsb = [c for h in range(NH) for c in range(h * 256, h * 256 + 128)] + \
            [c for h in range(NH) for c in range(h * 256 + 128, h * 256 + 256)]
    kvbw = np.ascontiguousarray(kvb[:, colsb])

    def tile_in(w, n_in_c):
        d_in, f = w.shape
        return np.ascontiguousarray(w.reshape(n_in_c, 128, f).transpose(1, 0, 2))

    qaw = tile_in(np.asarray(inputs["L0_attn_q_a_w"]), DCN)
    qbw_t = tile_in(qbw, QLC)
    kvaw_t = tile_in(kva, DCN)
    kvbw_t = tile_in(kvbw, KLC)

    ow = np.ascontiguousarray(
        np.asarray(inputs["L0_attn_o_w"]).reshape(DCN, 128, DCN, 128)
        .transpose(2, 1, 0, 3))
    w13 = np.asarray(inputs["L0_ffn_w13"])
    w1w = np.ascontiguousarray(
        w13[:FF].T.reshape(DCN, 128, FFC, 128).transpose(2, 1, 0, 3))
    w3w = np.ascontiguousarray(
        w13[FF:].T.reshape(DCN, 128, FFC, 128).transpose(2, 1, 0, 3))
    w2w = np.ascontiguousarray(
        np.asarray(inputs["L0_ffn_w2"]).T.reshape(FFC, 128, DCN, 128)
        .transpose(2, 1, 0, 3))

    gq = np.ascontiguousarray(np.asarray(inputs["L0_attn_q_a_ln_w"]).reshape(QLC, 128))
    gkv = np.ascontiguousarray(np.asarray(inputs["L0_attn_kv_a_ln_w"]).reshape(KLC, 128))
    gf = np.ascontiguousarray(np.asarray(inputs["L0_ffn_norm_w"]).reshape(FFC, 128))

    return {
        "xT": xT, "x_nat": np.ascontiguousarray(x), "x_own": x_own,
        "x_ownT": x_ownT, "dmask": dmask,
        "qaw": qaw, "qbw": qbw_t, "kvaw": kvaw_t, "kvbw": kvbw_t,
        "ow": ow, "w1w": w1w, "w3w": w3w, "w2w": w2w,
        "gq": gq, "gkv": gkv, "gf": gf,
    }


def kernel(**inputs):
    T = int(np.asarray(inputs["L0_attn_hidden_states"]).shape[1])
    B = int(np.asarray(inputs["L0_attn_hidden_states"]).shape[0])
    n_cores = 8
    nc = build_nc(T=T, n_cores=n_cores)
    in_maps = []
    for c in range(n_cores):
        b, j = c // 2, c % 2
        in_maps.append(_prep_core(inputs, b, j, T))
    res = run_bass_kernel_spmd(nc, in_maps, core_ids=list(range(n_cores)))
    out = np.empty((B, T, D), dtype=np.float32)
    for c in range(n_cores):
        b, j = c // 2, c % 2
        out[b, _own_idx(T, j), :] = res.results[c]["y"].T
    return out


# version 13
# speedup vs baseline: 1.0327x; 1.0041x over previous
import math
import numpy as np
import ml_dtypes
from contextlib import ExitStack

import concourse.bass as bass
import concourse.mybir as mybir
from concourse.bass_utils import run_bass_kernel_spmd

F32 = mybir.dt.float32
BF16 = mybir.dt.bfloat16
I32 = mybir.dt.int32
AF = mybir.ActivationFunctionType
ALU = mybir.AluOpType
AX = mybir.AxisListType

D = 2048
DCN = 16
QL = 1536
QLC = 12
KL = 512
KLC = 4
NH = 16
FF = 5632
FFC = 44
TB = 512
SCALE = float(np.float32(1.0) / np.float32(np.sqrt(np.float32(192.0))))
EPS_LN = 1e-5
EPS_RMS_ATTN = 1e-6
EPS_RMS_FFN = 1e-5
INV127 = 1.0 / 127.0

ENG_SEM = {"t": "pe", "v": "vs", "s": "ss", "g": "gs"}


class Plan:

    def __init__(self, nc):
        self.nc = nc
        self.pend = []
        self.sems = {}
        self.cnt = {}
        self.wm = {k: {} for k in "tvsgq"}
        for name in ("pe", "vs", "ss", "gs", "cc"):
            self._sem(name)

    def _sem(self, name):
        if name not in self.sems:
            self.sems[name] = self.nc.alloc_semaphore(name)
            self.cnt[name] = 0
        return self.sems[name]

    def _waits(self, eng, deps):
        need = {}
        for d in deps:
            if d is None:
                continue
            s, v = d
            if self.wm[eng].get(s, 0) >= v:
                continue
            if s not in ENG_SEM.values() and s != "cc":
                assert v == self.cnt[s], (
                    f"unsound DMA wait: {s} at {v}, latest {self.cnt[s]}")
            if need.get(s, 0) < v:
                need[s] = v
        items = sorted(need.items())
        attach = None
        if items:
            attach = items[-1]
            for s, v in items[:-1]:
                sem = self.sems[s]
                self.pend.append((eng, lambda e, sem=sem, v=v: e.wait_ge(sem, v)))
                self.wm[eng][s] = v
            self.wm[eng][attach[0]] = attach[1]
        return attach

    def op(self, eng, fn, deps=(), inc="auto", inc_val=1):
        attach = self._waits(eng, deps)
        sem_name = ENG_SEM.get(eng) if inc == "auto" else inc
        tok = None
        if sem_name:
            self._sem(sem_name)
            self.cnt[sem_name] += inc_val
            tok = (sem_name, self.cnt[sem_name])
        a_sem = self.sems[attach[0]] if attach else None
        a_val = attach[1] if attach else 0
        m_sem = self.sems[sem_name] if sem_name else None

        def run(e, fn=fn, a_sem=a_sem, a_val=a_val, m_sem=m_sem, iv=inc_val):
            inst = fn(e)
            if a_sem is not None:
                inst._wait_ge(a_sem, a_val)
            if m_sem is not None:
                inst.then_inc(m_sem, iv)
        self.pend.append((eng, run))
        return tok

    def latest(self, stream):
        if stream not in self.cnt or self.cnt[stream] == 0:
            return None
        return (stream, self.cnt[stream])

    def dma(self, eng, stream, out, in_, deps=()):
        self._sem(stream)
        return self.op(eng, lambda e: e.dma_start(out=out, in_=in_), deps,
                       inc=stream, inc_val=16)

    def barrier(self):
        toks = [(s, self.cnt[s]) for s in self.sems if self.cnt[s] > 0]
        for eng in "tvsgq":
            need = {}
            for s, v in toks:
                if s == ENG_SEM.get(eng):
                    continue
                if self.wm[eng].get(s, 0) >= v:
                    continue
                need[s] = v
            for s, v in sorted(need.items()):
                sem = self.sems[s]
                self.pend.append((eng, lambda e, sem=sem, v=v: e.wait_ge(sem, v)))
                self.wm[eng][s] = v

    def flush(self):
        if not self.pend:
            return
        pend, self.pend = self.pend, []
        with self.nc.Block() as block:
            for key, meth in (("t", block.tensor), ("v", block.vector),
                              ("s", block.scalar), ("g", block.gpsimd),
                              ("q", block.sync)):
                ops = [f for (k, f) in pend if k == key]
                if ops:
                    meth(lambda e, ops=ops: [f(e) for f in ops])


def build_nc(T=4096, n_cores=8):
    NTB = T // TB
    MYT = T // 2
    MYB = MYT // TB
    nc = bass.Bass("TRN2", target_bir_lowering=False)

    xT_h = nc.dram_tensor("xT", [DCN, 128, T], F32, kind="ExternalInput")
    xn_h = nc.dram_tensor("x_nat", [T, D], F32, kind="ExternalInput")
    xo_h = nc.dram_tensor("x_own", [MYT, D], F32, kind="ExternalInput")
    xoT_h = nc.dram_tensor("x_ownT", [DCN, 128, MYT], F32, kind="ExternalInput")
    dmask_h = nc.dram_tensor("dmask", [8, 128, 512], BF16, kind="ExternalInput")
    qaw_h = nc.dram_tensor("qaw", [128, DCN, QL], F32, kind="ExternalInput")
    qbw_h = nc.dram_tensor("qbw", [128, QLC, 3072], F32, kind="ExternalInput")
    kvaw_h = nc.dram_tensor("kvaw", [128, DCN, QL], F32, kind="ExternalInput")
    kvbw_h = nc.dram_tensor("kvbw", [128, KLC, 4096], F32, kind="ExternalInput")
    ow_h = nc.dram_tensor("ow", [DCN, 128, DCN, 128], F32, kind="ExternalInput")
    w1_h = nc.dram_tensor("w1w", [FFC, 128, DCN, 128], F32, kind="ExternalInput")
    w3_h = nc.dram_tensor("w3w", [FFC, 128, DCN, 128], F32, kind="ExternalInput")
    w2_h = nc.dram_tensor("w2w", [DCN, 128, FFC, 128], F32, kind="ExternalInput")
    gq_h = nc.dram_tensor("gq", [QLC, 128], F32, kind="ExternalInput")
    gkv_h = nc.dram_tensor("gkv", [KLC, 128], F32, kind="ExternalInput")
    gf_h = nc.dram_tensor("gf", [FFC, 128], F32, kind="ExternalInput")

    y_h = nc.dram_tensor("y", [D, MYT], F32, kind="ExternalOutput")

    qT_s = nc.dram_tensor("qT_s", [3072, MYT], BF16)
    kTn_s = nc.dram_tensor("kTn_s", [2048, T], BF16)
    kTr_s = nc.dram_tensor("kTr_s", [1024, T], BF16)
    v_s = nc.dram_tensor("v_s", [NH, 128, T // 128, 128], BF16)
    attnT_s = nc.dram_tensor("attnT_s", [2048, MYT], BF16)
    y1_s = nc.dram_tensor("y1_s", [DCN, 128, MYT], F32)
    gw_s = nc.dram_tensor("gw_s", [FFC, 128, MYT], BF16)

    eye_f_h = nc.inline_tensor(np.eye(128, dtype=np.float32), "eye_f")
    eye_b_h = nc.inline_tensor(np.eye(128).astype(ml_dtypes.bfloat16), "eye_b")
    onescol_h = nc.inline_tensor(np.ones((128, 1)).astype(ml_dtypes.bfloat16), "onescol")
    onesrow_h = nc.inline_tensor(np.ones((1, 128), dtype=np.float32), "onesrow")

    p = Plan(nc)

    with ExitStack() as PERS:
        ep = PERS.enter_context
        ident_f = ep(nc.sbuf_tensor("ident_f", [128, 128], F32))
        ident_b = ep(nc.sbuf_tensor("ident_b", [128, 128], BF16))
        ones_st = ep(nc.sbuf_tensor("ones_st", [128, 1], BF16))
        ones_bc = ep(nc.sbuf_tensor("ones_bc", [1, 128], F32))
        mask_sb = ep(nc.sbuf_tensor("mask_sb", [128, 8 * 512], BF16))
        gq_sb = ep(nc.sbuf_tensor("gq_sb", [128, QLC], F32))
        gkv_sb = ep(nc.sbuf_tensor("gkv_sb", [128, KLC], F32))
        gf_sb = ep(nc.sbuf_tensor("gf_sb", [128, FFC], F32))

        p.dma("q", "ini", ident_f[:], eye_f_h.ap())
        p.dma("q", "ini", ident_b[:], eye_b_h.ap())
        p.dma("q", "ini", ones_st[:], onescol_h.ap())
        p.dma("q", "ini", ones_bc[:], onesrow_h.ap())
        p.dma("q", "ini", mask_sb[:].rearrange("p (m c) -> p m c", m=8),
              dmask_h.ap().rearrange("m p c -> p m c"))

        with ExitStack() as P0:
            e0 = P0.enter_context
            tp_ps = e0(nc.psum_tensor("tp0_ps", [128, 128], F32))
            gtmp = e0(nc.sbuf_tensor("gtmp", [FFC, 128], F32))
            p.barrier()
            gtoks = []
            prev = None
            for gh, gsb, n in ((gq_h, gq_sb, QLC), (gkv_h, gkv_sb, KLC),
                               (gf_h, gf_sb, FFC)):
                tok = p.dma("q", f"gld{n}", gtmp[0:n, :], gh.ap(), deps=[prev])
                tt = p.op("t", lambda e, n=n: e.transpose(
                    tp_ps[:, 0:n], gtmp[0:n, :], ident_f[0:n, 0:n]),
                    deps=[tok, prev])
                cp = p.op("v", lambda e, gsb=gsb, n=n: e.tensor_copy(
                    gsb[:, 0:n], tp_ps[:, 0:n]), deps=[tt])
                gtoks.append(cp)
                prev = cp
            g_q, g_kv, g_f = gtoks
            p.barrier()
            p.flush()

        with ExitStack() as P1AO:
            eo = P1AO.enter_context
            qlatN = eo(nc.sbuf_tensor("qlatN", [128, QLC * MYT], BF16))
            qn_tok = [[None] * QLC for _ in range(MYB)]

            with ExitStack() as P1A:
                e1 = P1A.enter_context
                tp_ps = e1(nc.psum_tensor("tp1_ps", [128, 128], F32))
                bc_ps = e1(nc.psum_tensor("bc1_ps", [128, 512], F32))
                mm_ps = [e1(nc.psum_tensor(f"mm1_ps{i}", [128, 512], F32)) for i in range(2)]
                stat_ps = e1(nc.psum_tensor("st1_ps", [1, 512], F32))
                qaw_sb = e1(nc.sbuf_tensor("qaw_sb", [128, DCN * QL], BF16))
                xn_t = [e1(nc.sbuf_tensor(f"xn_t{i}", [128, D], F32)) for i in range(2)]
                x2_t = e1(nc.sbuf_tensor("x2_t", [128, D], F32))
                st_sc = e1(nc.sbuf_tensor("st_sc", [128, 8], F32))
                rows = e1(nc.sbuf_tensor("rows", [1, 1024], F32))
                mu_bc = e1(nc.sbuf_tensor("mu_bc", [128, 512], F32))
                rs_bc = e1(nc.sbuf_tensor("rs_bc", [128, 512], F32))
                xt_c = [e1(nc.sbuf_tensor(f"xt_c{i}", [128, 512], F32)) for i in range(3)]
                tmp_c = e1(nc.sbuf_tensor("tmp_c", [128, 512], F32))
                lnT = [e1(nc.sbuf_tensor("lnT0", [128, DCN * 512], BF16))] * 2
                qlat = e1(nc.sbuf_tensor("qlat", [128, QLC * 512], F32))
                sq_c = [e1(nc.sbuf_tensor(f"sq_c{i}", [128, 512], BF16)) for i in range(2)]
                qvar = e1(nc.sbuf_tensor("qvar", [1, 512], F32))
                qrs = e1(nc.sbuf_tensor("qrs", [1, 512], F32))
                rsq_bc = e1(nc.sbuf_tensor("rsq_bc", [128, 512], F32))

                p.dma("g", "w1a", qaw_sb[:].rearrange("p (c f) -> p c f", c=DCN),
                      qaw_h.ap())
                t_w1a = p.latest("w1a")

                state = dict(xn_war=[None, None], x2_war=None, xt_war=[None] * 3,
                             tp_war=g_f, bc_war=None, mu_war=None, rows_war=None)
                ln_war = [None, None]
                ln_tok = [[None] * DCN, [None] * DCN]
                mm_war = [None, None]
                stat_war = [None, None]

                def ln_block(tb, par, c0, src2d_h, srcT_h, tag, st):
                    for ti in range(4):
                        tpar = ti & 1
                        gt = 4 * tb + ti
                        ld = p.dma("q", f"xn{tpar}{tag}", xn_t[tpar][:],
                                   src2d_h.ap()[gt * 128:(gt + 1) * 128, :],
                                   deps=(st["xn_war"][tpar] or []))
                        s1 = p.op("v", lambda e, b=tpar, ti=ti: e.reduce_sum(
                            st_sc[:, ti:ti + 1], xn_t[b][:], AX.X), deps=[ld])
                        x2 = p.op("g", lambda e, b=tpar: e.tensor_tensor(
                            x2_t[:], xn_t[b][:], xn_t[b][:], ALU.mult),
                            deps=[ld, st["x2_war"]])
                        s2 = p.op("v", lambda e: e.reduce_sum(
                            st_sc[:, 4:5], x2_t[:], AX.X), deps=[x2])
                        st["x2_war"] = s2
                        st["xn_war"][tpar] = [s1, x2]
                        mu = p.op("v", lambda e, ti=ti: e.tensor_scalar_mul(
                            st_sc[:, ti:ti + 1], st_sc[:, ti:ti + 1], 1.0 / D),
                            deps=[s1])
                        m2 = p.op("v", lambda e, ti=ti: e.tensor_tensor(
                            st_sc[:, 6:7], st_sc[:, ti:ti + 1], st_sc[:, ti:ti + 1],
                            ALU.mult), deps=[mu])
                        var = p.op("v", lambda e: e.tensor_scalar(
                            st_sc[:, 4:5], st_sc[:, 4:5], 1.0 / D, EPS_LN,
                            ALU.mult, ALU.add), deps=[s2])
                        var2 = p.op("v", lambda e: e.tensor_tensor(
                            st_sc[:, 5:6], st_sc[:, 4:5], st_sc[:, 6:7], ALU.subtract),
                            deps=[var, m2])
                        lg = p.op("s", lambda e: e.activation(
                            st_sc[:, 7:8], st_sc[:, 5:6], AF.Ln), deps=[var2])
                        rstd = p.op("s", lambda e: e.activation(
                            st_sc[:, 5:6], st_sc[:, 7:8], AF.Exp, scale=-0.5),
                            deps=[lg])
                        tt1 = p.op("t", lambda e, ti=ti: e.transpose(
                            tp_ps[0:1, :], st_sc[:, ti:ti + 1], ident_f[:]),
                            deps=[mu, m2, st["tp_war"]])
                        cp1 = p.op("v", lambda e, ti=ti: e.tensor_copy(
                            rows[0:1, ti * 128:(ti + 1) * 128], tp_ps[0:1, :]),
                            deps=[tt1, st["rows_war"]])
                        tt2 = p.op("t", lambda e: e.transpose(
                            tp_ps[0:1, :], st_sc[:, 5:6], ident_f[:]),
                            deps=[rstd, cp1])
                        cp2 = p.op("v", lambda e, ti=ti: e.tensor_copy(
                            rows[0:1, 512 + ti * 128:512 + (ti + 1) * 128],
                            tp_ps[0:1, :]), deps=[tt2])
                        st["tp_war"] = cp2
                    b1 = p.op("t", lambda e: e.matmul(bc_ps[:], ones_bc[:],
                                                      rows[0:1, 0:512], start=True,
                                                      stop=True, skip_group_check=True),
                              deps=[st["tp_war"], st["bc_war"]])
                    mub = p.op("v", lambda e: e.tensor_copy(mu_bc[:], bc_ps[:]),
                               deps=[b1, st["mu_war"]])
                    b2 = p.op("t", lambda e: e.matmul(bc_ps[:], ones_bc[:],
                                                      rows[0:1, 512:1024], start=True,
                                                      stop=True, skip_group_check=True),
                              deps=[mub])
                    rsb = p.op("v", lambda e: e.tensor_copy(rs_bc[:], bc_ps[:]),
                               deps=[b2, st["mu_war"]])
                    st["bc_war"] = rsb
                    st["rows_war"] = b2
                    t2 = st.get("tmp_war")
                    for dc in range(DCN):
                        cr = dc % 3
                        xt = p.dma("q", f"xt{cr}{tag}", xt_c[cr][:],
                                   srcT_h.ap()[dc, :, c0:c0 + TB],
                                   deps=[st["xt_war"][cr]])
                        t1 = p.op("v", lambda e, cr=cr: e.tensor_tensor(
                            tmp_c[:], xt_c[cr][:], mu_bc[:], ALU.subtract),
                            deps=[xt, mub, t2])
                        st["xt_war"][cr] = t1
                        t2 = p.op("v", lambda e, dc=dc, par=par: e.tensor_tensor(
                            lnT[par][:, dc * 512:(dc + 1) * 512], tmp_c[:], rs_bc[:],
                            ALU.mult), deps=[t1, rsb, ln_war[par]])
                        ln_tok[par][dc] = t2
                    st["mu_war"] = t2
                    st["tmp_war"] = t2

                for tb in range(MYB):
                    par = tb & 1
                    c0 = tb * TB
                    ln_block(tb, par, c0, xo_h, xoT_h, "a", state)
                    qlat_tok = [None] * QLC
                    sq_war = stat_war
                    last_stop = None
                    stop = None
                    for fc in range(QLC):
                        bank = fc & 1
                        for dc in range(DCN):
                            dd = [ln_tok[par][dc], t_w1a]
                            if dc == 0:
                                dd.append(mm_war[bank])
                            stop = p.op("t", lambda e, fc=fc, dc=dc, bank=bank, par=par:
                                        e.matmul(mm_ps[bank][:],
                                                 qaw_sb[:, dc * QL + fc * 128:dc * QL + (fc + 1) * 128],
                                                 lnT[par][:, dc * 512:(dc + 1) * 512],
                                                 start=(dc == 0), stop=(dc == DCN - 1),
                                                 skip_group_check=True), deps=dd)
                        ev = p.op("v", lambda e, fc=fc, bank=bank: e.tensor_copy(
                            qlat[:, fc * 512:(fc + 1) * 512], mm_ps[bank][:]),
                            deps=[stop])
                        qlat_tok[fc] = ev
                        mm_war[bank] = ev
                        sp = fc & 1
                        sqt = p.op("v", lambda e, fc=fc, sp=sp: e.tensor_tensor(
                            sq_c[sp][:], qlat[:, fc * 512:(fc + 1) * 512],
                            qlat[:, fc * 512:(fc + 1) * 512], ALU.mult),
                            deps=[ev, sq_war[sp]])
                        last_stop = p.op("t", lambda e, fc=fc, sp=sp: e.matmul(
                            stat_ps[:], ones_st[:], sq_c[sp][:],
                            start=(fc == 0), stop=(fc == QLC - 1),
                            skip_group_check=True), deps=[sqt])
                        sq_war[sp] = last_stop
                    stat_war = sq_war
                    qv = p.op("v", lambda e: e.tensor_scalar(
                        qvar[:], stat_ps[:], 1.0 / QL, EPS_RMS_ATTN, ALU.mult,
                        ALU.add), deps=[last_stop])
                    ql1 = p.op("s", lambda e: e.activation(qrs[:], qvar[:], AF.Ln),
                               deps=[qv])
                    ql2 = p.op("s", lambda e: e.activation(qrs[:], qrs[:], AF.Exp,
                                                           scale=-0.5), deps=[ql1])
                    b3 = p.op("t", lambda e: e.matmul(bc_ps[:], ones_bc[:], qrs[:],
                                                      start=True, stop=True,
                                                      skip_group_check=True),
                              deps=[ql2, state["bc_war"]])
                    rqb = p.op("v", lambda e: e.tensor_copy(rsq_bc[:], bc_ps[:]),
                               deps=[b3])
                    state["bc_war"] = rqb
                    for fc in range(QLC):
                        qn_tok[tb][fc] = p.op("v", lambda e, fc=fc, tb=tb, c0=c0:
                                              e.scalar_tensor_tensor(
                                                  qlatN[:, fc * MYT + c0:fc * MYT + c0 + 512],
                                                  qlat[:, fc * 512:(fc + 1) * 512],
                                                  gq_sb[:, fc:fc + 1], rsq_bc[:],
                                                  ALU.mult, ALU.mult),
                                              deps=[qlat_tok[fc], rqb, g_q])
                    ln_war[par] = stop
                p.barrier()
                p.flush()

            with ExitStack() as P1A2:
                e1 = P1A2.enter_context
                mm_ps = [e1(nc.psum_tensor(f"mm1b_ps{i}", [128, 512], F32))
                         for i in range(2)]
                qbw_sb = e1(nc.sbuf_tensor("qbw_sb", [128, QLC * 3072], BF16))
                qstage = [e1(nc.sbuf_tensor(f"qstage{i}", [128, 512], BF16))
                          for i in range(2)]
                p.dma("g", "w1a2", qbw_sb[:].rearrange("p (c f) -> p c f", c=QLC),
                      qbw_h.ap())
                t_w1a2 = p.latest("w1a2")
                qs_war = [None, None]
                mm_war = [None, None]
                for tb in range(MYB):
                    c0 = tb * TB
                    for fc2 in range(24):
                        bank = fc2 & 1
                        stop = None
                        for kc in range(QLC):
                            dd = [qn_tok[tb][kc], t_w1a2]
                            if kc == 0:
                                dd.append(mm_war[bank])
                            stop = p.op("t", lambda e, fc2=fc2, kc=kc, bank=bank, c0=c0:
                                        e.matmul(mm_ps[bank][:],
                                                 qbw_sb[:, kc * 3072 + fc2 * 128:kc * 3072 + (fc2 + 1) * 128],
                                                 qlatN[:, kc * MYT + c0:kc * MYT + c0 + 512],
                                                 start=(kc == 0), stop=(kc == QLC - 1),
                                                 skip_group_check=True), deps=dd)
                        ev = p.op("v", lambda e, bank=bank: e.tensor_scalar_mul(
                            qstage[bank][:], mm_ps[bank][:], SCALE),
                            deps=[stop, qs_war[bank]])
                        mm_war[bank] = ev
                        qs_war[bank] = p.dma("q", f"qst{bank}",
                                             qT_s.ap()[fc2 * 128:(fc2 + 1) * 128,
                                                       c0:c0 + TB],
                                             qstage[bank][:], deps=[ev])
                p.barrier()
                p.flush()

        with ExitStack() as P1B:
            e1 = P1B.enter_context
            tp_ps = e1(nc.psum_tensor("tp2_ps", [128, 128], F32))
            bc_ps = e1(nc.psum_tensor("bc2_ps", [128, 512], F32))
            mm_ps = [e1(nc.psum_tensor(f"mm2_ps{i}", [128, 512], F32)) for i in range(2)]
            v_ps = [e1(nc.psum_tensor(f"v_ps{i}", [128, 512], F32)) for i in range(2)]
            stat_ps = e1(nc.psum_tensor("st2_ps", [1, 512], F32))
            kvaw_sb = e1(nc.sbuf_tensor("kvaw_sb", [128, DCN * QL], BF16))
            kvbw_sb = e1(nc.sbuf_tensor("kvbw_sb", [128, KLC * 4096], BF16))
            xn_t = [e1(nc.sbuf_tensor(f"xnb_t{i}", [128, D], F32)) for i in range(2)]
            x2_t = e1(nc.sbuf_tensor("x2b_t", [128, D], F32))
            st_sc = e1(nc.sbuf_tensor("stb_sc", [128, 8], F32))
            rows = e1(nc.sbuf_tensor("rowsb", [1, 1024], F32))
            mu_bc = e1(nc.sbuf_tensor("mub_bc", [128, 512], F32))
            rs_bc = e1(nc.sbuf_tensor("rsb_bc", [128, 512], F32))
            xt_c = [e1(nc.sbuf_tensor(f"xtb_c{i}", [128, 512], F32)) for i in range(3)]
            tmp_c = e1(nc.sbuf_tensor("tmpb_c", [128, 512], F32))
            lnT = [e1(nc.sbuf_tensor(f"lnTb{i}", [128, DCN * 512], BF16)) for i in range(2)]
            klat = e1(nc.sbuf_tensor("klat", [128, KLC * 512], F32))
            sq_c = [e1(nc.sbuf_tensor(f"sqb_c{i}", [128, 512], BF16)) for i in range(2)]
            kvar = e1(nc.sbuf_tensor("kvar", [1, 512], F32))
            krs = e1(nc.sbuf_tensor("krs", [1, 512], F32))
            rsk_bc = e1(nc.sbuf_tensor("rsk_bc", [128, 512], F32))
            klatN = e1(nc.sbuf_tensor("klatN", [128, KLC * 512], BF16))
            kstage = [e1(nc.sbuf_tensor(f"kstage{i}", [128, 512], BF16)) for i in range(2)]
            vstage = [e1(nc.sbuf_tensor(f"vstage{i}", [128, 2048], BF16)) for i in range(2)]

            p.dma("g", "w1b", kvaw_sb[:].rearrange("p (c f) -> p c f", c=DCN),
                  kvaw_h.ap())
            t_w1b = p.dma("g", "w1b", kvbw_sb[:].rearrange("p (c f) -> p c f", c=KLC),
                          kvbw_h.ap())

            state = dict(xn_war=[None, None], x2_war=None, xt_war=[None] * 3,
                         tp_war=None, bc_war=None, mu_war=None, rows_war=None)
            ln_war = [None, None]
            ln_tok = [[None] * DCN, [None] * DCN]
            mm_war = [None, None]
            kst_war = [None, None]
            vst_war = [None, None]
            vps_war = [None, None]
            stat_war = [None, None]

            def ln_block_b(tb, par, c0, st):
                for ti in range(4):
                    tpar = ti & 1
                    gt = 4 * tb + ti
                    ld = p.dma("q", f"xnb{tpar}", xn_t[tpar][:],
                               xn_h.ap()[gt * 128:(gt + 1) * 128, :],
                               deps=(st["xn_war"][tpar] or []))
                    s1 = p.op("v", lambda e, b=tpar, ti=ti: e.reduce_sum(
                        st_sc[:, ti:ti + 1], xn_t[b][:], AX.X), deps=[ld])
                    x2 = p.op("g", lambda e, b=tpar: e.tensor_tensor(
                        x2_t[:], xn_t[b][:], xn_t[b][:], ALU.mult),
                        deps=[ld, st["x2_war"]])
                    s2 = p.op("v", lambda e: e.reduce_sum(
                        st_sc[:, 4:5], x2_t[:], AX.X), deps=[x2])
                    st["x2_war"] = s2
                    st["xn_war"][tpar] = [s1, x2]
                    mu = p.op("v", lambda e, ti=ti: e.tensor_scalar_mul(
                        st_sc[:, ti:ti + 1], st_sc[:, ti:ti + 1], 1.0 / D), deps=[s1])
                    m2 = p.op("v", lambda e, ti=ti: e.tensor_tensor(
                        st_sc[:, 6:7], st_sc[:, ti:ti + 1], st_sc[:, ti:ti + 1],
                        ALU.mult), deps=[mu])
                    var = p.op("v", lambda e: e.tensor_scalar(
                        st_sc[:, 4:5], st_sc[:, 4:5], 1.0 / D, EPS_LN,
                        ALU.mult, ALU.add), deps=[s2])
                    var2 = p.op("v", lambda e: e.tensor_tensor(
                        st_sc[:, 5:6], st_sc[:, 4:5], st_sc[:, 6:7], ALU.subtract),
                        deps=[var, m2])
                    lg = p.op("s", lambda e: e.activation(
                        st_sc[:, 7:8], st_sc[:, 5:6], AF.Ln), deps=[var2])
                    rstd = p.op("s", lambda e: e.activation(
                        st_sc[:, 5:6], st_sc[:, 7:8], AF.Exp, scale=-0.5), deps=[lg])
                    tt1 = p.op("t", lambda e, ti=ti: e.transpose(
                        tp_ps[0:1, :], st_sc[:, ti:ti + 1], ident_f[:]),
                        deps=[mu, m2, st["tp_war"]])
                    cp1 = p.op("v", lambda e, ti=ti: e.tensor_copy(
                        rows[0:1, ti * 128:(ti + 1) * 128], tp_ps[0:1, :]),
                        deps=[tt1, st["rows_war"]])
                    tt2 = p.op("t", lambda e: e.transpose(
                        tp_ps[0:1, :], st_sc[:, 5:6], ident_f[:]), deps=[rstd, cp1])
                    cp2 = p.op("v", lambda e, ti=ti: e.tensor_copy(
                        rows[0:1, 512 + ti * 128:512 + (ti + 1) * 128],
                        tp_ps[0:1, :]), deps=[tt2])
                    st["tp_war"] = cp2
                b1 = p.op("t", lambda e: e.matmul(bc_ps[:], ones_bc[:],
                                                  rows[0:1, 0:512], start=True,
                                                  stop=True, skip_group_check=True),
                          deps=[st["tp_war"], st["bc_war"]])
                mub = p.op("v", lambda e: e.tensor_copy(mu_bc[:], bc_ps[:]),
                           deps=[b1, st["mu_war"]])
                b2 = p.op("t", lambda e: e.matmul(bc_ps[:], ones_bc[:],
                                                  rows[0:1, 512:1024], start=True,
                                                  stop=True, skip_group_check=True),
                          deps=[mub])
                rsb = p.op("v", lambda e: e.tensor_copy(rs_bc[:], bc_ps[:]),
                           deps=[b2, st["mu_war"]])
                st["bc_war"] = rsb
                st["rows_war"] = b2
                t2 = st.get("tmp_war")
                for dc in range(DCN):
                    cr = dc % 3
                    xt = p.dma("q", f"xtb{cr}", xt_c[cr][:],
                               xT_h.ap()[dc, :, c0:c0 + TB], deps=[st["xt_war"][cr]])
                    t1 = p.op("v", lambda e, cr=cr: e.tensor_tensor(
                        tmp_c[:], xt_c[cr][:], mu_bc[:], ALU.subtract),
                        deps=[xt, mub, t2])
                    st["xt_war"][cr] = t1
                    t2 = p.op("v", lambda e, dc=dc, par=par: e.tensor_tensor(
                        lnT[par][:, dc * 512:(dc + 1) * 512], tmp_c[:], rs_bc[:],
                        ALU.mult), deps=[t1, rsb, ln_war[par]])
                    ln_tok[par][dc] = t2
                st["mu_war"] = t2
                st["tmp_war"] = t2

            for tb in range(NTB):
                par = tb & 1
                c0 = tb * TB
                ln_block_b(tb, par, c0, state)
                klat_tok = [None] * KLC
                sq_war = stat_war
                last_stop = None
                last_mm = None
                for fc in range(QLC):
                    bank = fc & 1
                    stop = None
                    for dc in range(DCN):
                        dd = [ln_tok[par][dc], t_w1b]
                        if dc == 0:
                            dd.append(mm_war[bank])
                        stop = p.op("t", lambda e, fc=fc, dc=dc, bank=bank, par=par:
                                    e.matmul(mm_ps[bank][:],
                                             kvaw_sb[:, dc * QL + fc * 128:dc * QL + (fc + 1) * 128],
                                             lnT[par][:, dc * 512:(dc + 1) * 512],
                                             start=(dc == 0), stop=(dc == DCN - 1),
                                             skip_group_check=True), deps=dd)
                    last_mm = stop
                    if fc < KLC:
                        ev = p.op("v", lambda e, fc=fc, bank=bank: e.tensor_copy(
                            klat[:, fc * 512:(fc + 1) * 512], mm_ps[bank][:]),
                            deps=[stop])
                        klat_tok[fc] = ev
                        mm_war[bank] = ev
                        sp = fc & 1
                        sqt = p.op("v", lambda e, fc=fc, sp=sp: e.tensor_tensor(
                            sq_c[sp][:], klat[:, fc * 512:(fc + 1) * 512],
                            klat[:, fc * 512:(fc + 1) * 512], ALU.mult),
                            deps=[ev, sq_war[sp]])
                        last_stop = p.op("t", lambda e, fc=fc, sp=sp: e.matmul(
                            stat_ps[:], ones_st[:], sq_c[sp][:],
                            start=(fc == 0), stop=(fc == KLC - 1),
                            skip_group_check=True), deps=[sqt])
                        sq_war[sp] = last_stop
                    else:
                        sp = fc & 1
                        ev = p.op("v", lambda e, bank=bank, sp=sp: e.tensor_copy(
                            kstage[sp][:], mm_ps[bank][:]), deps=[stop, kst_war[sp]])
                        mm_war[bank] = ev
                        kst_war[sp] = p.dma("q", f"kst{sp}",
                                            kTr_s.ap()[(fc - KLC) * 128:(fc - 3) * 128,
                                                       c0:c0 + TB],
                                            kstage[sp][:], deps=[ev])
                stat_war = sq_war
                kv1 = p.op("v", lambda e: e.tensor_scalar(
                    kvar[:], stat_ps[:], 1.0 / KL, EPS_RMS_ATTN, ALU.mult, ALU.add),
                    deps=[last_stop])
                kl1 = p.op("s", lambda e: e.activation(krs[:], kvar[:], AF.Ln), deps=[kv1])
                kl2 = p.op("s", lambda e: e.activation(krs[:], krs[:], AF.Exp,
                                                       scale=-0.5), deps=[kl1])
                b4 = p.op("t", lambda e: e.matmul(bc_ps[:], ones_bc[:], krs[:],
                                                  start=True, stop=True,
                                                  skip_group_check=True),
                          deps=[kl2, state["bc_war"]])
                rkb = p.op("v", lambda e: e.tensor_copy(rsk_bc[:], bc_ps[:]), deps=[b4])
                state["bc_war"] = rkb
                kn_tok = [None] * KLC
                for fc in range(KLC):
                    kn_tok[fc] = p.op("v", lambda e, fc=fc: e.scalar_tensor_tensor(
                        klatN[:, fc * 512:(fc + 1) * 512],
                        klat[:, fc * 512:(fc + 1) * 512],
                        gkv_sb[:, fc:fc + 1], rsk_bc[:], ALU.mult, ALU.mult),
                        deps=[klat_tok[fc], rkb, g_kv])
                for fc2 in range(NH):
                    bank = fc2 & 1
                    stop = None
                    for kc in range(KLC):
                        dd = [kn_tok[kc], t_w1b]
                        if kc == 0:
                            dd.append(mm_war[bank])
                        stop = p.op("t", lambda e, fc2=fc2, kc=kc, bank=bank:
                                    e.matmul(mm_ps[bank][:],
                                             kvbw_sb[:, kc * 4096 + fc2 * 128:kc * 4096 + (fc2 + 1) * 128],
                                             klatN[:, kc * 512:(kc + 1) * 512],
                                             start=(kc == 0), stop=(kc == KLC - 1),
                                             skip_group_check=True), deps=dd)
                    sp = fc2 & 1
                    ev = p.op("v", lambda e, sp=sp, bank=bank: e.tensor_copy(
                        kstage[sp][:], mm_ps[bank][:]), deps=[stop, kst_war[sp]])
                    mm_war[bank] = ev
                    kst_war[sp] = p.dma("q", f"kst{sp}",
                                        kTn_s.ap()[fc2 * 128:(fc2 + 1) * 128, c0:c0 + TB],
                                        kstage[sp][:], deps=[ev])
                for ts in range(4):
                    vp = ts & 1
                    gchunk = 4 * tb + ts
                    evs = []
                    for half in range(4):
                        stop = None
                        for kc in range(KLC):
                            dd = [kn_tok[kc]]
                            if kc == 0:
                                dd.append(vps_war[vp])
                            stop = p.op("t", lambda e, ts=ts, kc=kc, half=half, vp=vp:
                                        e.matmul(v_ps[vp][:],
                                                 klatN[:, kc * 512 + ts * 128:kc * 512 + (ts + 1) * 128],
                                                 kvbw_sb[:, kc * 4096 + 2048 + half * 512:
                                                         kc * 4096 + 2048 + (half + 1) * 512],
                                                 start=(kc == 0), stop=(kc == KLC - 1),
                                                 skip_group_check=True), deps=dd)
                        ev = p.op("v", lambda e, vp=vp, half=half: e.tensor_copy(
                            vstage[vp][:, half * 512:(half + 1) * 512], v_ps[vp][:]),
                            deps=([stop, vst_war[vp]] if half == 0 else [stop]))
                        vps_war[vp] = ev
                        evs.append(ev)
                    vst_war[vp] = p.dma("q", f"vst{vp}",
                                        v_s.ap()[:, :, gchunk, :].rearrange("h p d -> p h d"),
                                        vstage[vp][:].rearrange("p (h d) -> p h d", h=NH),
                                        deps=evs)
                ln_war[par] = last_mm
            p.barrier()
            p.flush()

        with ExitStack() as P2:
            e2 = P2.enter_context
            sc_ps = [e2(nc.psum_tensor(f"sc_ps{i}", [128, 512], F32)) for i in range(2)]
            av_ps = e2(nc.psum_tensor("av_ps", [128, 512], F32))
            r_ps = e2(nc.psum_tensor("r_ps", [1, 512], F32))
            s2_ps = e2(nc.psum_tensor("s2_ps", [1, 512], F32))
            bc_ps = e2(nc.psum_tensor("bc3_ps", [128, 512], F32))
            kTn = [e2(nc.sbuf_tensor(f"kTn{i}", [128, T], BF16)) for i in range(2)]
            kTr = [e2(nc.sbuf_tensor(f"kTr{i}", [64, T], BF16)) for i in range(2)]
            vh = [e2(nc.sbuf_tensor(f"vh{i}", [128, T], BF16)) for i in range(2)]
            qTn = [e2(nc.sbuf_tensor(f"qTn{i}", [128, MYT], BF16)) for i in range(2)]
            qTr = [e2(nc.sbuf_tensor(f"qTr{i}", [64, MYT], BF16)) for i in range(2)]
            sqn = e2(nc.sbuf_tensor("sqn", [128, T], BF16))
            sqr = e2(nc.sbuf_tensor("sqr", [64, T], BF16))
            k2row = e2(nc.sbuf_tensor("k2row", [1, T], F32))
            q2row = e2(nc.sbuf_tensor("q2row", [1, MYT], F32))
            k2m = e2(nc.sbuf_tensor("k2m", [1, 1], F32))
            q2m = e2(nc.sbuf_tensor("q2m", [1, MYB], F32))
            ncg = e2(nc.sbuf_tensor("ncg", [1, MYB], F32))
            negc_bc = e2(nc.sbuf_tensor("negc_bc", [128, 1], F32))
            probs = [e2(nc.sbuf_tensor(f"probs{i}", [128, 512], BF16)) for i in range(4)]
            rrow = e2(nc.sbuf_tensor("rrow", [1, 512], F32))
            rinv = e2(nc.sbuf_tensor("rinv", [1, 512], F32))
            rinv_bc = e2(nc.sbuf_tensor("rinv_bc", [128, 512], F32))
            astage = [e2(nc.sbuf_tensor(f"astage{i}", [128, 512], BF16)) for i in range(2)]

            ld = [[None] * 5, [None] * 5]
            buf_war = [[None] * 5, [None] * 5]

            def load_head(h):
                par = h & 1
                ld[par][0] = p.dma("q", f"hld{par}", kTn[par][:],
                                   kTn_s.ap()[h * 128:(h + 1) * 128, :],
                                   deps=[buf_war[par][0]])
                ld[par][1] = p.dma("s", f"hlds{par}", kTr[par][:],
                                   kTr_s.ap()[h * 64:(h + 1) * 64, :],
                                   deps=[buf_war[par][1]])
                ld[par][2] = p.dma("q", f"hld{par}", vh[par][:],
                                   v_s.ap()[h].rearrange("p c d -> p (c d)"),
                                   deps=[buf_war[par][2]])
                ld[par][3] = p.dma("s", f"hlds{par}", qTn[par][:],
                                   qT_s.ap()[h * 128:(h + 1) * 128, :],
                                   deps=[buf_war[par][3]])
                ld[par][4] = p.dma("s", f"hlds{par}", qTr[par][:],
                                   qT_s.ap()[2048 + h * 64:2048 + (h + 1) * 64, :],
                                   deps=[buf_war[par][4]])
                ld[par][0] = ld[par][2]
                ld[par][1] = ld[par][4]
                ld[par][3] = ld[par][4]

            load_head(0)
            sq_war = None
            ps2_war = None
            ncg_war = None
            negc_war = None
            probs_war = [None] * 4
            sc_war = [None, None]
            av_war = None
            r_war = None
            bc_war = None
            ast_war = [None, None]
            for h in range(NH):
                par = h & 1
                if h + 1 < NH:
                    load_head(h + 1)
                sn = p.op("v", lambda e, par=par: e.tensor_tensor(
                    sqn[:], kTn[par][:], kTn[par][:], ALU.mult),
                    deps=[ld[par][0], sq_war])
                sr = p.op("v", lambda e, par=par: e.tensor_tensor(
                    sqr[:], kTr[par][:], kTr[par][:], ALU.mult),
                    deps=[ld[par][1], sq_war])
                kc_toks = []
                for c8 in range(T // 512):
                    m1 = p.op("t", lambda e, c8=c8: e.matmul(
                        s2_ps[:], ones_st[:], sqn[:, c8 * 512:(c8 + 1) * 512],
                        start=True, stop=False, skip_group_check=True),
                        deps=[sn, ps2_war])
                    m2 = p.op("t", lambda e, c8=c8: e.matmul(
                        s2_ps[:], ones_st[0:64, :], sqr[:, c8 * 512:(c8 + 1) * 512],
                        start=False, stop=True, skip_group_check=True), deps=[m1, sr])
                    cp = p.op("v", lambda e, c8=c8: e.tensor_copy(
                        k2row[:, c8 * 512:(c8 + 1) * 512], s2_ps[:]), deps=[m2])
                    ps2_war = cp
                    kc_toks.append(cp)
                k2t = p.op("v", lambda e: e.reduce_max(k2m[:], k2row[:], AX.X),
                           deps=kc_toks)
                sn2 = p.op("v", lambda e, par=par: e.tensor_tensor(
                    sqn[:, 0:MYT], qTn[par][:], qTn[par][:], ALU.mult),
                    deps=[ld[par][3]] + kc_toks)
                sr2 = p.op("v", lambda e, par=par: e.tensor_tensor(
                    sqr[:, 0:MYT], qTr[par][:], qTr[par][:], ALU.mult),
                    deps=[ld[par][4]] + kc_toks)
                qc_toks = []
                for c8 in range(MYT // 512):
                    m1 = p.op("t", lambda e, c8=c8: e.matmul(
                        s2_ps[:], ones_st[:], sqn[:, c8 * 512:(c8 + 1) * 512],
                        start=True, stop=False, skip_group_check=True),
                        deps=[sn2, ps2_war])
                    m2 = p.op("t", lambda e, c8=c8: e.matmul(
                        s2_ps[:], ones_st[0:64, :], sqr[:, c8 * 512:(c8 + 1) * 512],
                        start=False, stop=True, skip_group_check=True), deps=[m1, sr2])
                    cp = p.op("v", lambda e, c8=c8: e.tensor_copy(
                        q2row[:, c8 * 512:(c8 + 1) * 512], s2_ps[:]), deps=[m2])
                    ps2_war = cp
                    qc_toks.append(cp)
                sq_war = sr2
                qm = p.op("v", lambda e: e.reduce_max(
                    q2m[:], q2row[:].rearrange("o (q f) -> o q f", q=MYB), AX.X),
                    deps=qc_toks)
                pr = p.op("v", lambda e: e.tensor_single_scalar(
                    ncg[:], q2m[:], k2m[:], ALU.mult), deps=[qm, k2t, ncg_war])
                cl = p.op("s", lambda e: e.activation(ncg[:], ncg[:], AF.Ln), deps=[pr])
                ce = p.op("s", lambda e: e.activation(ncg[:], ncg[:], AF.Exp, scale=0.5),
                          deps=[cl])
                ngc = p.op("v", lambda e: e.tensor_scalar(
                    ncg[:], ncg[:], -1.0, -1.0, ALU.mult, ALU.add), deps=[ce])
                for Qi in range(MYB):
                    ncch = 8 * (Qi + 1)
                    qc0 = Qi * 512
                    bq = p.op("t", lambda e, Qi=Qi: e.matmul(
                        bc_ps[:, 0:1], ones_bc[:], ncg[0:1, Qi:Qi + 1],
                        start=True, stop=True, skip_group_check=True),
                        deps=[ngc, bc_war])
                    ncg_war = bq
                    nbc = p.op("v", lambda e: e.tensor_copy(negc_bc[:], bc_ps[:, 0:1]),
                               deps=[bq, negc_war])
                    bc_war = nbc
                    av_stop = None
                    r_stop = None
                    ex = None
                    mm1 = mm2 = None
                    pend_av = None
                    for ci in range(ncch + 1):
                      if ci < ncch:
                        bank = ci & 1
                        slot = ci & 3
                        mm1 = p.op("t", lambda e, ci=ci, bank=bank, par=par, qc0=qc0:
                                   e.matmul(sc_ps[bank][:],
                                            kTn[par][:, ci * 128:(ci + 1) * 128],
                                            qTn[par][:, qc0:qc0 + 512],
                                            start=True, stop=False,
                                            skip_group_check=True),
                                   deps=[ld[par][0], ld[par][3], sc_war[bank]])
                        mm2 = p.op("t", lambda e, ci=ci, bank=bank, par=par, qc0=qc0:
                                   e.matmul(sc_ps[bank][:],
                                            kTr[par][:, ci * 128:(ci + 1) * 128],
                                            qTr[par][:, qc0:qc0 + 512],
                                            start=False, stop=True,
                                            skip_group_check=True),
                                   deps=[ld[par][1], ld[par][4]])
                        ex = p.op("s", lambda e, bank=bank, slot=slot: e.activation(
                            probs[slot][:], sc_ps[bank][:], AF.Exp,
                            bias=negc_bc[:], scale=1.0),
                            deps=[mm2, nbc, probs_war[slot]])
                        sc_war[bank] = ex
                        ptok = ex
                        if ci >= ncch - 8:
                            off = ci - (ncch - 8)
                            ptok = p.op("v", lambda e, slot=slot, off=off: e.tensor_tensor(
                                probs[slot][:], probs[slot][:],
                                mask_sb[:, off * 512:(off + 1) * 512], ALU.mult),
                                deps=[ex])
                      if pend_av is not None:
                        cj, slot_j, ptok_j = pend_av
                        avd = [ptok_j, ld[par][2]]
                        if cj == 0:
                            avd.append(av_war)
                        av_stop = p.op("t", lambda e, cj=cj, slot=slot_j, par=par, n=ncch:
                                       e.matmul(av_ps[:],
                                                vh[par][:, cj * 128:(cj + 1) * 128],
                                                probs[slot][:],
                                                start=(cj == 0), stop=(cj == n - 1),
                                                skip_group_check=True), deps=avd)
                        rd = [ptok_j]
                        if cj == 0:
                            rd.append(r_war)
                        r_stop = p.op("t", lambda e, cj=cj, slot=slot_j, n=ncch:
                                      e.matmul(r_ps[:], ones_st[:], probs[slot][:],
                                               start=(cj == 0), stop=(cj == n - 1),
                                               skip_group_check=True), deps=rd)
                        probs_war[slot_j] = r_stop
                      if ci < ncch:
                        pend_av = (ci, slot, ptok)
                    rr = p.op("v", lambda e: e.tensor_copy(rrow[:], r_ps[:]),
                              deps=[r_stop])
                    r_war = rr
                    ri = p.op("v", lambda e: e.reciprocal(rinv[:], rrow[:]), deps=[rr])
                    bq2 = p.op("t", lambda e: e.matmul(
                        bc_ps[:], ones_bc[:], rinv[:], start=True, stop=True,
                        skip_group_check=True), deps=[ri, nbc])
                    rbc = p.op("v", lambda e: e.tensor_copy(rinv_bc[:], bc_ps[:]),
                               deps=[bq2, av_war])
                    bc_war = rbc
                    apar = Qi & 1
                    ast = p.op("v", lambda e, apar=apar: e.tensor_tensor(
                        astage[apar][:], av_ps[:], rinv_bc[:], ALU.mult),
                        deps=[av_stop, rbc, ast_war[apar]])
                    av_war = ast
                    ast_war[apar] = p.dma("q", f"ast{apar}",
                                          attnT_s.ap()[h * 128:(h + 1) * 128,
                                                       qc0:qc0 + 512],
                                          astage[apar][:], deps=[ast])
                    negc_war = ex
                buf_war[par] = [sc_war[0], sc_war[1], av_stop, mm1, mm2]
            p.barrier()
            p.flush()

        with ExitStack() as SB:
            eb = SB.enter_context
            xqT = eb(nc.sbuf_tensor("xqT", [128, DCN * MYT], BF16))
            xq_tok = [[None] * DCN for _ in range(MYB)]
            cf_tok = [None] * MYB

            SB2 = ExitStack()
            eb2 = SB2.enter_context
            si3_bc = eb2(nc.sbuf_tensor("si3_bc", [128, MYB * 512], BF16))

            with ExitStack() as P3:
                e3 = P3.enter_context
                o_ps = [e3(nc.psum_tensor(f"o_ps{i}", [128, 512], F32)) for i in range(2)]
                stm_ps = e3(nc.psum_tensor("stm_ps", [1, 512], F32))
                st2_ps = e3(nc.psum_tensor("st2b_ps", [1, 512], F32))
                tp_ps = e3(nc.psum_tensor("tp3_ps", [128, 128], F32))
                bc_ps = e3(nc.psum_tensor("bc4_ps", [128, 512], F32))
                at_sb = e3(nc.sbuf_tensor("at_sb", [128, DCN * 512], BF16))
                ow_sb = [e3(nc.sbuf_tensor(f"ow_sb{i}", [128, DCN * 128], BF16))
                         for i in range(2)]
                xm_c = [e3(nc.sbuf_tensor(f"xm_c{i}", [128, 512], F32)) for i in range(3)]
                y1blk = e3(nc.sbuf_tensor("y1blk", [128, DCN * 512], F32))
                y1b = [e3(nc.sbuf_tensor(f"y1b{i}", [128, 512], BF16)) for i in range(2)]
                sqy = [e3(nc.sbuf_tensor(f"sqy{i}", [128, 512], BF16)) for i in range(2)]
                mx = e3(nc.sbuf_tensor("mx", [128, 512], F32))
                mn = e3(nc.sbuf_tensor("mn", [128, 512], F32))
                tr64 = e3(nc.sbuf_tensor("tr64", [64, 512], F32))
                tr32 = e3(nc.sbuf_tensor("tr32", [32, 512], F32))
                toks = e3(nc.sbuf_tensor("toks", [128, 24], F32))
                rows3 = e3(nc.sbuf_tensor("rows3", [1, 2048], F32))
                mu2_bc = e3(nc.sbuf_tensor("mu2_bc", [128, 512], F32))
                rs2_bc = e3(nc.sbuf_tensor("rs2_bc", [128, 512], F32))
                s_bc = e3(nc.sbuf_tensor("s_bc", [128, 512], F32))
                si_bc = e3(nc.sbuf_tensor("si_bc", [128, 512], F32))
                si3f = e3(nc.sbuf_tensor("si3f", [128, 512], F32))
                tq = [e3(nc.sbuf_tensor(f"tq{i}", [128, 512], F32)) for i in range(2)]
                yq = [e3(nc.sbuf_tensor(f"yq{i}", [128, 512], F32)) for i in range(2)]
                i32_t = [e3(nc.sbuf_tensor(f"i32_t{i}", [128, 512], I32)) for i in range(2)]
                r0_t = [e3(nc.sbuf_tensor(f"r0_t{i}", [128, 512], F32)) for i in range(2)]
                d_t = [e3(nc.sbuf_tensor(f"d_t{i}", [128, 512], F32)) for i in range(2)]
                a_t = [e3(nc.sbuf_tensor(f"a_t{i}", [128, 512], F32)) for i in range(2)]
                b_t = [e3(nc.sbuf_tensor(f"b_t{i}", [128, 512], F32)) for i in range(2)]

                at_tok = None
                at_war = None
                ow_tok = [None, None]
                ow_war = [None, None]
                xm_war = [None] * 3
                ops_war = [None, None]
                stm_war = None
                st2_war = None
                y1b_war = [None, None]
                sqy_war = [None, None]
                y1_war = [None] * DCN
                y1blk_war = [None] * DCN
                tp_war = None
                bc_war = None
                mxtree_war = None
                si3f_war = None
                chain_war = [None, None]
                tq_war = [None, None]
                yq_war3 = [None, None]
                i32_war3 = [None, None]
                r0_war3 = [None, None]
                d_war3 = [None, None]
                a_war3 = [None, None]
                b_war3 = [None, None]
                for blk in range(MYB):
                    gc0 = blk * 512
                    for fc in range(DCN):
                        at_tok = p.dma("s", "atld", at_sb[:, fc * 512:(fc + 1) * 512],
                                       attnT_s.ap()[fc * 128:(fc + 1) * 128,
                                                    gc0:gc0 + 512],
                                       deps=([at_war] if fc == 0 else []))
                    mxi = p.op("v", lambda e: e.memset(mx[:], -3.0e38),
                               deps=[mxtree_war])
                    mni = p.op("v", lambda e: e.memset(mn[:], 3.0e38),
                               deps=[mxtree_war])
                    stop = None
                    for dc in range(DCN):
                        opar = dc & 1
                        ow_tok[opar] = p.dma("g", f"owl{opar}",
                                             ow_sb[opar][:].rearrange(
                                                 "p (c f) -> p c f", c=DCN),
                                             ow_h.ap()[dc], deps=[ow_war[opar]])
                        cr = dc % 3
                        xm = p.dma("q", f"xm{cr}", xm_c[cr][:],
                                   xoT_h.ap()[dc, :, gc0:gc0 + 512],
                                   deps=[xm_war[cr]])
                        for fc in range(DCN):
                            dd = [at_tok, ow_tok[opar]]
                            if fc == 0:
                                dd.append(ops_war[opar])
                            stop = p.op("t", lambda e, fc=fc, opar=opar: e.matmul(
                                o_ps[opar][:],
                                ow_sb[opar][:, fc * 128:(fc + 1) * 128],
                                at_sb[:, fc * 512:(fc + 1) * 512],
                                start=(fc == 0), stop=(fc == DCN - 1),
                                skip_group_check=True), deps=dd)
                        ow_war[opar] = stop
                        y1c = p.op("v", lambda e, dc=dc, opar=opar, cr=cr: e.tensor_tensor(
                            y1blk[:, dc * 512:(dc + 1) * 512], o_ps[opar][:],
                            xm_c[cr][:], ALU.add),
                            deps=[stop, xm, y1blk_war[dc], p.latest("y1st")])
                        ops_war[opar] = y1c
                        xm_war[cr] = y1c
                        mxi = p.op("v", lambda e, dc=dc: e.tensor_tensor(
                            mx[:], mx[:], y1blk[:, dc * 512:(dc + 1) * 512], ALU.max),
                            deps=[y1c, mxi])
                        mni = p.op("v", lambda e, dc=dc: e.tensor_tensor(
                            mn[:], mn[:], y1blk[:, dc * 512:(dc + 1) * 512], ALU.min),
                            deps=[y1c, mni])
                        sp = dc & 1
                        yb = p.op("s", lambda e, dc=dc, sp=sp: e.copy(
                            y1b[sp][:], y1blk[:, dc * 512:(dc + 1) * 512]),
                            deps=[y1c] + (y1b_war[sp] or []))
                        sy = p.op("g", lambda e, sp=sp: e.tensor_tensor(
                            sqy[sp][:], y1b[sp][:], y1b[sp][:], ALU.mult),
                            deps=[yb, sqy_war[sp]])
                        stm = p.op("t", lambda e, dc=dc, sp=sp: e.matmul(
                            stm_ps[:], ones_st[:], y1b[sp][:],
                            start=(dc == 0), stop=(dc == DCN - 1),
                            skip_group_check=True),
                            deps=[yb] + ([stm_war] if dc == 0 else []))
                        st2 = p.op("t", lambda e, dc=dc, sp=sp: e.matmul(
                            st2_ps[:], ones_st[:], sqy[sp][:],
                            start=(dc == 0), stop=(dc == DCN - 1),
                            skip_group_check=True),
                            deps=[sy] + ([st2_war] if dc == 0 else []))
                        y1b_war[sp] = [stm, sy]
                        sqy_war[sp] = st2
                        y1_war[dc] = p.dma("q", "y1st", y1_s.ap()[dc, :, gc0:gc0 + 512],
                                           y1blk[:, dc * 512:(dc + 1) * 512],
                                           deps=[y1c])
                    at_war = stop
                    muv = p.op("v", lambda e: e.tensor_scalar_mul(
                        rows3[0:1, 0:512], stm_ps[:], 1.0 / D), deps=[stm])
                    stm_war = muv
                    ex2 = p.op("v", lambda e: e.tensor_scalar_mul(
                        rows3[0:1, 512:1024], st2_ps[:], 1.0 / D), deps=[st2])
                    st2_war = ex2
                    ms2 = p.op("v", lambda e: e.tensor_tensor(
                        rows3[0:1, 1024:1536], rows3[0:1, 0:512], rows3[0:1, 0:512],
                        ALU.mult), deps=[muv])
                    vr = p.op("v", lambda e: e.scalar_tensor_tensor(
                        rows3[0:1, 512:1024], rows3[0:1, 512:1024], EPS_LN,
                        rows3[0:1, 1024:1536], ALU.add, ALU.subtract),
                        deps=[ex2, ms2])
                    l1 = p.op("s", lambda e: e.activation(
                        rows3[0:1, 512:1024], rows3[0:1, 512:1024], AF.Ln), deps=[vr])
                    l2 = p.op("s", lambda e: e.activation(
                        rows3[0:1, 512:1024], rows3[0:1, 512:1024], AF.Exp,
                        scale=-0.5), deps=[l1])
                    for k in range(4):
                        t1 = p.op("t", lambda e, k=k: e.transpose(
                            tp_ps[:, 0:1], rows3[0:1, k * 128:(k + 1) * 128],
                            ident_f[0:1, 0:1]), deps=[muv, tp_war])
                        c1 = p.op("v", lambda e, k=k: e.tensor_copy(
                            toks[:, k:k + 1], tp_ps[:, 0:1]), deps=[t1])
                        t2 = p.op("t", lambda e, k=k: e.transpose(
                            tp_ps[:, 0:1],
                            rows3[0:1, 512 + k * 128:512 + (k + 1) * 128],
                            ident_f[0:1, 0:1]), deps=[l2, c1])
                        c2 = p.op("v", lambda e, k=k: e.tensor_copy(
                            toks[:, 4 + k:5 + k], tp_ps[:, 0:1]), deps=[t2])
                        tp_war = c2
                    t64c = p.op("v", lambda e: e.tensor_copy(
                        tr64[:], mx[64:128, :]), deps=[mxi])
                    t64 = p.op("v", lambda e: e.tensor_tensor(
                        tr64[:], tr64[:], mx[0:64, :], ALU.max), deps=[t64c])
                    t32c = p.op("v", lambda e: e.tensor_copy(
                        tr32[:], tr64[32:64, :]), deps=[t64])
                    t32 = p.op("v", lambda e: e.tensor_tensor(
                        tr32[:], tr32[:], tr64[0:32, :], ALU.max), deps=[t32c])
                    for k in range(4):
                        tt = p.op("t", lambda e, k=k: e.transpose(
                            tp_ps[:, 0:32], tr32[:, k * 128:(k + 1) * 128],
                            ident_f[0:32, 0:32]), deps=[t32, tp_war])
                        rm = p.op("v", lambda e, k=k: e.reduce_max(
                            toks[:, 8 + k:9 + k], tp_ps[:, 0:32], AX.X), deps=[tt])
                        tp_war = rm
                    t64nc = p.op("v", lambda e: e.tensor_copy(
                        tr64[:], mn[64:128, :]), deps=[mni, t32])
                    t64n = p.op("v", lambda e: e.tensor_tensor(
                        tr64[:], tr64[:], mn[0:64, :], ALU.min), deps=[t64nc])
                    t32nc = p.op("v", lambda e: e.tensor_copy(
                        tr32[:], tr64[32:64, :]), deps=[t64n])
                    t32n = p.op("v", lambda e: e.tensor_tensor(
                        tr32[:], tr32[:], tr64[0:32, :], ALU.min), deps=[t32nc])
                    for k in range(4):
                        tt = p.op("t", lambda e, k=k: e.transpose(
                            tp_ps[:, 0:32], tr32[:, k * 128:(k + 1) * 128],
                            ident_f[0:32, 0:32]), deps=[t32n, tp_war])
                        rm = p.op("v", lambda e, k=k: e.tensor_reduce(
                            toks[:, 12 + k:13 + k], tp_ps[:, 0:32], AX.X, ALU.min,
                            negate=True), deps=[tt])
                        tp_war = rm
                    mxtree_war = tp_war
                    v1 = p.op("v", lambda e: e.tensor_tensor(
                        toks[:, 16:20], toks[:, 8:12], toks[:, 0:4], ALU.subtract),
                        deps=[tp_war])
                    v2 = p.op("v", lambda e: e.tensor_tensor(
                        toks[:, 20:24], toks[:, 12:16], toks[:, 0:4], ALU.add),
                        deps=[v1])
                    v3 = p.op("v", lambda e: e.tensor_tensor(
                        toks[:, 16:20], toks[:, 16:20], toks[:, 20:24], ALU.max),
                        deps=[v2])
                    v4 = p.op("v", lambda e: e.tensor_tensor(
                        toks[:, 16:20], toks[:, 16:20], toks[:, 4:8], ALU.mult),
                        deps=[v3])
                    v5 = p.op("v", lambda e: e.tensor_scalar_max(
                        toks[:, 16:20], toks[:, 16:20], 1e-5), deps=[v4])
                    v6 = p.op("v", lambda e: e.reciprocal(
                        toks[:, 20:24], toks[:, 16:20]), deps=[v5])
                    v7 = p.op("v", lambda e: e.tensor_scalar_mul(
                        toks[:, 20:24], toks[:, 20:24], 127.0), deps=[v6])
                    v8 = p.op("v", lambda e: e.tensor_scalar_mul(
                        toks[:, 16:20], toks[:, 16:20], INV127), deps=[v5, v7])
                    for k in range(4):
                        t1 = p.op("t", lambda e, k=k: e.transpose(
                            tp_ps[0:1, :], toks[:, 20 + k:21 + k], ident_f[:]),
                            deps=[v7, tp_war])
                        c1 = p.op("v", lambda e, k=k: e.tensor_copy(
                            rows3[0:1, 1024 + k * 128:1024 + (k + 1) * 128],
                            tp_ps[0:1, :]), deps=[t1])
                        t2 = p.op("t", lambda e, k=k: e.transpose(
                            tp_ps[0:1, :], toks[:, 16 + k:17 + k], ident_f[:]),
                            deps=[v8, c1])
                        c2 = p.op("v", lambda e, k=k: e.tensor_copy(
                            rows3[0:1, 1536 + k * 128:1536 + (k + 1) * 128],
                            tp_ps[0:1, :]), deps=[t2])
                        tp_war = c2
                    toks_done = tp_war
                    bt = p.op("t", lambda e: e.matmul(
                        bc_ps[:], ones_bc[:], rows3[0:1, 0:512], start=True, stop=True,
                        skip_group_check=True), deps=[muv, bc_war])
                    bmu = p.op("v", lambda e: e.tensor_copy(mu2_bc[:], bc_ps[:]),
                               deps=[bt, chain_war[0], chain_war[1]])
                    bt = p.op("t", lambda e: e.matmul(
                        bc_ps[:], ones_bc[:], rows3[0:1, 512:1024], start=True,
                        stop=True, skip_group_check=True), deps=[l2, bmu])
                    brs = p.op("v", lambda e: e.tensor_copy(rs2_bc[:], bc_ps[:]),
                               deps=[bt, chain_war[1], chain_war[0]])
                    bt = p.op("t", lambda e: e.matmul(
                        bc_ps[:], ones_bc[:], rows3[0:1, 1024:1536], start=True,
                        stop=True, skip_group_check=True), deps=[toks_done, brs])
                    bs = p.op("v", lambda e: e.tensor_copy(s_bc[:], bc_ps[:]),
                              deps=[bt])
                    bt = p.op("t", lambda e: e.matmul(
                        bc_ps[:], ones_bc[:], rows3[0:1, 1536:2048], start=True,
                        stop=True, skip_group_check=True), deps=[toks_done, bs])
                    bsi = p.op("v", lambda e: e.tensor_copy(si_bc[:], bc_ps[:]),
                               deps=[bt])
                    s3a = p.op("v", lambda e, blk=blk: e.tensor_tensor(
                        si3f[:], si_bc[:], si_bc[:], ALU.mult), deps=[bsi, si3f_war])
                    s3b = p.op("v", lambda e, blk=blk: e.tensor_tensor(
                        si3_bc[:, blk * 512:(blk + 1) * 512], si3f[:], si_bc[:],
                        ALU.mult), deps=[s3a])
                    si3f_war = s3b
                    bc_war = bsi
                    for dc0 in range(0, DCN, 2):
                        pair = (dc0, dc0 + 1)
                        hh = {}
                        for dc in pair:
                            cp = dc & 1
                            hh[(dc, "q1")] = p.op(
                                "v", lambda e, dc=dc, cp=cp: e.tensor_tensor(
                                    tq[cp][:], y1blk[:, dc * 512:(dc + 1) * 512],
                                    mu2_bc[:], ALU.subtract),
                                deps=[bmu, tq_war[cp]])
                            y1blk_war[dc] = hh[(dc, "q1")]
                        for dc in pair:
                            cp = dc & 1
                            hh[(dc, "q2")] = p.op("g", lambda e, cp=cp: e.tensor_tensor(
                                tq[cp][:], tq[cp][:], rs2_bc[:], ALU.mult),
                                deps=[hh[(dc, "q1")], brs])
                        for dc in pair:
                            cp = dc & 1
                            hh[(dc, "q3")] = p.op("v", lambda e, cp=cp: e.tensor_tensor(
                                yq[cp][:], tq[cp][:], s_bc[:], ALU.mult),
                                deps=[hh[(dc, "q2")], bs, yq_war3[cp]])
                            tq_war[cp] = hh[(dc, "q3")]
                        for dc in pair:
                            cp = dc & 1
                            c1 = p.op("s", lambda e, cp=cp: e.copy(
                                i32_t[cp][:], yq[cp][:]),
                                deps=[hh[(dc, "q3")], i32_war3[cp]])
                            c2 = p.op("s", lambda e, cp=cp: e.copy(
                                r0_t[cp][:], i32_t[cp][:]), deps=[c1, r0_war3[cp]])
                            i32_war3[cp] = c2
                            hh[(dc, "c2")] = c2
                        for dc in pair:
                            cp = dc & 1
                            hh[(dc, "dd")] = p.op("g", lambda e, cp=cp: e.tensor_tensor(
                                d_t[cp][:], yq[cp][:], r0_t[cp][:], ALU.subtract),
                                deps=[hh[(dc, "c2")], d_war3[cp]])
                        for dc in pair:
                            cp = dc & 1
                            ab = p.op("s", lambda e, cp=cp: e.activation(
                                yq[cp][:], d_t[cp][:], AF.Abs), deps=[hh[(dc, "dd")]])
                            hh[(dc, "sg")] = p.op("s", lambda e, cp=cp: e.activation(
                                d_t[cp][:], d_t[cp][:], AF.Sign), deps=[ab])
                            hh[(dc, "ab")] = ab
                        cmh = {}
                        for dc in pair:
                            cp = dc & 1
                            cmh[dc] = p.op("v", lambda e, cp=cp: e.tensor_single_scalar(
                                a_t[cp][:], yq[cp][:], 0.5, ALU.is_gt),
                                deps=[hh[(dc, "ab")], a_war3[cp]])
                        for thr in (1.5, 2.5, 3.5):
                            for dc in pair:
                                cp = dc & 1
                                cmh[dc] = p.op("v", lambda e, cp=cp, thr=thr:
                                               e.scalar_tensor_tensor(
                                                   a_t[cp][:], yq[cp][:], thr,
                                                   a_t[cp][:], ALU.is_gt, ALU.add),
                                               deps=[cmh[dc]])
                        for dc in pair:
                            cp = dc & 1
                            hh[(dc, "co")] = p.op("g", lambda e, cp=cp: e.tensor_tensor(
                                b_t[cp][:], a_t[cp][:], d_t[cp][:], ALU.mult),
                                deps=[cmh[dc], hh[(dc, "sg")], b_war3[cp]])
                            yq_war3[cp] = cmh[dc]
                            a_war3[cp] = hh[(dc, "co")]
                            d_war3[cp] = hh[(dc, "co")]
                        for dc in pair:
                            cp = dc & 1
                            xq = p.op("v", lambda e, dc=dc, cp=cp, gc0=gc0:
                                      e.tensor_tensor(
                                          xqT[:, dc * MYT + gc0:dc * MYT + gc0 + 512],
                                          b_t[cp][:], r0_t[cp][:], ALU.add),
                                      deps=[hh[(dc, "co")]])
                            xq_tok[blk][dc] = xq
                            b_war3[cp] = xq
                            r0_war3[cp] = xq
                            chain_war[cp] = xq
                p.barrier()
                p.flush()

            gw_store = [[None] * MYB for _ in range(FFC)]
            cf_bc = eb2(nc.sbuf_tensor("cf_bc", [128, MYB * 512], BF16))
            sif_bc = eb2(nc.sbuf_tensor("sif_bc", [128, MYB * 512], BF16))
            mxf = eb2(nc.sbuf_tensor("mxf", [128, MYB * 512], BF16))
            mnf = eb2(nc.sbuf_tensor("mnf", [128, MYB * 512], BF16))
            statf_row = eb2(nc.sbuf_tensor("statf_row", [1, MYB * 512], BF16))
            with ExitStack() as P4:
                e4 = P4.enter_context
                f_ps = [e4(nc.psum_tensor(f"f_ps{i}", [128, 512], F32)) for i in range(4)]
                stf_ps = e4(nc.psum_tensor("stf_ps", [1, MYB * 512], F32))
                w1_sb = [e4(nc.sbuf_tensor(f"w1_sb{i}", [128, DCN * 128], BF16))
                         for i in range(2)]
                w3_sb = [e4(nc.sbuf_tensor(f"w3_sb{i}", [128, DCN * 128], BF16))
                         for i in range(2)]
                w3l_sb = [e4(nc.sbuf_tensor(f"w3l_sb{i}", [128, DCN * 128], BF16))
                          for i in range(2)]
                w3f_sb = [e4(nc.sbuf_tensor(f"w3f_sb{i}", [128, DCN * 128], F32))
                          for i in range(2)]
                u_t = [e4(nc.sbuf_tensor(f"u_t{i}", [128, 512], F32)) for i in range(2)]
                u2_t = [e4(nc.sbuf_tensor(f"u2_t{i}", [128, 512], F32)) for i in range(2)]
                g_t = [e4(nc.sbuf_tensor(f"g_t{i}", [128, 512], F32)) for i in range(2)]
                gw_t = [e4(nc.sbuf_tensor(f"gw_t{i}", [128, 512], BF16)) for i in range(2)]
                sqf = [e4(nc.sbuf_tensor(f"sqf{i}", [128, 512], BF16)) for i in range(2)]

                w1_tok = [None, None]
                w3_tok = [None, None]
                w3l_tok = [None, None]
                w3f_war = [None, None]
                w_war = [None, None]
                fps_war = [None] * 4
                u_war = [None, None]
                g_war = [None, None]
                gsq_war = [None, None]
                gw_war = [None, None]
                sq_war = [None, None]
                mx_tok = p.op("v", lambda e: e.memset(mxf[:], -3.0e38), deps=[])
                mn_tok = p.op("v", lambda e: e.memset(mnf[:], 3.0e38), deps=[])
                stf_stop = [None] * MYB
                pend_stat = []
                for fc in range(FFC):
                    wpar = fc & 1
                    w1_tok[wpar] = p.dma("g", f"wld{wpar}",
                                         w1_sb[wpar][:].rearrange("p (c f) -> p c f", c=DCN),
                                         w1_h.ap()[fc], deps=[w_war[wpar]])
                    w3_tok[wpar] = p.dma("g", f"wld{wpar}",
                                         w3_sb[wpar][:].rearrange("p (c f) -> p c f", c=DCN),
                                         w3_h.ap()[fc], deps=[])
                    wf = p.dma("q", f"wf{wpar}",
                               w3f_sb[wpar][:].rearrange("p (c f) -> p c f", c=DCN),
                               w3_h.ap()[fc], deps=[w3f_war[wpar]])
                    w3l_tok[wpar] = p.op("v", lambda e, wpar=wpar: e.tensor_tensor(
                        w3l_sb[wpar][:], w3f_sb[wpar][:], w3_sb[wpar][:],
                        ALU.subtract), deps=[wf, w3_tok[wpar], w_war[wpar]])
                    w3f_war[wpar] = w3l_tok[wpar]
                    for blk in range(MYB):
                        bp = blk & 1
                        p1b, p3b = bp, 2 + bp
                        stop1 = None
                        stop3 = None
                        for dc in range(DCN):
                            dd1 = [xq_tok[blk][dc], w3_tok[wpar]]
                            if dc == 0:
                                dd1.append(fps_war[p1b])
                            stop1 = p.op("t", lambda e, dc=dc, blk=blk, wpar=wpar, p1b=p1b:
                                         e.matmul(f_ps[p1b][:],
                                                  w1_sb[wpar][:, dc * 128:(dc + 1) * 128],
                                                  xqT[:, dc * MYT + blk * 512:
                                                      dc * MYT + (blk + 1) * 512],
                                                  start=(dc == 0), stop=(dc == DCN - 1),
                                                  skip_group_check=True), deps=dd1)
                        for dcl in range(2 * DCN):
                            dc, lo = dcl % DCN, dcl >= DCN
                            dd3 = [xq_tok[blk][dc],
                                   w3l_tok[wpar] if lo else w3_tok[wpar]]
                            if dcl == 0:
                                dd3.append(fps_war[p3b])
                            stop3 = p.op("t", lambda e, dc=dc, dcl=dcl, lo=lo, blk=blk,
                                         wpar=wpar, p3b=p3b:
                                         e.matmul(f_ps[p3b][:],
                                                  (w3l_sb if lo else w3_sb)[wpar][:, dc * 128:(dc + 1) * 128],
                                                  xqT[:, dc * MYT + blk * 512:
                                                      dc * MYT + (blk + 1) * 512],
                                                  start=(dcl == 0), stop=(dcl == 2 * DCN - 1),
                                                  skip_group_check=True), deps=dd3)
                        sp = (fc * MYB + blk) & 1
                        u = p.op("s", lambda e, sp=sp, p1b=p1b: e.activation(
                            u_t[sp][:], f_ps[p1b][:], AF.Relu), deps=[stop1, u_war[sp]])
                        u2 = p.op("v", lambda e, sp=sp: e.tensor_tensor(
                            u2_t[sp][:], u_t[sp][:], u_t[sp][:], ALU.mult), deps=[u])
                        u_war[sp] = u2
                        g = p.op("v", lambda e, sp=sp, p3b=p3b: e.tensor_tensor(
                            g_t[sp][:], u2_t[sp][:], f_ps[p3b][:], ALU.mult),
                            deps=[u2, stop3, g_war[sp], gsq_war[sp]])
                        g = p.op("v", lambda e, sp=sp, blk=blk: e.tensor_tensor(
                            g_t[sp][:], g_t[sp][:],
                            si3_bc[:, blk * 512:(blk + 1) * 512], ALU.mult),
                            deps=[g])
                        fps_war[p1b] = u
                        fps_war[p3b] = g
                        gw = p.op("v", lambda e, sp=sp, fc=fc: e.tensor_scalar_mul(
                            gw_t[sp][:], g_t[sp][:], gf_sb[:, fc:fc + 1]),
                            deps=[g, g_f, gw_war[sp], mn_tok])
                        g_war[sp] = gw
                        sq = p.op("g", lambda e, sp=sp: e.tensor_tensor(
                            sqf[sp][:], g_t[sp][:], g_t[sp][:], ALU.mult),
                            deps=[g, sq_war[sp]])
                        gsq_war[sp] = sq
                        pend_stat.append((fc, blk, sp, sq))
                        if len(pend_stat) > 1:
                            fc2, blk2, sp2, sq2 = pend_stat.pop(0)
                            stf_stop[blk2] = p.op(
                                "t", lambda e, fc2=fc2, blk2=blk2, sp2=sp2:
                                e.matmul(stf_ps[0:1, blk2 * 512:(blk2 + 1) * 512],
                                         ones_st[:], sqf[sp2][:],
                                         start=(fc2 == 0), stop=(fc2 == FFC - 1),
                                         skip_group_check=True), deps=[sq2])
                            sq_war[sp2] = stf_stop[blk2]
                        mx_tok = p.op("v", lambda e, blk=blk, sp=sp: e.tensor_tensor(
                            mxf[:, blk * 512:(blk + 1) * 512],
                            mxf[:, blk * 512:(blk + 1) * 512], gw_t[sp][:], ALU.max),
                            deps=[gw, mx_tok])
                        mn_tok = p.op("v", lambda e, blk=blk, sp=sp: e.tensor_tensor(
                            mnf[:, blk * 512:(blk + 1) * 512],
                            mnf[:, blk * 512:(blk + 1) * 512], gw_t[sp][:], ALU.min),
                            deps=[gw, mx_tok])
                        gw_store[fc][blk] = p.dma("q", f"gst{sp}",
                                                  gw_s.ap()[fc, :, blk * 512:(blk + 1) * 512],
                                                  gw_t[sp][:], deps=[gw])
                        gw_war[sp] = gw_store[fc][blk]
                    w_war[wpar] = stop3
                while pend_stat:
                    fc2, blk2, sp2, sq2 = pend_stat.pop(0)
                    stf_stop[blk2] = p.op(
                        "t", lambda e, fc2=fc2, blk2=blk2, sp2=sp2:
                        e.matmul(stf_ps[0:1, blk2 * 512:(blk2 + 1) * 512],
                                 ones_st[:], sqf[sp2][:],
                                 start=(fc2 == 0), stop=(fc2 == FFC - 1),
                                 skip_group_check=True), deps=[sq2])
                    sq_war[sp2] = stf_stop[blk2]
                sfr = p.op("v", lambda e: e.tensor_scalar(
                    statf_row[:], stf_ps[:], 1.0 / FF, EPS_RMS_FFN, ALU.mult, ALU.add),
                    deps=stf_stop)
                p.barrier()
                p.flush()

            with ExitStack() as P4T:
                e5 = P4T.enter_context
                tp_ps = e5(nc.psum_tensor("tp4_ps", [128, 128], F32))
                tpb_ps = e5(nc.psum_tensor("tpb_ps", [128, 128], BF16))
                bc_ps = e5(nc.psum_tensor("bc5_ps", [128, 512], F32))
                amf = e5(nc.sbuf_tensor("amf", [128, 512], BF16))
                trb64 = e5(nc.sbuf_tensor("trb64", [64, 512], BF16))
                trb32 = e5(nc.sbuf_tensor("trb32", [32, 512], BF16))
                toks4 = e5(nc.sbuf_tensor("toks4", [128, 8], F32))
                rows4 = e5(nc.sbuf_tensor("rows4", [1, 1536], F32))
                rsf = p.op("s", lambda e: e.activation(statf_row[:], statf_row[:],
                                                       AF.Ln), deps=[sfr])
                rsf2 = p.op("s", lambda e: e.activation(statf_row[:], statf_row[:],
                                                        AF.Exp, scale=-0.5), deps=[rsf])
                tp_war = None
                bc_war = None
                am_war = None
                for blk in range(MYB):
                    ng = p.op("v", lambda e, blk=blk: e.tensor_scalar_mul(
                        amf[:], mnf[:, blk * 512:(blk + 1) * 512], -1.0),
                        deps=[mn_tok, am_war])
                    am = p.op("v", lambda e, blk=blk: e.tensor_tensor(
                        amf[:], amf[:], mxf[:, blk * 512:(blk + 1) * 512], ALU.max),
                        deps=[ng, mx_tok])
                    t64c = p.op("v", lambda e: e.tensor_copy(
                        trb64[:], amf[64:128, :]), deps=[am])
                    t64 = p.op("v", lambda e: e.tensor_tensor(
                        trb64[:], trb64[:], amf[0:64, :], ALU.max), deps=[t64c])
                    t32c = p.op("v", lambda e: e.tensor_copy(
                        trb32[:], trb64[32:64, :]), deps=[t64])
                    t32 = p.op("v", lambda e: e.tensor_tensor(
                        trb32[:], trb32[:], trb64[0:32, :], ALU.max), deps=[t32c])
                    am_war = t64
                    ktoks = []
                    for k in range(4):
                        tt = p.op("t", lambda e, k=k: e.transpose(
                            tpb_ps[:, 0:32], trb32[:, k * 128:(k + 1) * 128],
                            ident_b[0:32, 0:32]), deps=[t32, tp_war])
                        rm = p.op("v", lambda e, k=k: e.reduce_max(
                            toks4[:, 0:1], tpb_ps[:, 0:32], AX.X), deps=[tt])
                        Mc = p.op("v", lambda e: e.tensor_scalar_max(
                            toks4[:, 0:1], toks4[:, 0:1], 1e-5), deps=[rm])
                        R = p.op("v", lambda e: e.reciprocal(
                            toks4[:, 1:2], toks4[:, 0:1]), deps=[Mc])
                        sv = p.op("v", lambda e: e.tensor_scalar_mul(
                            toks4[:, 2:3], toks4[:, 1:2], 127.0), deps=[R])
                        siv = p.op("v", lambda e: e.tensor_scalar_mul(
                            toks4[:, 3:4], toks4[:, 0:1], INV127), deps=[Mc, sv])
                        ts1 = p.op("t", lambda e: e.transpose(
                            tp_ps[0:1, :], toks4[:, 2:3], ident_f[:]),
                            deps=[sv, tt])
                        cs1 = p.op("v", lambda e, k=k: e.tensor_copy(
                            rows4[0:1, k * 128:(k + 1) * 128], tp_ps[0:1, :]),
                            deps=[ts1])
                        ts2 = p.op("t", lambda e: e.transpose(
                            tp_ps[0:1, :], toks4[:, 3:4], ident_f[:]),
                            deps=[siv, cs1])
                        cs2 = p.op("v", lambda e, k=k: e.tensor_copy(
                            rows4[0:1, 512 + k * 128:512 + (k + 1) * 128],
                            tp_ps[0:1, :]), deps=[ts2])
                        tp_war = cs2
                        ktoks.append(cs2)
                    cr = p.op("v", lambda e, blk=blk: e.tensor_tensor(
                        rows4[0:1, 1024:1536], rows4[0:1, 512:1024],
                        statf_row[0:1, blk * 512:(blk + 1) * 512], ALU.mult),
                        deps=ktoks + [rsf2])
                    b1 = p.op("t", lambda e: e.matmul(
                        bc_ps[:], ones_bc[:], rows4[0:1, 0:512], start=True,
                        stop=True, skip_group_check=True), deps=ktoks + [bc_war])
                    c1 = p.op("v", lambda e, blk=blk: e.tensor_copy(
                        cf_bc[:, blk * 512:(blk + 1) * 512], bc_ps[:]), deps=[b1])
                    b2 = p.op("t", lambda e: e.matmul(
                        bc_ps[:], ones_bc[:], rows4[0:1, 1024:1536], start=True,
                        stop=True, skip_group_check=True), deps=[c1, cr])
                    c2 = p.op("v", lambda e, blk=blk: e.tensor_copy(
                        sif_bc[:, blk * 512:(blk + 1) * 512], bc_ps[:]), deps=[b2])
                    bc_war = c2
                    cf_tok[blk] = c2
                p.barrier()
                p.flush()

            with ExitStack() as P5:
                e6 = P5.enter_context
                o5_ps = [e6(nc.psum_tensor(f"o5_ps{i}", [128, 512], F32)) for i in range(2)]
                iqT = e6(nc.sbuf_tensor("iqT", [128, FFC * 512], BF16))
                w2_sb = [e6(nc.sbuf_tensor(f"w2_sb{i}", [128, FFC * 128], BF16))
                         for i in range(2)]
                NCH = 4
                gw_c = [e6(nc.sbuf_tensor(f"gw_c{i}", [128, 512], BF16)) for i in range(NCH)]
                yq5 = [e6(nc.sbuf_tensor(f"yq5_{i}", [128, 512], F32)) for i in range(NCH)]
                i32_5 = [e6(nc.sbuf_tensor(f"i32_5_{i}", [128, 512], I32)) for i in range(2)]
                r0n_5 = [e6(nc.sbuf_tensor(f"r0n_5_{i}", [128, 512], BF16)) for i in range(NCH)]
                sg_5 = [e6(nc.sbuf_tensor(f"sg_5_{i}", [128, 512], BF16)) for i in range(NCH)]
                cb_5 = [e6(nc.sbuf_tensor(f"cb_5_{i}", [128, 512], BF16)) for i in range(NCH)]
                mb_5 = [e6(nc.sbuf_tensor(f"mb_5_{i}", [128, 512], F32)) for i in range(NCH)]
                y1c = [e6(nc.sbuf_tensor(f"y1c{i}", [128, 512], F32)) for i in range(2)]

                gw_ld = [None] * NCH
                q1_last = [None] * NCH
                iq_tok = [None] * FFC
                iq_guard = [None] * FFC
                w2_tok = [None, None]
                w2_war = [None, None]
                o5_war = [None, None]
                y1_war5 = [None, None]
                y2_war = [None, None]
                i32_war = [None, None]
                yq_war = [None] * NCH
                mb_war = [None] * NCH
                sg_war = [None] * NCH
                cb_war = [None] * NCH
                r0_war = [None] * NCH
                for blk in range(MYB):
                    gc0 = blk * 512
                    def head5(kc0, blk):
                        gc0b = blk * 512
                        pair = (kc0, kc0 + 1)
                        hh = {}
                        for kc in pair:
                            gp = kc & 3
                            if blk == 0 and kc < 2:
                                gw_ld[gp] = p.dma("q", f"gwld{gp}", gw_c[gp][:],
                                                  gw_s.ap()[kc, :, gc0b:gc0b + 512],
                                                  deps=[gw_store[kc][blk]])
                        for kc in pair:
                            gp = kc & 3
                            hh[(kc, "q1")] = p.op(
                                "v", lambda e, gp=gp, blk=blk: e.tensor_tensor(
                                    yq5[gp][:], gw_c[gp][:],
                                    cf_bc[:, blk * 512:(blk + 1) * 512], ALU.mult),
                                deps=[gw_ld[gp], cf_tok[blk], yq_war[gp]])
                            q1_last[gp] = hh[(kc, "q1")]
                        for kc in pair:
                            nkc, nblk = kc + 2, blk
                            if nkc >= FFC:
                                nkc, nblk = kc + 2 - FFC, blk + 1
                            gpn = nkc & 3
                            if nblk < MYB:
                                gw_ld[gpn] = p.dma("q", f"gwld{gpn}", gw_c[gpn][:],
                                                   gw_s.ap()[nkc, :, nblk * 512:nblk * 512 + 512],
                                                   deps=[q1_last[gpn], gw_store[nkc][nblk]])
                        for kc in pair:
                            gp = kc & 3
                            ip = kc & 1
                            c1 = p.op("s", lambda e, gp=gp, ip=ip: e.copy(
                                i32_5[ip][:], yq5[gp][:]),
                                deps=[hh[(kc, "q1")], i32_war[ip]])
                            c2 = p.op("s", lambda e, gp=gp, ip=ip: e.activation(
                                r0n_5[gp][:], i32_5[ip][:], AF.Copy, scale=-1.0),
                                deps=[c1, r0_war[gp]])
                            i32_war[ip] = c2
                            hh[(kc, "c2")] = c2
                        for kc in pair:
                            gp = kc & 3
                            hh[(kc, "dd")] = p.op("v", lambda e, gp=gp: e.tensor_tensor(
                                mb_5[gp][:], yq5[gp][:], r0n_5[gp][:], ALU.add),
                                deps=[hh[(kc, "c2")], mb_war[gp]])
                        for kc in pair:
                            gp = kc & 3
                            sg = p.op("s", lambda e, gp=gp: e.activation(
                                sg_5[gp][:], mb_5[gp][:], AF.Sign),
                                deps=[hh[(kc, "dd")], sg_war[gp]])
                            hh[(kc, "sg")] = sg
                            hh[(kc, "ab")] = p.op("s", lambda e, gp=gp: e.activation(
                                mb_5[gp][:], mb_5[gp][:], AF.Abs), deps=[sg])
                        lvh = {kc: hh[(kc, "ab")] for kc in pair}
                        for li, w in enumerate((128.0, 64.0, 32.0, 16.0, 8.0, 4.0)):
                            for kc in pair:
                                gp = kc & 3
                                ddl = [lvh[kc]]
                                if li == 0:
                                    ddl.append(cb_war[gp])
                                lvh[kc] = p.op("v", lambda e, gp=gp, w=w:
                                               e.tensor_scalar(cb_5[gp][:], mb_5[gp][:],
                                                               w, w, ALU.is_gt,
                                                               ALU.mult), deps=ddl)
                            for kc in pair:
                                gp = kc & 3
                                lvh[kc] = p.op("v", lambda e, gp=gp:
                                               e.tensor_tensor(mb_5[gp][:], mb_5[gp][:],
                                                               cb_5[gp][:],
                                                               ALU.subtract),
                                               deps=[lvh[kc]])
                        for kc in pair:
                            hh[(kc, "lv")] = lvh[kc]
                        return hh

                    def tail5(kc0, blk, hh):
                        pair = (kc0, kc0 + 1)
                        cmh = {}
                        for kc in pair:
                            gp = kc & 3
                            cmh[kc] = p.op("v", lambda e, gp=gp: e.tensor_single_scalar(
                                cb_5[gp][:], mb_5[gp][:], 0.5, ALU.is_gt),
                                deps=[hh[(kc, "lv")], cb_war[gp]])
                        for thr in (1.5, 2.5, 3.5):
                            for kc in pair:
                                gp = kc & 3
                                cmh[kc] = p.op("v", lambda e, gp=gp, thr=thr:
                                               e.scalar_tensor_tensor(
                                                   cb_5[gp][:], mb_5[gp][:], thr,
                                                   cb_5[gp][:], ALU.is_gt, ALU.add),
                                               deps=[cmh[kc]])
                        for kc in pair:
                            gp = kc & 3
                            fr = p.op("v", lambda e, gp=gp: e.tensor_tensor(
                                mb_5[gp][:], mb_5[gp][:], cb_5[gp][:], ALU.subtract),
                                deps=[cmh[kc]])
                            cb_war[gp] = fr
                            cmh[kc] = fr
                        for kc in pair:
                            gp = kc & 3
                            sfr = p.op("v", lambda e, gp=gp: e.tensor_tensor(
                                mb_5[gp][:], mb_5[gp][:], sg_5[gp][:], ALU.mult),
                                deps=[cmh[kc]])
                            sg_war[gp] = sfr
                            cmh[kc] = sfr
                        for kc in pair:
                            gp = kc & 3
                            rfin = p.op("v", lambda e, gp=gp: e.tensor_tensor(
                                mb_5[gp][:], yq5[gp][:], mb_5[gp][:], ALU.subtract),
                                deps=[cmh[kc]])
                            yq_war[gp] = rfin
                            iq = p.op("v", lambda e, kc=kc, blk=blk, gp=gp:
                                      e.tensor_tensor(
                                          iqT[:, kc * 512:(kc + 1) * 512], mb_5[gp][:],
                                          sif_bc[:, blk * 512:(blk + 1) * 512],
                                          ALU.mult),
                                      deps=[rfin, cf_tok[blk], iq_guard[kc]])
                            iq_tok[kc] = iq
                            mb_war[gp] = iq
                            r0_war[gp] = hh[(kc, "dd")]

                    prev = None
                    for kc0 in range(0, FFC, 2):
                        hh = head5(kc0, blk)
                        if prev is not None:
                            pk, ph = prev
                            tail5(pk, blk, ph)
                        prev = (kc0, hh)
                    pk, ph = prev
                    tail5(pk, blk, ph)
                    if blk == 0:
                        for dc in (0, 1):
                            w2_tok[dc] = p.dma("g", f"w2ld{dc}",
                                               w2_sb[dc][:].rearrange("p (c f) -> p c f", c=FFC),
                                               w2_h.ap()[dc], deps=[])
                    for dc in range(DCN):
                        wp = dc & 1
                        ypar = dc & 1
                        yld = p.dma("q", f"y1ld{ypar}", y1c[ypar][:],
                                    y1_s.ap()[dc, :, gc0:gc0 + 512],
                                    deps=[y1_war5[ypar]])
                        stop = None
                        for kc in range(FFC):
                            dd2 = [iq_tok[kc], w2_tok[wp]]
                            if kc == 0:
                                dd2.append(o5_war[wp])
                            stop = p.op("t", lambda e, kc=kc, wp=wp: e.matmul(
                                o5_ps[wp][:],
                                w2_sb[wp][:, kc * 128:(kc + 1) * 128],
                                iqT[:, kc * 512:(kc + 1) * 512],
                                start=(kc == 0), stop=(kc == FFC - 1),
                                skip_group_check=True), deps=dd2)
                            if dc == DCN - 1:
                                iq_guard[kc] = stop
                        ndc = dc + 2
                        if not (blk == MYB - 1 and ndc >= DCN):
                            nb = blk if ndc < DCN else blk + 1
                            w2_tok[wp] = p.dma("g", f"w2ld{wp}",
                                               w2_sb[wp][:].rearrange("p (c f) -> p c f", c=FFC),
                                               w2_h.ap()[ndc % DCN], deps=[stop])
                        y2 = p.op("v", lambda e, wp=wp, ypar=ypar: e.tensor_tensor(
                            y1c[ypar][:], o5_ps[wp][:], y1c[ypar][:], ALU.add),
                            deps=[stop, yld, y2_war[ypar]])
                        o5_war[wp] = y2
                        y2_war[ypar] = p.dma("q", f"yst{ypar}",
                                             y_h.ap()[dc * 128:(dc + 1) * 128,
                                                      gc0:gc0 + 512],
                                             y1c[ypar][:], deps=[y2])
                        y1_war5[ypar] = y2_war[ypar]
                p.barrier()
                p.flush()
            SB2.close()

        p.barrier()
        p.op("q", lambda e: e.nop(), deps=[], inc=None)
        p.flush()

    return nc


def _own_idx(T, j):
    MYT = T // 2
    return np.concatenate([np.arange((2 * Z + j) * 512, (2 * Z + j + 1) * 512)
                           for Z in range(MYT // 512)])


def _prep_core(inputs, b, j, T):
    x = np.asarray(inputs["L0_attn_hidden_states"][b])
    MYT = T // 2
    idx = _own_idx(T, j)
    x_own = np.ascontiguousarray(x[idx])
    xT = np.ascontiguousarray(x.T).reshape(DCN, 128, T)
    x_ownT = np.ascontiguousarray(x_own.T).reshape(DCN, 128, MYT)

    jj = np.arange(512)[None, :]
    ii = np.arange(128)[:, None]
    trils = [(jj >= ii + off) for off in (0, 128, 256, 384)]
    if j == 0:
        tiles = trils + [np.zeros((128, 512), bool)] * 4
    else:
        tiles = [np.ones((128, 512), bool)] * 4 + trils
    dmask = np.stack([t.astype(ml_dtypes.bfloat16) for t in tiles])

    qb = np.asarray(inputs["L0_attn_q_b_w"])
    cols = [c for h in range(NH) for c in range(h * 192, h * 192 + 128)] + \
           [c for h in range(NH) for c in range(h * 192 + 128, h * 192 + 192)]
    qbw = np.ascontiguousarray(qb[:, cols])
    kva = np.ascontiguousarray(np.asarray(inputs["L0_attn_kv_a_w"]))
    kvb = np.asarray(inputs["L0_attn_kv_b_w"])
    colsb = [c for h in range(NH) for c in range(h * 256, h * 256 + 128)] + \
            [c for h in range(NH) for c in range(h * 256 + 128, h * 256 + 256)]
    kvbw = np.ascontiguousarray(kvb[:, colsb])

    def tile_in(w, n_in_c):
        d_in, f = w.shape
        return np.ascontiguousarray(w.reshape(n_in_c, 128, f).transpose(1, 0, 2))

    qaw = tile_in(np.asarray(inputs["L0_attn_q_a_w"]), DCN)
    qbw_t = tile_in(qbw, QLC)
    kvaw_t = tile_in(kva, DCN)
    kvbw_t = tile_in(kvbw, KLC)

    ow = np.ascontiguousarray(
        np.asarray(inputs["L0_attn_o_w"]).reshape(DCN, 128, DCN, 128)
        .transpose(2, 1, 0, 3))
    w13 = np.asarray(inputs["L0_ffn_w13"])
    w1w = np.ascontiguousarray(
        w13[:FF].T.reshape(DCN, 128, FFC, 128).transpose(2, 1, 0, 3))
    w3w = np.ascontiguousarray(
        w13[FF:].T.reshape(DCN, 128, FFC, 128).transpose(2, 1, 0, 3))
    w2w = np.ascontiguousarray(
        np.asarray(inputs["L0_ffn_w2"]).T.reshape(FFC, 128, DCN, 128)
        .transpose(2, 1, 0, 3))

    gq = np.ascontiguousarray(np.asarray(inputs["L0_attn_q_a_ln_w"]).reshape(QLC, 128))
    gkv = np.ascontiguousarray(np.asarray(inputs["L0_attn_kv_a_ln_w"]).reshape(KLC, 128))
    gf = np.ascontiguousarray(np.asarray(inputs["L0_ffn_norm_w"]).reshape(FFC, 128))

    return {
        "xT": xT, "x_nat": np.ascontiguousarray(x), "x_own": x_own,
        "x_ownT": x_ownT, "dmask": dmask,
        "qaw": qaw, "qbw": qbw_t, "kvaw": kvaw_t, "kvbw": kvbw_t,
        "ow": ow, "w1w": w1w, "w3w": w3w, "w2w": w2w,
        "gq": gq, "gkv": gkv, "gf": gf,
    }


def kernel(**inputs):
    T = int(np.asarray(inputs["L0_attn_hidden_states"]).shape[1])
    B = int(np.asarray(inputs["L0_attn_hidden_states"]).shape[0])
    n_cores = 8
    nc = build_nc(T=T, n_cores=n_cores)
    in_maps = []
    for c in range(n_cores):
        b, j = c // 2, c % 2
        in_maps.append(_prep_core(inputs, b, j, T))
    res = run_bass_kernel_spmd(nc, in_maps, core_ids=list(range(n_cores)))
    out = np.empty((B, T, D), dtype=np.float32)
    for c in range(n_cores):
        b, j = c // 2, c % 2
        out[b, _own_idx(T, j), :] = res.results[c]["y"].T
    return out


# version 14
# speedup vs baseline: 1.0998x; 1.0650x over previous
import math
import numpy as np
import ml_dtypes
from contextlib import ExitStack

import concourse.bass as bass
import concourse.mybir as mybir
from concourse.bass_utils import run_bass_kernel_spmd

F32 = mybir.dt.float32
BF16 = mybir.dt.bfloat16
I32 = mybir.dt.int32
AF = mybir.ActivationFunctionType
ALU = mybir.AluOpType
AX = mybir.AxisListType

D = 2048
DCN = 16
QL = 1536
QLC = 12
KL = 512
KLC = 4
NH = 16
FF = 5632
FFC = 44
TB = 512
SCALE = float(np.float32(1.0) / np.float32(np.sqrt(np.float32(192.0))))
EPS_LN = 1e-5
EPS_RMS_ATTN = 1e-6
EPS_RMS_FFN = 1e-5
INV127 = 1.0 / 127.0

ENG_SEM = {"t": "pe", "v": "vs", "s": "ss", "g": "gs"}


class Plan:

    def __init__(self, nc):
        self.nc = nc
        self.pend = []
        self.sems = {}
        self.cnt = {}
        self.wm = {k: {} for k in "tvsgq"}
        for name in ("pe", "vs", "ss", "gs", "cc"):
            self._sem(name)

    def _sem(self, name):
        if name not in self.sems:
            self.sems[name] = self.nc.alloc_semaphore(name)
            self.cnt[name] = 0
        return self.sems[name]

    def _waits(self, eng, deps):
        need = {}
        for d in deps:
            if d is None:
                continue
            s, v = d
            if self.wm[eng].get(s, 0) >= v:
                continue
            if s not in ENG_SEM.values() and s != "cc":
                assert v == self.cnt[s], (
                    f"unsound DMA wait: {s} at {v}, latest {self.cnt[s]}")
            if need.get(s, 0) < v:
                need[s] = v
        items = sorted(need.items())
        attach = None
        if items:
            attach = items[-1]
            for s, v in items[:-1]:
                sem = self.sems[s]
                self.pend.append((eng, lambda e, sem=sem, v=v: e.wait_ge(sem, v)))
                self.wm[eng][s] = v
            self.wm[eng][attach[0]] = attach[1]
        return attach

    def op(self, eng, fn, deps=(), inc="auto", inc_val=1):
        attach = self._waits(eng, deps)
        sem_name = ENG_SEM.get(eng) if inc == "auto" else inc
        tok = None
        if sem_name:
            self._sem(sem_name)
            self.cnt[sem_name] += inc_val
            tok = (sem_name, self.cnt[sem_name])
        a_sem = self.sems[attach[0]] if attach else None
        a_val = attach[1] if attach else 0
        m_sem = self.sems[sem_name] if sem_name else None

        def run(e, fn=fn, a_sem=a_sem, a_val=a_val, m_sem=m_sem, iv=inc_val):
            inst = fn(e)
            if a_sem is not None:
                inst._wait_ge(a_sem, a_val)
            if m_sem is not None:
                inst.then_inc(m_sem, iv)
        self.pend.append((eng, run))
        return tok

    def latest(self, stream):
        if stream not in self.cnt or self.cnt[stream] == 0:
            return None
        return (stream, self.cnt[stream])

    def dma(self, eng, stream, out, in_, deps=()):
        self._sem(stream)
        return self.op(eng, lambda e: e.dma_start(out=out, in_=in_), deps,
                       inc=stream, inc_val=16)

    def barrier(self):
        toks = [(s, self.cnt[s]) for s in self.sems if self.cnt[s] > 0]
        for eng in "tvsgq":
            need = {}
            for s, v in toks:
                if s == ENG_SEM.get(eng):
                    continue
                if self.wm[eng].get(s, 0) >= v:
                    continue
                need[s] = v
            for s, v in sorted(need.items()):
                sem = self.sems[s]
                self.pend.append((eng, lambda e, sem=sem, v=v: e.wait_ge(sem, v)))
                self.wm[eng][s] = v

    def flush(self):
        if not self.pend:
            return
        pend, self.pend = self.pend, []
        with self.nc.Block() as block:
            for key, meth in (("t", block.tensor), ("v", block.vector),
                              ("s", block.scalar), ("g", block.gpsimd),
                              ("q", block.sync)):
                ops = [f for (k, f) in pend if k == key]
                if ops:
                    meth(lambda e, ops=ops: [f(e) for f in ops])


def build_nc(T=4096, n_cores=8):
    NTB = T // TB
    MYT = T // 2
    MYB = MYT // TB
    nc = bass.Bass("TRN2", target_bir_lowering=False)

    xT_h = nc.dram_tensor("xT", [DCN, 128, T], F32, kind="ExternalInput")
    xn_h = nc.dram_tensor("x_nat", [T, D], F32, kind="ExternalInput")
    xo_h = nc.dram_tensor("x_own", [MYT, D], F32, kind="ExternalInput")
    xoT_h = nc.dram_tensor("x_ownT", [DCN, 128, MYT], F32, kind="ExternalInput")
    dmask_h = nc.dram_tensor("dmask", [8, 128, 512], BF16, kind="ExternalInput")
    qaw_h = nc.dram_tensor("qaw", [128, DCN, QL], F32, kind="ExternalInput")
    qbw_h = nc.dram_tensor("qbw", [128, QLC, 3072], F32, kind="ExternalInput")
    kvaw_h = nc.dram_tensor("kvaw", [128, DCN, QL], F32, kind="ExternalInput")
    kvbw_h = nc.dram_tensor("kvbw", [128, KLC, 4096], F32, kind="ExternalInput")
    ow_h = nc.dram_tensor("ow", [DCN, 128, DCN, 128], F32, kind="ExternalInput")
    w1_h = nc.dram_tensor("w1w", [FFC, 128, DCN, 128], F32, kind="ExternalInput")
    w3_h = nc.dram_tensor("w3w", [FFC, 128, DCN, 128], F32, kind="ExternalInput")
    w2_h = nc.dram_tensor("w2w", [DCN, 128, FFC, 128], F32, kind="ExternalInput")
    gq_h = nc.dram_tensor("gq", [QLC, 128], F32, kind="ExternalInput")
    gkv_h = nc.dram_tensor("gkv", [KLC, 128], F32, kind="ExternalInput")
    gf_h = nc.dram_tensor("gf", [FFC, 128], F32, kind="ExternalInput")

    y_h = nc.dram_tensor("y", [D, MYT], F32, kind="ExternalOutput")

    qT_s = nc.dram_tensor("qT_s", [3072, MYT], BF16)
    kTn_s = nc.dram_tensor("kTn_s", [2048, T], BF16)
    kTr_s = nc.dram_tensor("kTr_s", [1024, T], BF16)
    v_s = nc.dram_tensor("v_s", [NH, 128, T // 128, 128], BF16)
    attnT_s = nc.dram_tensor("attnT_s", [2048, MYT], BF16)
    y1_s = nc.dram_tensor("y1_s", [DCN, 128, MYT], F32)
    gw_s = nc.dram_tensor("gw_s", [FFC, 128, MYT], BF16)

    eye_f_h = nc.inline_tensor(np.eye(128, dtype=np.float32), "eye_f")
    eye_b_h = nc.inline_tensor(np.eye(128).astype(ml_dtypes.bfloat16), "eye_b")
    onescol_h = nc.inline_tensor(np.ones((128, 1)).astype(ml_dtypes.bfloat16), "onescol")
    onesrow_h = nc.inline_tensor(np.ones((1, 128), dtype=np.float32), "onesrow")

    p = Plan(nc)

    with ExitStack() as PERS:
        ep = PERS.enter_context
        ident_f = ep(nc.sbuf_tensor("ident_f", [128, 128], F32))
        ident_b = ep(nc.sbuf_tensor("ident_b", [128, 128], BF16))
        ones_st = ep(nc.sbuf_tensor("ones_st", [128, 1], BF16))
        ones_bc = ep(nc.sbuf_tensor("ones_bc", [1, 128], F32))
        mask_sb = ep(nc.sbuf_tensor("mask_sb", [128, 8 * 512], BF16))
        gq_sb = ep(nc.sbuf_tensor("gq_sb", [128, QLC], F32))
        gkv_sb = ep(nc.sbuf_tensor("gkv_sb", [128, KLC], F32))
        gf_sb = ep(nc.sbuf_tensor("gf_sb", [128, FFC], F32))

        p.dma("q", "ini", ident_f[:], eye_f_h.ap())
        p.dma("q", "ini", ident_b[:], eye_b_h.ap())
        p.dma("q", "ini", ones_st[:], onescol_h.ap())
        p.dma("q", "ini", ones_bc[:], onesrow_h.ap())
        p.dma("q", "ini", mask_sb[:].rearrange("p (m c) -> p m c", m=8),
              dmask_h.ap().rearrange("m p c -> p m c"))

        with ExitStack() as P0:
            e0 = P0.enter_context
            tp_ps = e0(nc.psum_tensor("tp0_ps", [128, 128], F32))
            gtmp = e0(nc.sbuf_tensor("gtmp", [FFC, 128], F32))
            p.barrier()
            gtoks = []
            prev = None
            for gh, gsb, n in ((gq_h, gq_sb, QLC), (gkv_h, gkv_sb, KLC),
                               (gf_h, gf_sb, FFC)):
                tok = p.dma("q", f"gld{n}", gtmp[0:n, :], gh.ap(), deps=[prev])
                tt = p.op("t", lambda e, n=n: e.transpose(
                    tp_ps[:, 0:n], gtmp[0:n, :], ident_f[0:n, 0:n]),
                    deps=[tok, prev])
                cp = p.op("v", lambda e, gsb=gsb, n=n: e.tensor_copy(
                    gsb[:, 0:n], tp_ps[:, 0:n]), deps=[tt])
                gtoks.append(cp)
                prev = cp
            g_q, g_kv, g_f = gtoks
            p.barrier()
            p.flush()

        with ExitStack() as P1AO:
            eo = P1AO.enter_context
            qlatN = eo(nc.sbuf_tensor("qlatN", [128, QLC * MYT], BF16))
            qn_tok = [[None] * QLC for _ in range(MYB)]

            with ExitStack() as P1A:
                e1 = P1A.enter_context
                tp_ps = e1(nc.psum_tensor("tp1_ps", [128, 128], F32))
                bc_ps = e1(nc.psum_tensor("bc1_ps", [128, 512], F32))
                mm_ps = [e1(nc.psum_tensor(f"mm1_ps{i}", [128, 512], F32)) for i in range(2)]
                stat_ps = e1(nc.psum_tensor("st1_ps", [1, 512], F32))
                qaw_sb = e1(nc.sbuf_tensor("qaw_sb", [128, DCN * QL], BF16))
                xn_t = [e1(nc.sbuf_tensor(f"xn_t{i}", [128, D], F32)) for i in range(2)]
                x2_t = e1(nc.sbuf_tensor("x2_t", [128, D], F32))
                st_sc = e1(nc.sbuf_tensor("st_sc", [128, 8], F32))
                rows = e1(nc.sbuf_tensor("rows", [1, 1024], F32))
                mu_bc = e1(nc.sbuf_tensor("mu_bc", [128, 512], F32))
                rs_bc = e1(nc.sbuf_tensor("rs_bc", [128, 512], F32))
                xt_c = [e1(nc.sbuf_tensor(f"xt_c{i}", [128, 512], F32)) for i in range(3)]
                tmp_c = e1(nc.sbuf_tensor("tmp_c", [128, 512], F32))
                lnT = [e1(nc.sbuf_tensor("lnT0", [128, DCN * 512], BF16))] * 2
                qlat = e1(nc.sbuf_tensor("qlat", [128, QLC * 512], F32))
                sq_c = [e1(nc.sbuf_tensor(f"sq_c{i}", [128, 512], BF16)) for i in range(2)]
                qvar = e1(nc.sbuf_tensor("qvar", [1, 512], F32))
                qrs = e1(nc.sbuf_tensor("qrs", [1, 512], F32))
                rsq_bc = e1(nc.sbuf_tensor("rsq_bc", [128, 512], F32))

                p.dma("g", "w1a", qaw_sb[:].rearrange("p (c f) -> p c f", c=DCN),
                      qaw_h.ap())
                t_w1a = p.latest("w1a")

                state = dict(xn_war=[None, None], x2_war=None, xt_war=[None] * 3,
                             tp_war=g_f, bc_war=None, mu_war=None, rows_war=None)
                ln_war = [None, None]
                ln_tok = [[None] * DCN, [None] * DCN]
                mm_war = [None, None]
                stat_war = [None, None]

                def ln_block(tb, par, c0, src2d_h, srcT_h, tag, st):
                    for ti in range(4):
                        tpar = ti & 1
                        gt = 4 * tb + ti
                        ld = p.dma("q", f"xn{tpar}{tag}", xn_t[tpar][:],
                                   src2d_h.ap()[gt * 128:(gt + 1) * 128, :],
                                   deps=(st["xn_war"][tpar] or []))
                        s1 = p.op("v", lambda e, b=tpar, ti=ti: e.reduce_sum(
                            st_sc[:, ti:ti + 1], xn_t[b][:], AX.X), deps=[ld])
                        x2 = p.op("g", lambda e, b=tpar: e.tensor_tensor(
                            x2_t[:], xn_t[b][:], xn_t[b][:], ALU.mult),
                            deps=[ld, st["x2_war"]])
                        s2 = p.op("v", lambda e: e.reduce_sum(
                            st_sc[:, 4:5], x2_t[:], AX.X), deps=[x2])
                        st["x2_war"] = s2
                        st["xn_war"][tpar] = [s1, x2]
                        mu = p.op("v", lambda e, ti=ti: e.tensor_scalar_mul(
                            st_sc[:, ti:ti + 1], st_sc[:, ti:ti + 1], 1.0 / D),
                            deps=[s1])
                        m2 = p.op("v", lambda e, ti=ti: e.tensor_tensor(
                            st_sc[:, 6:7], st_sc[:, ti:ti + 1], st_sc[:, ti:ti + 1],
                            ALU.mult), deps=[mu])
                        var = p.op("v", lambda e: e.tensor_scalar(
                            st_sc[:, 4:5], st_sc[:, 4:5], 1.0 / D, EPS_LN,
                            ALU.mult, ALU.add), deps=[s2])
                        var2 = p.op("v", lambda e: e.tensor_tensor(
                            st_sc[:, 5:6], st_sc[:, 4:5], st_sc[:, 6:7], ALU.subtract),
                            deps=[var, m2])
                        lg = p.op("s", lambda e: e.activation(
                            st_sc[:, 7:8], st_sc[:, 5:6], AF.Ln), deps=[var2])
                        rstd = p.op("s", lambda e: e.activation(
                            st_sc[:, 5:6], st_sc[:, 7:8], AF.Exp, scale=-0.5),
                            deps=[lg])
                        tt1 = p.op("t", lambda e, ti=ti: e.transpose(
                            tp_ps[0:1, :], st_sc[:, ti:ti + 1], ident_f[:]),
                            deps=[mu, m2, st["tp_war"]])
                        cp1 = p.op("v", lambda e, ti=ti: e.tensor_copy(
                            rows[0:1, ti * 128:(ti + 1) * 128], tp_ps[0:1, :]),
                            deps=[tt1, st["rows_war"]])
                        tt2 = p.op("t", lambda e: e.transpose(
                            tp_ps[0:1, :], st_sc[:, 5:6], ident_f[:]),
                            deps=[rstd, cp1])
                        cp2 = p.op("v", lambda e, ti=ti: e.tensor_copy(
                            rows[0:1, 512 + ti * 128:512 + (ti + 1) * 128],
                            tp_ps[0:1, :]), deps=[tt2])
                        st["tp_war"] = cp2
                    b1 = p.op("t", lambda e: e.matmul(bc_ps[:], ones_bc[:],
                                                      rows[0:1, 0:512], start=True,
                                                      stop=True, skip_group_check=True),
                              deps=[st["tp_war"], st["bc_war"]])
                    mub = p.op("v", lambda e: e.tensor_copy(mu_bc[:], bc_ps[:]),
                               deps=[b1, st["mu_war"]])
                    b2 = p.op("t", lambda e: e.matmul(bc_ps[:], ones_bc[:],
                                                      rows[0:1, 512:1024], start=True,
                                                      stop=True, skip_group_check=True),
                              deps=[mub])
                    rsb = p.op("v", lambda e: e.tensor_copy(rs_bc[:], bc_ps[:]),
                               deps=[b2, st["mu_war"]])
                    st["bc_war"] = rsb
                    st["rows_war"] = b2
                    t2 = st.get("tmp_war")
                    for dc in range(DCN):
                        cr = dc % 3
                        xt = p.dma("q", f"xt{cr}{tag}", xt_c[cr][:],
                                   srcT_h.ap()[dc, :, c0:c0 + TB],
                                   deps=[st["xt_war"][cr]])
                        t1 = p.op("v", lambda e, cr=cr: e.tensor_tensor(
                            tmp_c[:], xt_c[cr][:], mu_bc[:], ALU.subtract),
                            deps=[xt, mub, t2])
                        st["xt_war"][cr] = t1
                        t2 = p.op("v", lambda e, dc=dc, par=par: e.tensor_tensor(
                            lnT[par][:, dc * 512:(dc + 1) * 512], tmp_c[:], rs_bc[:],
                            ALU.mult), deps=[t1, rsb, ln_war[par]])
                        ln_tok[par][dc] = t2
                    st["mu_war"] = t2
                    st["tmp_war"] = t2

                for tb in range(MYB):
                    par = tb & 1
                    c0 = tb * TB
                    ln_block(tb, par, c0, xo_h, xoT_h, "a", state)
                    qlat_tok = [None] * QLC
                    sq_war = stat_war
                    last_stop = None
                    stop = None
                    for fc in range(QLC):
                        bank = fc & 1
                        for dc in range(DCN):
                            dd = [ln_tok[par][dc], t_w1a]
                            if dc == 0:
                                dd.append(mm_war[bank])
                            stop = p.op("t", lambda e, fc=fc, dc=dc, bank=bank, par=par:
                                        e.matmul(mm_ps[bank][:],
                                                 qaw_sb[:, dc * QL + fc * 128:dc * QL + (fc + 1) * 128],
                                                 lnT[par][:, dc * 512:(dc + 1) * 512],
                                                 start=(dc == 0), stop=(dc == DCN - 1),
                                                 skip_group_check=True), deps=dd)
                        ev = p.op("v", lambda e, fc=fc, bank=bank: e.tensor_copy(
                            qlat[:, fc * 512:(fc + 1) * 512], mm_ps[bank][:]),
                            deps=[stop])
                        qlat_tok[fc] = ev
                        mm_war[bank] = ev
                        sp = fc & 1
                        sqt = p.op("v", lambda e, fc=fc, sp=sp: e.tensor_tensor(
                            sq_c[sp][:], qlat[:, fc * 512:(fc + 1) * 512],
                            qlat[:, fc * 512:(fc + 1) * 512], ALU.mult),
                            deps=[ev, sq_war[sp]])
                        last_stop = p.op("t", lambda e, fc=fc, sp=sp: e.matmul(
                            stat_ps[:], ones_st[:], sq_c[sp][:],
                            start=(fc == 0), stop=(fc == QLC - 1),
                            skip_group_check=True), deps=[sqt])
                        sq_war[sp] = last_stop
                    stat_war = sq_war
                    qv = p.op("v", lambda e: e.tensor_scalar(
                        qvar[:], stat_ps[:], 1.0 / QL, EPS_RMS_ATTN, ALU.mult,
                        ALU.add), deps=[last_stop])
                    ql1 = p.op("s", lambda e: e.activation(qrs[:], qvar[:], AF.Ln),
                               deps=[qv])
                    ql2 = p.op("s", lambda e: e.activation(qrs[:], qrs[:], AF.Exp,
                                                           scale=-0.5), deps=[ql1])
                    b3 = p.op("t", lambda e: e.matmul(bc_ps[:], ones_bc[:], qrs[:],
                                                      start=True, stop=True,
                                                      skip_group_check=True),
                              deps=[ql2, state["bc_war"]])
                    rqb = p.op("v", lambda e: e.tensor_copy(rsq_bc[:], bc_ps[:]),
                               deps=[b3])
                    state["bc_war"] = rqb
                    for fc in range(QLC):
                        qn_tok[tb][fc] = p.op("v", lambda e, fc=fc, tb=tb, c0=c0:
                                              e.scalar_tensor_tensor(
                                                  qlatN[:, fc * MYT + c0:fc * MYT + c0 + 512],
                                                  qlat[:, fc * 512:(fc + 1) * 512],
                                                  gq_sb[:, fc:fc + 1], rsq_bc[:],
                                                  ALU.mult, ALU.mult),
                                              deps=[qlat_tok[fc], rqb, g_q])
                    ln_war[par] = stop
                p.barrier()
                p.flush()

            with ExitStack() as P1A2:
                e1 = P1A2.enter_context
                mm_ps = [e1(nc.psum_tensor(f"mm1b_ps{i}", [128, 512], F32))
                         for i in range(2)]
                qbw_sb = e1(nc.sbuf_tensor("qbw_sb", [128, QLC * 3072], BF16))
                qstage = [e1(nc.sbuf_tensor(f"qstage{i}", [128, 512], BF16))
                          for i in range(2)]
                p.dma("g", "w1a2", qbw_sb[:].rearrange("p (c f) -> p c f", c=QLC),
                      qbw_h.ap())
                t_w1a2 = p.latest("w1a2")
                qs_war = [None, None]
                mm_war = [None, None]
                for tb in range(MYB):
                    c0 = tb * TB
                    for fc2 in range(24):
                        bank = fc2 & 1
                        stop = None
                        for kc in range(QLC):
                            dd = [qn_tok[tb][kc], t_w1a2]
                            if kc == 0:
                                dd.append(mm_war[bank])
                            stop = p.op("t", lambda e, fc2=fc2, kc=kc, bank=bank, c0=c0:
                                        e.matmul(mm_ps[bank][:],
                                                 qbw_sb[:, kc * 3072 + fc2 * 128:kc * 3072 + (fc2 + 1) * 128],
                                                 qlatN[:, kc * MYT + c0:kc * MYT + c0 + 512],
                                                 start=(kc == 0), stop=(kc == QLC - 1),
                                                 skip_group_check=True), deps=dd)
                        ev = p.op("v", lambda e, bank=bank: e.tensor_scalar_mul(
                            qstage[bank][:], mm_ps[bank][:], SCALE),
                            deps=[stop, qs_war[bank]])
                        mm_war[bank] = ev
                        qs_war[bank] = p.dma("q", f"qst{bank}",
                                             qT_s.ap()[fc2 * 128:(fc2 + 1) * 128,
                                                       c0:c0 + TB],
                                             qstage[bank][:], deps=[ev])
                p.barrier()
                p.flush()

        with ExitStack() as P1B:
            e1 = P1B.enter_context
            tp_ps = e1(nc.psum_tensor("tp2_ps", [128, 128], F32))
            bc_ps = e1(nc.psum_tensor("bc2_ps", [128, 512], F32))
            mm_ps = [e1(nc.psum_tensor(f"mm2_ps{i}", [128, 512], F32)) for i in range(2)]
            v_ps = [e1(nc.psum_tensor(f"v_ps{i}", [128, 512], F32)) for i in range(2)]
            stat_ps = e1(nc.psum_tensor("st2_ps", [1, 512], F32))
            kvaw_sb = e1(nc.sbuf_tensor("kvaw_sb", [128, DCN * QL], BF16))
            kvbw_sb = e1(nc.sbuf_tensor("kvbw_sb", [128, KLC * 4096], BF16))
            xn_t = [e1(nc.sbuf_tensor(f"xnb_t{i}", [128, D], F32)) for i in range(2)]
            x2_t = e1(nc.sbuf_tensor("x2b_t", [128, D], F32))
            st_sc = e1(nc.sbuf_tensor("stb_sc", [128, 8], F32))
            rows = e1(nc.sbuf_tensor("rowsb", [1, 1024], F32))
            mu_bc = e1(nc.sbuf_tensor("mub_bc", [128, 512], F32))
            rs_bc = e1(nc.sbuf_tensor("rsb_bc", [128, 512], F32))
            xt_c = [e1(nc.sbuf_tensor(f"xtb_c{i}", [128, 512], F32)) for i in range(3)]
            tmp_c = e1(nc.sbuf_tensor("tmpb_c", [128, 512], F32))
            lnT = [e1(nc.sbuf_tensor(f"lnTb{i}", [128, DCN * 512], BF16)) for i in range(2)]
            klat = e1(nc.sbuf_tensor("klat", [128, KLC * 512], F32))
            sq_c = [e1(nc.sbuf_tensor(f"sqb_c{i}", [128, 512], BF16)) for i in range(2)]
            kvar = e1(nc.sbuf_tensor("kvar", [1, 512], F32))
            krs = e1(nc.sbuf_tensor("krs", [1, 512], F32))
            rsk_bc = e1(nc.sbuf_tensor("rsk_bc", [128, 512], F32))
            klatN = e1(nc.sbuf_tensor("klatN", [128, KLC * 512], BF16))
            kstage = [e1(nc.sbuf_tensor(f"kstage{i}", [128, 512], BF16)) for i in range(2)]
            vstage = [e1(nc.sbuf_tensor(f"vstage{i}", [128, 2048], BF16)) for i in range(2)]

            p.dma("g", "w1b", kvaw_sb[:].rearrange("p (c f) -> p c f", c=DCN),
                  kvaw_h.ap())
            t_w1b = p.dma("g", "w1b", kvbw_sb[:].rearrange("p (c f) -> p c f", c=KLC),
                          kvbw_h.ap())

            state = dict(xn_war=[None, None], x2_war=None, xt_war=[None] * 3,
                         tp_war=None, bc_war=None, mu_war=None, rows_war=None)
            ln_war = [None, None]
            ln_tok = [[None] * DCN, [None] * DCN]
            mm_war = [None, None]
            kst_war = [None, None]
            vst_war = [None, None]
            vps_war = [None, None]
            stat_war = [None, None]

            def ln_block_b(tb, par, c0, st):
                for ti in range(4):
                    tpar = ti & 1
                    gt = 4 * tb + ti
                    ld = p.dma("q", f"xnb{tpar}", xn_t[tpar][:],
                               xn_h.ap()[gt * 128:(gt + 1) * 128, :],
                               deps=(st["xn_war"][tpar] or []))
                    s1 = p.op("v", lambda e, b=tpar, ti=ti: e.reduce_sum(
                        st_sc[:, ti:ti + 1], xn_t[b][:], AX.X), deps=[ld])
                    x2 = p.op("g", lambda e, b=tpar: e.tensor_tensor(
                        x2_t[:], xn_t[b][:], xn_t[b][:], ALU.mult),
                        deps=[ld, st["x2_war"]])
                    s2 = p.op("v", lambda e: e.reduce_sum(
                        st_sc[:, 4:5], x2_t[:], AX.X), deps=[x2])
                    st["x2_war"] = s2
                    st["xn_war"][tpar] = [s1, x2]
                    mu = p.op("v", lambda e, ti=ti: e.tensor_scalar_mul(
                        st_sc[:, ti:ti + 1], st_sc[:, ti:ti + 1], 1.0 / D), deps=[s1])
                    m2 = p.op("v", lambda e, ti=ti: e.tensor_tensor(
                        st_sc[:, 6:7], st_sc[:, ti:ti + 1], st_sc[:, ti:ti + 1],
                        ALU.mult), deps=[mu])
                    var = p.op("v", lambda e: e.tensor_scalar(
                        st_sc[:, 4:5], st_sc[:, 4:5], 1.0 / D, EPS_LN,
                        ALU.mult, ALU.add), deps=[s2])
                    var2 = p.op("v", lambda e: e.tensor_tensor(
                        st_sc[:, 5:6], st_sc[:, 4:5], st_sc[:, 6:7], ALU.subtract),
                        deps=[var, m2])
                    lg = p.op("s", lambda e: e.activation(
                        st_sc[:, 7:8], st_sc[:, 5:6], AF.Ln), deps=[var2])
                    rstd = p.op("s", lambda e: e.activation(
                        st_sc[:, 5:6], st_sc[:, 7:8], AF.Exp, scale=-0.5), deps=[lg])
                    tt1 = p.op("t", lambda e, ti=ti: e.transpose(
                        tp_ps[0:1, :], st_sc[:, ti:ti + 1], ident_f[:]),
                        deps=[mu, m2, st["tp_war"]])
                    cp1 = p.op("v", lambda e, ti=ti: e.tensor_copy(
                        rows[0:1, ti * 128:(ti + 1) * 128], tp_ps[0:1, :]),
                        deps=[tt1, st["rows_war"]])
                    tt2 = p.op("t", lambda e: e.transpose(
                        tp_ps[0:1, :], st_sc[:, 5:6], ident_f[:]), deps=[rstd, cp1])
                    cp2 = p.op("v", lambda e, ti=ti: e.tensor_copy(
                        rows[0:1, 512 + ti * 128:512 + (ti + 1) * 128],
                        tp_ps[0:1, :]), deps=[tt2])
                    st["tp_war"] = cp2
                b1 = p.op("t", lambda e: e.matmul(bc_ps[:], ones_bc[:],
                                                  rows[0:1, 0:512], start=True,
                                                  stop=True, skip_group_check=True),
                          deps=[st["tp_war"], st["bc_war"]])
                mub = p.op("v", lambda e: e.tensor_copy(mu_bc[:], bc_ps[:]),
                           deps=[b1, st["mu_war"]])
                b2 = p.op("t", lambda e: e.matmul(bc_ps[:], ones_bc[:],
                                                  rows[0:1, 512:1024], start=True,
                                                  stop=True, skip_group_check=True),
                          deps=[mub])
                rsb = p.op("v", lambda e: e.tensor_copy(rs_bc[:], bc_ps[:]),
                           deps=[b2, st["mu_war"]])
                st["bc_war"] = rsb
                st["rows_war"] = b2
                t2 = st.get("tmp_war")
                for dc in range(DCN):
                    cr = dc % 3
                    xt = p.dma("q", f"xtb{cr}", xt_c[cr][:],
                               xT_h.ap()[dc, :, c0:c0 + TB], deps=[st["xt_war"][cr]])
                    t1 = p.op("v", lambda e, cr=cr: e.tensor_tensor(
                        tmp_c[:], xt_c[cr][:], mu_bc[:], ALU.subtract),
                        deps=[xt, mub, t2])
                    st["xt_war"][cr] = t1
                    t2 = p.op("v", lambda e, dc=dc, par=par: e.tensor_tensor(
                        lnT[par][:, dc * 512:(dc + 1) * 512], tmp_c[:], rs_bc[:],
                        ALU.mult), deps=[t1, rsb, ln_war[par]])
                    ln_tok[par][dc] = t2
                st["mu_war"] = t2
                st["tmp_war"] = t2

            for tb in range(NTB):
                par = tb & 1
                c0 = tb * TB
                ln_block_b(tb, par, c0, state)
                klat_tok = [None] * KLC
                sq_war = stat_war
                last_stop = None
                last_mm = None
                for fc in range(QLC):
                    bank = fc & 1
                    stop = None
                    for dc in range(DCN):
                        dd = [ln_tok[par][dc], t_w1b]
                        if dc == 0:
                            dd.append(mm_war[bank])
                        stop = p.op("t", lambda e, fc=fc, dc=dc, bank=bank, par=par:
                                    e.matmul(mm_ps[bank][:],
                                             kvaw_sb[:, dc * QL + fc * 128:dc * QL + (fc + 1) * 128],
                                             lnT[par][:, dc * 512:(dc + 1) * 512],
                                             start=(dc == 0), stop=(dc == DCN - 1),
                                             skip_group_check=True), deps=dd)
                    last_mm = stop
                    if fc < KLC:
                        ev = p.op("v", lambda e, fc=fc, bank=bank: e.tensor_copy(
                            klat[:, fc * 512:(fc + 1) * 512], mm_ps[bank][:]),
                            deps=[stop])
                        klat_tok[fc] = ev
                        mm_war[bank] = ev
                        sp = fc & 1
                        sqt = p.op("v", lambda e, fc=fc, sp=sp: e.tensor_tensor(
                            sq_c[sp][:], klat[:, fc * 512:(fc + 1) * 512],
                            klat[:, fc * 512:(fc + 1) * 512], ALU.mult),
                            deps=[ev, sq_war[sp]])
                        last_stop = p.op("t", lambda e, fc=fc, sp=sp: e.matmul(
                            stat_ps[:], ones_st[:], sq_c[sp][:],
                            start=(fc == 0), stop=(fc == KLC - 1),
                            skip_group_check=True), deps=[sqt])
                        sq_war[sp] = last_stop
                    else:
                        sp = fc & 1
                        ev = p.op("v", lambda e, bank=bank, sp=sp: e.tensor_copy(
                            kstage[sp][:], mm_ps[bank][:]), deps=[stop, kst_war[sp]])
                        mm_war[bank] = ev
                        kst_war[sp] = p.dma("q", f"kst{sp}",
                                            kTr_s.ap()[(fc - KLC) * 128:(fc - 3) * 128,
                                                       c0:c0 + TB],
                                            kstage[sp][:], deps=[ev])
                stat_war = sq_war
                kv1 = p.op("v", lambda e: e.tensor_scalar(
                    kvar[:], stat_ps[:], 1.0 / KL, EPS_RMS_ATTN, ALU.mult, ALU.add),
                    deps=[last_stop])
                kl1 = p.op("s", lambda e: e.activation(krs[:], kvar[:], AF.Ln), deps=[kv1])
                kl2 = p.op("s", lambda e: e.activation(krs[:], krs[:], AF.Exp,
                                                       scale=-0.5), deps=[kl1])
                b4 = p.op("t", lambda e: e.matmul(bc_ps[:], ones_bc[:], krs[:],
                                                  start=True, stop=True,
                                                  skip_group_check=True),
                          deps=[kl2, state["bc_war"]])
                rkb = p.op("v", lambda e: e.tensor_copy(rsk_bc[:], bc_ps[:]), deps=[b4])
                state["bc_war"] = rkb
                kn_tok = [None] * KLC
                for fc in range(KLC):
                    kn_tok[fc] = p.op("v", lambda e, fc=fc: e.scalar_tensor_tensor(
                        klatN[:, fc * 512:(fc + 1) * 512],
                        klat[:, fc * 512:(fc + 1) * 512],
                        gkv_sb[:, fc:fc + 1], rsk_bc[:], ALU.mult, ALU.mult),
                        deps=[klat_tok[fc], rkb, g_kv])
                for fc2 in range(NH):
                    bank = fc2 & 1
                    stop = None
                    for kc in range(KLC):
                        dd = [kn_tok[kc], t_w1b]
                        if kc == 0:
                            dd.append(mm_war[bank])
                        stop = p.op("t", lambda e, fc2=fc2, kc=kc, bank=bank:
                                    e.matmul(mm_ps[bank][:],
                                             kvbw_sb[:, kc * 4096 + fc2 * 128:kc * 4096 + (fc2 + 1) * 128],
                                             klatN[:, kc * 512:(kc + 1) * 512],
                                             start=(kc == 0), stop=(kc == KLC - 1),
                                             skip_group_check=True), deps=dd)
                    sp = fc2 & 1
                    ev = p.op("v", lambda e, sp=sp, bank=bank: e.tensor_copy(
                        kstage[sp][:], mm_ps[bank][:]), deps=[stop, kst_war[sp]])
                    mm_war[bank] = ev
                    kst_war[sp] = p.dma("q", f"kst{sp}",
                                        kTn_s.ap()[fc2 * 128:(fc2 + 1) * 128, c0:c0 + TB],
                                        kstage[sp][:], deps=[ev])
                for ts in range(4):
                    vp = ts & 1
                    gchunk = 4 * tb + ts
                    evs = []
                    for half in range(4):
                        stop = None
                        for kc in range(KLC):
                            dd = [kn_tok[kc]]
                            if kc == 0:
                                dd.append(vps_war[vp])
                            stop = p.op("t", lambda e, ts=ts, kc=kc, half=half, vp=vp:
                                        e.matmul(v_ps[vp][:],
                                                 klatN[:, kc * 512 + ts * 128:kc * 512 + (ts + 1) * 128],
                                                 kvbw_sb[:, kc * 4096 + 2048 + half * 512:
                                                         kc * 4096 + 2048 + (half + 1) * 512],
                                                 start=(kc == 0), stop=(kc == KLC - 1),
                                                 skip_group_check=True), deps=dd)
                        ev = p.op("v", lambda e, vp=vp, half=half: e.tensor_copy(
                            vstage[vp][:, half * 512:(half + 1) * 512], v_ps[vp][:]),
                            deps=([stop, vst_war[vp]] if half == 0 else [stop]))
                        vps_war[vp] = ev
                        evs.append(ev)
                    vst_war[vp] = p.dma("q", f"vst{vp}",
                                        v_s.ap()[:, :, gchunk, :].rearrange("h p d -> p h d"),
                                        vstage[vp][:].rearrange("p (h d) -> p h d", h=NH),
                                        deps=evs)
                ln_war[par] = last_mm
            p.barrier()
            p.flush()

        with ExitStack() as P2:
            e2 = P2.enter_context
            sc_ps = [e2(nc.psum_tensor(f"sc_ps{i}", [128, 512], F32)) for i in range(2)]
            av_ps = e2(nc.psum_tensor("av_ps", [128, 512], F32))
            r_ps = e2(nc.psum_tensor("r_ps", [1, 512], F32))
            s2_ps = e2(nc.psum_tensor("s2_ps", [1, 512], F32))
            bc_ps = e2(nc.psum_tensor("bc3_ps", [128, 512], F32))
            kTn = [e2(nc.sbuf_tensor(f"kTn{i}", [128, T], BF16)) for i in range(2)]
            kTr = [e2(nc.sbuf_tensor(f"kTr{i}", [64, T], BF16)) for i in range(2)]
            vh = [e2(nc.sbuf_tensor(f"vh{i}", [128, T], BF16)) for i in range(2)]
            qTn = [e2(nc.sbuf_tensor(f"qTn{i}", [128, MYT], BF16)) for i in range(2)]
            qTr = [e2(nc.sbuf_tensor(f"qTr{i}", [64, MYT], BF16)) for i in range(2)]
            sqn = e2(nc.sbuf_tensor("sqn", [128, T], BF16))
            sqr = e2(nc.sbuf_tensor("sqr", [64, T], BF16))
            k2row = e2(nc.sbuf_tensor("k2row", [1, T], F32))
            q2row = e2(nc.sbuf_tensor("q2row", [1, MYT], F32))
            k2m = e2(nc.sbuf_tensor("k2m", [1, 1], F32))
            q2m = e2(nc.sbuf_tensor("q2m", [1, MYB], F32))
            ncg = e2(nc.sbuf_tensor("ncg", [1, MYB], F32))
            negc_bc = e2(nc.sbuf_tensor("negc_bc", [128, 1], F32))
            probs = [e2(nc.sbuf_tensor(f"probs{i}", [128, 512], BF16)) for i in range(4)]
            rrow = e2(nc.sbuf_tensor("rrow", [1, 512], F32))
            rinv = e2(nc.sbuf_tensor("rinv", [1, 512], F32))
            rinv_bc = e2(nc.sbuf_tensor("rinv_bc", [128, 512], F32))
            astage = [e2(nc.sbuf_tensor(f"astage{i}", [128, 512], BF16)) for i in range(2)]

            ld = [[None] * 5, [None] * 5]
            buf_war = [[None] * 5, [None] * 5]

            def load_head(h):
                par = h & 1
                ld[par][0] = p.dma("q", f"hld{par}", kTn[par][:],
                                   kTn_s.ap()[h * 128:(h + 1) * 128, :],
                                   deps=[buf_war[par][0]])
                ld[par][1] = p.dma("s", f"hlds{par}", kTr[par][:],
                                   kTr_s.ap()[h * 64:(h + 1) * 64, :],
                                   deps=[buf_war[par][1]])
                ld[par][2] = p.dma("q", f"hld{par}", vh[par][:],
                                   v_s.ap()[h].rearrange("p c d -> p (c d)"),
                                   deps=[buf_war[par][2]])
                ld[par][3] = p.dma("s", f"hlds{par}", qTn[par][:],
                                   qT_s.ap()[h * 128:(h + 1) * 128, :],
                                   deps=[buf_war[par][3]])
                ld[par][4] = p.dma("s", f"hlds{par}", qTr[par][:],
                                   qT_s.ap()[2048 + h * 64:2048 + (h + 1) * 64, :],
                                   deps=[buf_war[par][4]])
                ld[par][0] = ld[par][2]
                ld[par][1] = ld[par][4]
                ld[par][3] = ld[par][4]

            load_head(0)
            sq_war = None
            ps2_war = None
            ncg_war = None
            negc_war = None
            probs_war = [None] * 4
            sc_war = [None, None]
            av_war = None
            r_war = None
            bc_war = None
            ast_war = [None, None]
            for h in range(NH):
                par = h & 1
                if h + 1 < NH:
                    load_head(h + 1)
                sn = p.op("v", lambda e, par=par: e.tensor_tensor(
                    sqn[:], kTn[par][:], kTn[par][:], ALU.mult),
                    deps=[ld[par][0], sq_war])
                sr = p.op("v", lambda e, par=par: e.tensor_tensor(
                    sqr[:], kTr[par][:], kTr[par][:], ALU.mult),
                    deps=[ld[par][1], sq_war])
                kc_toks = []
                for c8 in range(T // 512):
                    m1 = p.op("t", lambda e, c8=c8: e.matmul(
                        s2_ps[:], ones_st[:], sqn[:, c8 * 512:(c8 + 1) * 512],
                        start=True, stop=False, skip_group_check=True),
                        deps=[sn, ps2_war])
                    m2 = p.op("t", lambda e, c8=c8: e.matmul(
                        s2_ps[:], ones_st[0:64, :], sqr[:, c8 * 512:(c8 + 1) * 512],
                        start=False, stop=True, skip_group_check=True), deps=[m1, sr])
                    cp = p.op("v", lambda e, c8=c8: e.tensor_copy(
                        k2row[:, c8 * 512:(c8 + 1) * 512], s2_ps[:]), deps=[m2])
                    ps2_war = cp
                    kc_toks.append(cp)
                k2t = p.op("v", lambda e: e.reduce_max(k2m[:], k2row[:], AX.X),
                           deps=kc_toks)
                sn2 = p.op("v", lambda e, par=par: e.tensor_tensor(
                    sqn[:, 0:MYT], qTn[par][:], qTn[par][:], ALU.mult),
                    deps=[ld[par][3]] + kc_toks)
                sr2 = p.op("v", lambda e, par=par: e.tensor_tensor(
                    sqr[:, 0:MYT], qTr[par][:], qTr[par][:], ALU.mult),
                    deps=[ld[par][4]] + kc_toks)
                qc_toks = []
                for c8 in range(MYT // 512):
                    m1 = p.op("t", lambda e, c8=c8: e.matmul(
                        s2_ps[:], ones_st[:], sqn[:, c8 * 512:(c8 + 1) * 512],
                        start=True, stop=False, skip_group_check=True),
                        deps=[sn2, ps2_war])
                    m2 = p.op("t", lambda e, c8=c8: e.matmul(
                        s2_ps[:], ones_st[0:64, :], sqr[:, c8 * 512:(c8 + 1) * 512],
                        start=False, stop=True, skip_group_check=True), deps=[m1, sr2])
                    cp = p.op("v", lambda e, c8=c8: e.tensor_copy(
                        q2row[:, c8 * 512:(c8 + 1) * 512], s2_ps[:]), deps=[m2])
                    ps2_war = cp
                    qc_toks.append(cp)
                sq_war = sr2
                qm = p.op("v", lambda e: e.reduce_max(
                    q2m[:], q2row[:].rearrange("o (q f) -> o q f", q=MYB), AX.X),
                    deps=qc_toks)
                pr = p.op("v", lambda e: e.tensor_single_scalar(
                    ncg[:], q2m[:], k2m[:], ALU.mult), deps=[qm, k2t, ncg_war])
                cl = p.op("s", lambda e: e.activation(ncg[:], ncg[:], AF.Ln), deps=[pr])
                ce = p.op("s", lambda e: e.activation(ncg[:], ncg[:], AF.Exp, scale=0.5),
                          deps=[cl])
                ngc = p.op("v", lambda e: e.tensor_scalar(
                    ncg[:], ncg[:], -1.0, -1.0, ALU.mult, ALU.add), deps=[ce])
                for Qi in range(MYB):
                    ncch = 8 * (Qi + 1)
                    qc0 = Qi * 512
                    bq = p.op("t", lambda e, Qi=Qi: e.matmul(
                        bc_ps[:, 0:1], ones_bc[:], ncg[0:1, Qi:Qi + 1],
                        start=True, stop=True, skip_group_check=True),
                        deps=[ngc, bc_war])
                    ncg_war = bq
                    nbc = p.op("v", lambda e: e.tensor_copy(negc_bc[:], bc_ps[:, 0:1]),
                               deps=[bq, negc_war])
                    bc_war = nbc
                    av_stop = None
                    r_stop = None
                    ex = None
                    mm1 = mm2 = None
                    pend_av = None
                    for ci in range(ncch + 1):
                      if ci < ncch:
                        bank = ci & 1
                        slot = ci & 3
                        mm1 = p.op("t", lambda e, ci=ci, bank=bank, par=par, qc0=qc0:
                                   e.matmul(sc_ps[bank][:],
                                            kTn[par][:, ci * 128:(ci + 1) * 128],
                                            qTn[par][:, qc0:qc0 + 512],
                                            start=True, stop=False,
                                            skip_group_check=True),
                                   deps=[ld[par][0], ld[par][3], sc_war[bank]])
                        mm2 = p.op("t", lambda e, ci=ci, bank=bank, par=par, qc0=qc0:
                                   e.matmul(sc_ps[bank][:],
                                            kTr[par][:, ci * 128:(ci + 1) * 128],
                                            qTr[par][:, qc0:qc0 + 512],
                                            start=False, stop=True,
                                            skip_group_check=True),
                                   deps=[ld[par][1], ld[par][4]])
                        ex = p.op("s", lambda e, bank=bank, slot=slot: e.activation(
                            probs[slot][:], sc_ps[bank][:], AF.Exp,
                            bias=negc_bc[:], scale=1.0),
                            deps=[mm2, nbc, probs_war[slot]])
                        sc_war[bank] = ex
                        ptok = ex
                        if ci >= ncch - 8:
                            off = ci - (ncch - 8)
                            ptok = p.op("v", lambda e, slot=slot, off=off: e.tensor_tensor(
                                probs[slot][:], probs[slot][:],
                                mask_sb[:, off * 512:(off + 1) * 512], ALU.mult),
                                deps=[ex])
                      if pend_av is not None:
                        cj, slot_j, ptok_j = pend_av
                        avd = [ptok_j, ld[par][2]]
                        if cj == 0:
                            avd.append(av_war)
                        av_stop = p.op("t", lambda e, cj=cj, slot=slot_j, par=par, n=ncch:
                                       e.matmul(av_ps[:],
                                                vh[par][:, cj * 128:(cj + 1) * 128],
                                                probs[slot][:],
                                                start=(cj == 0), stop=(cj == n - 1),
                                                skip_group_check=True), deps=avd)
                        rd = [ptok_j]
                        if cj == 0:
                            rd.append(r_war)
                        r_stop = p.op("t", lambda e, cj=cj, slot=slot_j, n=ncch:
                                      e.matmul(r_ps[:], ones_st[:], probs[slot][:],
                                               start=(cj == 0), stop=(cj == n - 1),
                                               skip_group_check=True), deps=rd)
                        probs_war[slot_j] = r_stop
                      if ci < ncch:
                        pend_av = (ci, slot, ptok)
                    rr = p.op("v", lambda e: e.tensor_copy(rrow[:], r_ps[:]),
                              deps=[r_stop])
                    r_war = rr
                    ri = p.op("v", lambda e: e.reciprocal(rinv[:], rrow[:]), deps=[rr])
                    bq2 = p.op("t", lambda e: e.matmul(
                        bc_ps[:], ones_bc[:], rinv[:], start=True, stop=True,
                        skip_group_check=True), deps=[ri, nbc])
                    rbc = p.op("v", lambda e: e.tensor_copy(rinv_bc[:], bc_ps[:]),
                               deps=[bq2, av_war])
                    bc_war = rbc
                    apar = Qi & 1
                    ast = p.op("v", lambda e, apar=apar: e.tensor_tensor(
                        astage[apar][:], av_ps[:], rinv_bc[:], ALU.mult),
                        deps=[av_stop, rbc, ast_war[apar]])
                    av_war = ast
                    ast_war[apar] = p.dma("q", f"ast{apar}",
                                          attnT_s.ap()[h * 128:(h + 1) * 128,
                                                       qc0:qc0 + 512],
                                          astage[apar][:], deps=[ast])
                    negc_war = ex
                buf_war[par] = [sc_war[0], sc_war[1], av_stop, mm1, mm2]
            p.barrier()
            p.flush()

        with ExitStack() as SB:
            eb = SB.enter_context
            xqT = eb(nc.sbuf_tensor("xqT", [128, DCN * MYT], BF16))
            xq_tok = [[None] * DCN for _ in range(MYB)]
            cf_tok = [None] * MYB

            SB2 = ExitStack()
            eb2 = SB2.enter_context
            si3_bc = eb2(nc.sbuf_tensor("si3_bc", [128, MYB * 512], BF16))

            with ExitStack() as P3:
                e3 = P3.enter_context
                o_ps = [e3(nc.psum_tensor(f"o_ps{i}", [128, 512], F32)) for i in range(2)]
                stm_ps = e3(nc.psum_tensor("stm_ps", [1, 512], F32))
                st2_ps = e3(nc.psum_tensor("st2b_ps", [1, 512], F32))
                tp_ps = e3(nc.psum_tensor("tp3_ps", [128, 128], F32))
                bc_ps = e3(nc.psum_tensor("bc4_ps", [128, 512], F32))
                at_sb = e3(nc.sbuf_tensor("at_sb", [128, DCN * 512], BF16))
                ow_sb = [e3(nc.sbuf_tensor(f"ow_sb{i}", [128, DCN * 128], BF16))
                         for i in range(2)]
                xm_c = [e3(nc.sbuf_tensor(f"xm_c{i}", [128, 512], F32)) for i in range(3)]
                y1blk = e3(nc.sbuf_tensor("y1blk", [128, DCN * 512], F32))
                y1b = [e3(nc.sbuf_tensor(f"y1b{i}", [128, 512], BF16)) for i in range(2)]
                sqy = [e3(nc.sbuf_tensor(f"sqy{i}", [128, 512], BF16)) for i in range(2)]
                mx = e3(nc.sbuf_tensor("mx", [128, 512], F32))
                mn = e3(nc.sbuf_tensor("mn", [128, 512], F32))
                tr64 = e3(nc.sbuf_tensor("tr64", [64, 512], F32))
                tr32 = e3(nc.sbuf_tensor("tr32", [32, 512], F32))
                toks = e3(nc.sbuf_tensor("toks", [128, 24], F32))
                rows3 = e3(nc.sbuf_tensor("rows3", [1, 2048], F32))
                mu2_bc = e3(nc.sbuf_tensor("mu2_bc", [128, 512], F32))
                rs2_bc = e3(nc.sbuf_tensor("rs2_bc", [128, 512], F32))
                s_bc = e3(nc.sbuf_tensor("s_bc", [128, 512], F32))
                si_bc = e3(nc.sbuf_tensor("si_bc", [128, 512], F32))
                si3f = e3(nc.sbuf_tensor("si3f", [128, 512], F32))
                tq = [e3(nc.sbuf_tensor(f"tq{i}", [128, 512], F32)) for i in range(2)]
                yq = [e3(nc.sbuf_tensor(f"yq{i}", [128, 512], F32)) for i in range(2)]
                i32_t = [e3(nc.sbuf_tensor(f"i32_t{i}", [128, 512], I32)) for i in range(2)]
                r0_t = [e3(nc.sbuf_tensor(f"r0_t{i}", [128, 512], F32)) for i in range(2)]
                d_t = [e3(nc.sbuf_tensor(f"d_t{i}", [128, 512], F32)) for i in range(2)]
                a_t = [e3(nc.sbuf_tensor(f"a_t{i}", [128, 512], F32)) for i in range(2)]
                b_t = [e3(nc.sbuf_tensor(f"b_t{i}", [128, 512], F32)) for i in range(2)]

                at_tok = None
                at_war = None
                ow_tok = [None, None]
                ow_war = [None, None]
                xm_war = [None] * 3
                ops_war = [None, None]
                stm_war = None
                st2_war = None
                y1b_war = [None, None]
                sqy_war = [None, None]
                y1_war = [None] * DCN
                y1blk_war = [None] * DCN
                tp_war = None
                bc_war = None
                mxtree_war = None
                si3f_war = None
                chain_war = [None, None]
                tq_war = [None, None]
                yq_war3 = [None, None]
                i32_war3 = [None, None]
                r0_war3 = [None, None]
                d_war3 = [None, None]
                a_war3 = [None, None]
                b_war3 = [None, None]
                for blk in range(MYB):
                    gc0 = blk * 512
                    for fc in range(DCN):
                        at_tok = p.dma("s", "atld", at_sb[:, fc * 512:(fc + 1) * 512],
                                       attnT_s.ap()[fc * 128:(fc + 1) * 128,
                                                    gc0:gc0 + 512],
                                       deps=([at_war] if fc == 0 else []))
                    mxi = p.op("v", lambda e: e.memset(mx[:], -3.0e38),
                               deps=[mxtree_war])
                    mni = p.op("v", lambda e: e.memset(mn[:], 3.0e38),
                               deps=[mxtree_war])
                    stop = None
                    for dc in range(DCN):
                        opar = dc & 1
                        ow_tok[opar] = p.dma("g", f"owl{opar}",
                                             ow_sb[opar][:].rearrange(
                                                 "p (c f) -> p c f", c=DCN),
                                             ow_h.ap()[dc], deps=[ow_war[opar]])
                        cr = dc % 3
                        xm = p.dma("q", f"xm{cr}", xm_c[cr][:],
                                   xoT_h.ap()[dc, :, gc0:gc0 + 512],
                                   deps=[xm_war[cr]])
                        for fc in range(DCN):
                            dd = [at_tok, ow_tok[opar]]
                            if fc == 0:
                                dd.append(ops_war[opar])
                            stop = p.op("t", lambda e, fc=fc, opar=opar: e.matmul(
                                o_ps[opar][:],
                                ow_sb[opar][:, fc * 128:(fc + 1) * 128],
                                at_sb[:, fc * 512:(fc + 1) * 512],
                                start=(fc == 0), stop=(fc == DCN - 1),
                                skip_group_check=True), deps=dd)
                        ow_war[opar] = stop
                        y1c = p.op("v", lambda e, dc=dc, opar=opar, cr=cr: e.tensor_tensor(
                            y1blk[:, dc * 512:(dc + 1) * 512], o_ps[opar][:],
                            xm_c[cr][:], ALU.add),
                            deps=[stop, xm, y1blk_war[dc], p.latest("y1st")])
                        ops_war[opar] = y1c
                        xm_war[cr] = y1c
                        mxi = p.op("v", lambda e, dc=dc: e.tensor_tensor(
                            mx[:], mx[:], y1blk[:, dc * 512:(dc + 1) * 512], ALU.max),
                            deps=[y1c, mxi])
                        mni = p.op("v", lambda e, dc=dc: e.tensor_tensor(
                            mn[:], mn[:], y1blk[:, dc * 512:(dc + 1) * 512], ALU.min),
                            deps=[y1c, mni])
                        sp = dc & 1
                        yb = p.op("s", lambda e, dc=dc, sp=sp: e.copy(
                            y1b[sp][:], y1blk[:, dc * 512:(dc + 1) * 512]),
                            deps=[y1c] + (y1b_war[sp] or []))
                        sy = p.op("g", lambda e, sp=sp: e.tensor_tensor(
                            sqy[sp][:], y1b[sp][:], y1b[sp][:], ALU.mult),
                            deps=[yb, sqy_war[sp]])
                        stm = p.op("t", lambda e, dc=dc, sp=sp: e.matmul(
                            stm_ps[:], ones_st[:], y1b[sp][:],
                            start=(dc == 0), stop=(dc == DCN - 1),
                            skip_group_check=True),
                            deps=[yb] + ([stm_war] if dc == 0 else []))
                        st2 = p.op("t", lambda e, dc=dc, sp=sp: e.matmul(
                            st2_ps[:], ones_st[:], sqy[sp][:],
                            start=(dc == 0), stop=(dc == DCN - 1),
                            skip_group_check=True),
                            deps=[sy] + ([st2_war] if dc == 0 else []))
                        y1b_war[sp] = [stm, sy]
                        sqy_war[sp] = st2
                        y1_war[dc] = p.dma("q", "y1st", y1_s.ap()[dc, :, gc0:gc0 + 512],
                                           y1blk[:, dc * 512:(dc + 1) * 512],
                                           deps=[y1c])
                    at_war = stop
                    muv = p.op("v", lambda e: e.tensor_scalar_mul(
                        rows3[0:1, 0:512], stm_ps[:], 1.0 / D), deps=[stm])
                    stm_war = muv
                    ex2 = p.op("v", lambda e: e.tensor_scalar_mul(
                        rows3[0:1, 512:1024], st2_ps[:], 1.0 / D), deps=[st2])
                    st2_war = ex2
                    ms2 = p.op("v", lambda e: e.tensor_tensor(
                        rows3[0:1, 1024:1536], rows3[0:1, 0:512], rows3[0:1, 0:512],
                        ALU.mult), deps=[muv])
                    vr = p.op("v", lambda e: e.scalar_tensor_tensor(
                        rows3[0:1, 512:1024], rows3[0:1, 512:1024], EPS_LN,
                        rows3[0:1, 1024:1536], ALU.add, ALU.subtract),
                        deps=[ex2, ms2])
                    l1 = p.op("s", lambda e: e.activation(
                        rows3[0:1, 512:1024], rows3[0:1, 512:1024], AF.Ln), deps=[vr])
                    l2 = p.op("s", lambda e: e.activation(
                        rows3[0:1, 512:1024], rows3[0:1, 512:1024], AF.Exp,
                        scale=-0.5), deps=[l1])
                    for k in range(4):
                        t1 = p.op("t", lambda e, k=k: e.transpose(
                            tp_ps[:, 0:1], rows3[0:1, k * 128:(k + 1) * 128],
                            ident_f[0:1, 0:1]), deps=[muv, tp_war])
                        c1 = p.op("v", lambda e, k=k: e.tensor_copy(
                            toks[:, k:k + 1], tp_ps[:, 0:1]), deps=[t1])
                        t2 = p.op("t", lambda e, k=k: e.transpose(
                            tp_ps[:, 0:1],
                            rows3[0:1, 512 + k * 128:512 + (k + 1) * 128],
                            ident_f[0:1, 0:1]), deps=[l2, c1])
                        c2 = p.op("v", lambda e, k=k: e.tensor_copy(
                            toks[:, 4 + k:5 + k], tp_ps[:, 0:1]), deps=[t2])
                        tp_war = c2
                    t64c = p.op("v", lambda e: e.tensor_copy(
                        tr64[:], mx[64:128, :]), deps=[mxi])
                    t64 = p.op("v", lambda e: e.tensor_tensor(
                        tr64[:], tr64[:], mx[0:64, :], ALU.max), deps=[t64c])
                    t32c = p.op("v", lambda e: e.tensor_copy(
                        tr32[:], tr64[32:64, :]), deps=[t64])
                    t32 = p.op("v", lambda e: e.tensor_tensor(
                        tr32[:], tr32[:], tr64[0:32, :], ALU.max), deps=[t32c])
                    for k in range(4):
                        tt = p.op("t", lambda e, k=k: e.transpose(
                            tp_ps[:, 0:32], tr32[:, k * 128:(k + 1) * 128],
                            ident_f[0:32, 0:32]), deps=[t32, tp_war])
                        rm = p.op("v", lambda e, k=k: e.reduce_max(
                            toks[:, 8 + k:9 + k], tp_ps[:, 0:32], AX.X), deps=[tt])
                        tp_war = rm
                    t64nc = p.op("v", lambda e: e.tensor_copy(
                        tr64[:], mn[64:128, :]), deps=[mni, t32])
                    t64n = p.op("v", lambda e: e.tensor_tensor(
                        tr64[:], tr64[:], mn[0:64, :], ALU.min), deps=[t64nc])
                    t32nc = p.op("v", lambda e: e.tensor_copy(
                        tr32[:], tr64[32:64, :]), deps=[t64n])
                    t32n = p.op("v", lambda e: e.tensor_tensor(
                        tr32[:], tr32[:], tr64[0:32, :], ALU.min), deps=[t32nc])
                    for k in range(4):
                        tt = p.op("t", lambda e, k=k: e.transpose(
                            tp_ps[:, 0:32], tr32[:, k * 128:(k + 1) * 128],
                            ident_f[0:32, 0:32]), deps=[t32n, tp_war])
                        rm = p.op("v", lambda e, k=k: e.tensor_reduce(
                            toks[:, 12 + k:13 + k], tp_ps[:, 0:32], AX.X, ALU.min,
                            negate=True), deps=[tt])
                        tp_war = rm
                    mxtree_war = tp_war
                    v1 = p.op("v", lambda e: e.tensor_tensor(
                        toks[:, 16:20], toks[:, 8:12], toks[:, 0:4], ALU.subtract),
                        deps=[tp_war])
                    v2 = p.op("v", lambda e: e.tensor_tensor(
                        toks[:, 20:24], toks[:, 12:16], toks[:, 0:4], ALU.add),
                        deps=[v1])
                    v3 = p.op("v", lambda e: e.tensor_tensor(
                        toks[:, 16:20], toks[:, 16:20], toks[:, 20:24], ALU.max),
                        deps=[v2])
                    v4 = p.op("v", lambda e: e.tensor_tensor(
                        toks[:, 16:20], toks[:, 16:20], toks[:, 4:8], ALU.mult),
                        deps=[v3])
                    v5 = p.op("v", lambda e: e.tensor_scalar_max(
                        toks[:, 16:20], toks[:, 16:20], 1e-5), deps=[v4])
                    v6 = p.op("v", lambda e: e.reciprocal(
                        toks[:, 20:24], toks[:, 16:20]), deps=[v5])
                    v7 = p.op("v", lambda e: e.tensor_scalar_mul(
                        toks[:, 20:24], toks[:, 20:24], 127.0), deps=[v6])
                    v8 = p.op("v", lambda e: e.tensor_scalar_mul(
                        toks[:, 16:20], toks[:, 16:20], INV127), deps=[v5, v7])
                    for k in range(4):
                        t1 = p.op("t", lambda e, k=k: e.transpose(
                            tp_ps[0:1, :], toks[:, 20 + k:21 + k], ident_f[:]),
                            deps=[v7, tp_war])
                        c1 = p.op("v", lambda e, k=k: e.tensor_copy(
                            rows3[0:1, 1024 + k * 128:1024 + (k + 1) * 128],
                            tp_ps[0:1, :]), deps=[t1])
                        t2 = p.op("t", lambda e, k=k: e.transpose(
                            tp_ps[0:1, :], toks[:, 16 + k:17 + k], ident_f[:]),
                            deps=[v8, c1])
                        c2 = p.op("v", lambda e, k=k: e.tensor_copy(
                            rows3[0:1, 1536 + k * 128:1536 + (k + 1) * 128],
                            tp_ps[0:1, :]), deps=[t2])
                        tp_war = c2
                    toks_done = tp_war
                    bt = p.op("t", lambda e: e.matmul(
                        bc_ps[:], ones_bc[:], rows3[0:1, 0:512], start=True, stop=True,
                        skip_group_check=True), deps=[muv, bc_war])
                    bmu = p.op("v", lambda e: e.tensor_copy(mu2_bc[:], bc_ps[:]),
                               deps=[bt, chain_war[0], chain_war[1]])
                    bt = p.op("t", lambda e: e.matmul(
                        bc_ps[:], ones_bc[:], rows3[0:1, 512:1024], start=True,
                        stop=True, skip_group_check=True), deps=[l2, bmu])
                    brs = p.op("v", lambda e: e.tensor_copy(rs2_bc[:], bc_ps[:]),
                               deps=[bt, chain_war[1], chain_war[0]])
                    bt = p.op("t", lambda e: e.matmul(
                        bc_ps[:], ones_bc[:], rows3[0:1, 1024:1536], start=True,
                        stop=True, skip_group_check=True), deps=[toks_done, brs])
                    bs = p.op("v", lambda e: e.tensor_copy(s_bc[:], bc_ps[:]),
                              deps=[bt])
                    bt = p.op("t", lambda e: e.matmul(
                        bc_ps[:], ones_bc[:], rows3[0:1, 1536:2048], start=True,
                        stop=True, skip_group_check=True), deps=[toks_done, bs])
                    bsi = p.op("v", lambda e: e.tensor_copy(si_bc[:], bc_ps[:]),
                               deps=[bt])
                    s3a = p.op("v", lambda e, blk=blk: e.tensor_tensor(
                        si3f[:], si_bc[:], si_bc[:], ALU.mult), deps=[bsi, si3f_war])
                    s3b = p.op("v", lambda e, blk=blk: e.tensor_tensor(
                        si3_bc[:, blk * 512:(blk + 1) * 512], si3f[:], si_bc[:],
                        ALU.mult), deps=[s3a])
                    si3f_war = s3b
                    bc_war = bsi
                    for dc0 in range(0, DCN, 2):
                        pair = (dc0, dc0 + 1)
                        hh = {}
                        for dc in pair:
                            cp = dc & 1
                            hh[(dc, "q1")] = p.op(
                                "v", lambda e, dc=dc, cp=cp: e.tensor_tensor(
                                    tq[cp][:], y1blk[:, dc * 512:(dc + 1) * 512],
                                    mu2_bc[:], ALU.subtract),
                                deps=[bmu, tq_war[cp]])
                            y1blk_war[dc] = hh[(dc, "q1")]
                        for dc in pair:
                            cp = dc & 1
                            hh[(dc, "q2")] = p.op("g", lambda e, cp=cp: e.tensor_tensor(
                                tq[cp][:], tq[cp][:], rs2_bc[:], ALU.mult),
                                deps=[hh[(dc, "q1")], brs])
                        for dc in pair:
                            cp = dc & 1
                            hh[(dc, "q3")] = p.op("v", lambda e, cp=cp: e.tensor_tensor(
                                yq[cp][:], tq[cp][:], s_bc[:], ALU.mult),
                                deps=[hh[(dc, "q2")], bs, yq_war3[cp]])
                            tq_war[cp] = hh[(dc, "q3")]
                        for dc in pair:
                            cp = dc & 1
                            c1 = p.op("s", lambda e, cp=cp: e.copy(
                                i32_t[cp][:], yq[cp][:]),
                                deps=[hh[(dc, "q3")], i32_war3[cp]])
                            c2 = p.op("s", lambda e, cp=cp: e.copy(
                                r0_t[cp][:], i32_t[cp][:]), deps=[c1, r0_war3[cp]])
                            i32_war3[cp] = c2
                            hh[(dc, "c2")] = c2
                        for dc in pair:
                            cp = dc & 1
                            hh[(dc, "dd")] = p.op("g", lambda e, cp=cp: e.tensor_tensor(
                                d_t[cp][:], yq[cp][:], r0_t[cp][:], ALU.subtract),
                                deps=[hh[(dc, "c2")], d_war3[cp]])
                        for dc in pair:
                            cp = dc & 1
                            ab = p.op("s", lambda e, cp=cp: e.activation(
                                yq[cp][:], d_t[cp][:], AF.Abs), deps=[hh[(dc, "dd")]])
                            hh[(dc, "sg")] = p.op("s", lambda e, cp=cp: e.activation(
                                d_t[cp][:], d_t[cp][:], AF.Sign), deps=[ab])
                            hh[(dc, "ab")] = ab
                        cmh = {}
                        for dc in pair:
                            cp = dc & 1
                            cmh[dc] = p.op("v", lambda e, cp=cp: e.tensor_single_scalar(
                                a_t[cp][:], yq[cp][:], 0.5, ALU.is_gt),
                                deps=[hh[(dc, "ab")], a_war3[cp]])
                        for thr in (1.5, 2.5, 3.5, 4.5, 5.5):
                            for dc in pair:
                                cp = dc & 1
                                cmh[dc] = p.op("v", lambda e, cp=cp, thr=thr:
                                               e.scalar_tensor_tensor(
                                                   a_t[cp][:], yq[cp][:], thr,
                                                   a_t[cp][:], ALU.is_gt, ALU.add),
                                               deps=[cmh[dc]])
                        for dc in pair:
                            cp = dc & 1
                            hh[(dc, "co")] = p.op("g", lambda e, cp=cp: e.tensor_tensor(
                                b_t[cp][:], a_t[cp][:], d_t[cp][:], ALU.mult),
                                deps=[cmh[dc], hh[(dc, "sg")], b_war3[cp]])
                            yq_war3[cp] = cmh[dc]
                            a_war3[cp] = hh[(dc, "co")]
                            d_war3[cp] = hh[(dc, "co")]
                        for dc in pair:
                            cp = dc & 1
                            xq = p.op("v", lambda e, dc=dc, cp=cp, gc0=gc0:
                                      e.tensor_tensor(
                                          xqT[:, dc * MYT + gc0:dc * MYT + gc0 + 512],
                                          b_t[cp][:], r0_t[cp][:], ALU.add),
                                      deps=[hh[(dc, "co")]])
                            xq_tok[blk][dc] = xq
                            b_war3[cp] = xq
                            r0_war3[cp] = xq
                            chain_war[cp] = xq
                p.barrier()
                p.flush()

            gw_store = [[None] * MYB for _ in range(FFC)]
            cf_bc = eb2(nc.sbuf_tensor("cf_bc", [128, MYB * 512], BF16))
            sif_bc = eb2(nc.sbuf_tensor("sif_bc", [128, MYB * 512], BF16))
            mxf = eb2(nc.sbuf_tensor("mxf", [128, MYB * 512], BF16))
            mnf = eb2(nc.sbuf_tensor("mnf", [128, MYB * 512], BF16))
            statf_row = eb2(nc.sbuf_tensor("statf_row", [1, MYB * 512], BF16))
            with ExitStack() as P4:
                e4 = P4.enter_context
                f_ps = [e4(nc.psum_tensor(f"f_ps{i}", [128, 512], F32)) for i in range(4)]
                stf_ps = e4(nc.psum_tensor("stf_ps", [1, MYB * 512], F32))
                w1_sb = [e4(nc.sbuf_tensor(f"w1_sb{i}", [128, DCN * 128], BF16))
                         for i in range(2)]
                w3_sb = [e4(nc.sbuf_tensor(f"w3_sb{i}", [128, DCN * 128], BF16))
                         for i in range(2)]
                u_t = [e4(nc.sbuf_tensor(f"u_t{i}", [128, 512], F32)) for i in range(2)]
                u2_t = [e4(nc.sbuf_tensor(f"u2_t{i}", [128, 512], F32)) for i in range(2)]
                g_t = [e4(nc.sbuf_tensor(f"g_t{i}", [128, 512], F32)) for i in range(2)]
                gw_t = [e4(nc.sbuf_tensor(f"gw_t{i}", [128, 512], BF16)) for i in range(2)]
                sqf = [e4(nc.sbuf_tensor(f"sqf{i}", [128, 512], BF16)) for i in range(2)]

                w1_tok = [None, None]
                w3_tok = [None, None]
                w_war = [None, None]
                fps_war = [None] * 4
                u_war = [None, None]
                g_war = [None, None]
                gsq_war = [None, None]
                gw_war = [None, None]
                sq_war = [None, None]
                mx_tok = p.op("v", lambda e: e.memset(mxf[:], -3.0e38), deps=[])
                mn_tok = p.op("v", lambda e: e.memset(mnf[:], 3.0e38), deps=[])
                stf_stop = [None] * MYB
                pend_stat = []
                for fc in range(FFC):
                    wpar = fc & 1
                    w1_tok[wpar] = p.dma("g", f"wld{wpar}",
                                         w1_sb[wpar][:].rearrange("p (c f) -> p c f", c=DCN),
                                         w1_h.ap()[fc], deps=[w_war[wpar]])
                    w3_tok[wpar] = p.dma("g", f"wld{wpar}",
                                         w3_sb[wpar][:].rearrange("p (c f) -> p c f", c=DCN),
                                         w3_h.ap()[fc], deps=[])
                    for blk in range(MYB):
                        bp = blk & 1
                        p1b, p3b = bp, 2 + bp
                        stop1 = None
                        stop3 = None
                        for dc in range(DCN):
                            dd1 = [xq_tok[blk][dc], w3_tok[wpar]]
                            if dc == 0:
                                dd1.append(fps_war[p1b])
                            stop1 = p.op("t", lambda e, dc=dc, blk=blk, wpar=wpar, p1b=p1b:
                                         e.matmul(f_ps[p1b][:],
                                                  w1_sb[wpar][:, dc * 128:(dc + 1) * 128],
                                                  xqT[:, dc * MYT + blk * 512:
                                                      dc * MYT + (blk + 1) * 512],
                                                  start=(dc == 0), stop=(dc == DCN - 1),
                                                  skip_group_check=True), deps=dd1)
                        for dc in range(DCN):
                            dd3 = [xq_tok[blk][dc], w3_tok[wpar]]
                            if dc == 0:
                                dd3.append(fps_war[p3b])
                            stop3 = p.op("t", lambda e, dc=dc, blk=blk,
                                         wpar=wpar, p3b=p3b:
                                         e.matmul(f_ps[p3b][:],
                                                  w3_sb[wpar][:, dc * 128:(dc + 1) * 128],
                                                  xqT[:, dc * MYT + blk * 512:
                                                      dc * MYT + (blk + 1) * 512],
                                                  start=(dc == 0), stop=(dc == DCN - 1),
                                                  skip_group_check=True), deps=dd3)
                        sp = (fc * MYB + blk) & 1
                        u = p.op("s", lambda e, sp=sp, p1b=p1b: e.activation(
                            u_t[sp][:], f_ps[p1b][:], AF.Relu), deps=[stop1, u_war[sp]])
                        u2 = p.op("v", lambda e, sp=sp: e.tensor_tensor(
                            u2_t[sp][:], u_t[sp][:], u_t[sp][:], ALU.mult), deps=[u])
                        u_war[sp] = u2
                        g = p.op("v", lambda e, sp=sp, p3b=p3b: e.tensor_tensor(
                            g_t[sp][:], u2_t[sp][:], f_ps[p3b][:], ALU.mult),
                            deps=[u2, stop3, g_war[sp], gsq_war[sp]])
                        g = p.op("v", lambda e, sp=sp, blk=blk: e.tensor_tensor(
                            g_t[sp][:], g_t[sp][:],
                            si3_bc[:, blk * 512:(blk + 1) * 512], ALU.mult),
                            deps=[g])
                        fps_war[p1b] = u
                        fps_war[p3b] = g
                        gw = p.op("v", lambda e, sp=sp, fc=fc: e.tensor_scalar_mul(
                            gw_t[sp][:], g_t[sp][:], gf_sb[:, fc:fc + 1]),
                            deps=[g, g_f, gw_war[sp], mn_tok])
                        g_war[sp] = gw
                        sq = p.op("g", lambda e, sp=sp: e.tensor_tensor(
                            sqf[sp][:], g_t[sp][:], g_t[sp][:], ALU.mult),
                            deps=[g, sq_war[sp]])
                        gsq_war[sp] = sq
                        pend_stat.append((fc, blk, sp, sq))
                        if len(pend_stat) > 1:
                            fc2, blk2, sp2, sq2 = pend_stat.pop(0)
                            stf_stop[blk2] = p.op(
                                "t", lambda e, fc2=fc2, blk2=blk2, sp2=sp2:
                                e.matmul(stf_ps[0:1, blk2 * 512:(blk2 + 1) * 512],
                                         ones_st[:], sqf[sp2][:],
                                         start=(fc2 == 0), stop=(fc2 == FFC - 1),
                                         skip_group_check=True), deps=[sq2])
                            sq_war[sp2] = stf_stop[blk2]
                        mx_tok = p.op("v", lambda e, blk=blk, sp=sp: e.tensor_tensor(
                            mxf[:, blk * 512:(blk + 1) * 512],
                            mxf[:, blk * 512:(blk + 1) * 512], gw_t[sp][:], ALU.max),
                            deps=[gw, mx_tok])
                        mn_tok = p.op("v", lambda e, blk=blk, sp=sp: e.tensor_tensor(
                            mnf[:, blk * 512:(blk + 1) * 512],
                            mnf[:, blk * 512:(blk + 1) * 512], gw_t[sp][:], ALU.min),
                            deps=[gw, mx_tok])
                        gw_store[fc][blk] = p.dma("q", f"gst{sp}",
                                                  gw_s.ap()[fc, :, blk * 512:(blk + 1) * 512],
                                                  gw_t[sp][:], deps=[gw])
                        gw_war[sp] = gw_store[fc][blk]
                    w_war[wpar] = stop3
                while pend_stat:
                    fc2, blk2, sp2, sq2 = pend_stat.pop(0)
                    stf_stop[blk2] = p.op(
                        "t", lambda e, fc2=fc2, blk2=blk2, sp2=sp2:
                        e.matmul(stf_ps[0:1, blk2 * 512:(blk2 + 1) * 512],
                                 ones_st[:], sqf[sp2][:],
                                 start=(fc2 == 0), stop=(fc2 == FFC - 1),
                                 skip_group_check=True), deps=[sq2])
                    sq_war[sp2] = stf_stop[blk2]
                sfr = p.op("v", lambda e: e.tensor_scalar(
                    statf_row[:], stf_ps[:], 1.0 / FF, EPS_RMS_FFN, ALU.mult, ALU.add),
                    deps=stf_stop)
                p.barrier()
                p.flush()

            with ExitStack() as P4T:
                e5 = P4T.enter_context
                tp_ps = e5(nc.psum_tensor("tp4_ps", [128, 128], F32))
                tpb_ps = e5(nc.psum_tensor("tpb_ps", [128, 128], BF16))
                bc_ps = e5(nc.psum_tensor("bc5_ps", [128, 512], F32))
                amf = e5(nc.sbuf_tensor("amf", [128, 512], BF16))
                trb64 = e5(nc.sbuf_tensor("trb64", [64, 512], BF16))
                trb32 = e5(nc.sbuf_tensor("trb32", [32, 512], BF16))
                toks4 = e5(nc.sbuf_tensor("toks4", [128, 8], F32))
                rows4 = e5(nc.sbuf_tensor("rows4", [1, 1536], F32))
                rsf = p.op("s", lambda e: e.activation(statf_row[:], statf_row[:],
                                                       AF.Ln), deps=[sfr])
                rsf2 = p.op("s", lambda e: e.activation(statf_row[:], statf_row[:],
                                                        AF.Exp, scale=-0.5), deps=[rsf])
                tp_war = None
                bc_war = None
                am_war = None
                for blk in range(MYB):
                    ng = p.op("v", lambda e, blk=blk: e.tensor_scalar_mul(
                        amf[:], mnf[:, blk * 512:(blk + 1) * 512], -1.0),
                        deps=[mn_tok, am_war])
                    am = p.op("v", lambda e, blk=blk: e.tensor_tensor(
                        amf[:], amf[:], mxf[:, blk * 512:(blk + 1) * 512], ALU.max),
                        deps=[ng, mx_tok])
                    t64c = p.op("v", lambda e: e.tensor_copy(
                        trb64[:], amf[64:128, :]), deps=[am])
                    t64 = p.op("v", lambda e: e.tensor_tensor(
                        trb64[:], trb64[:], amf[0:64, :], ALU.max), deps=[t64c])
                    t32c = p.op("v", lambda e: e.tensor_copy(
                        trb32[:], trb64[32:64, :]), deps=[t64])
                    t32 = p.op("v", lambda e: e.tensor_tensor(
                        trb32[:], trb32[:], trb64[0:32, :], ALU.max), deps=[t32c])
                    am_war = t64
                    ktoks = []
                    for k in range(4):
                        tt = p.op("t", lambda e, k=k: e.transpose(
                            tpb_ps[:, 0:32], trb32[:, k * 128:(k + 1) * 128],
                            ident_b[0:32, 0:32]), deps=[t32, tp_war])
                        rm = p.op("v", lambda e, k=k: e.reduce_max(
                            toks4[:, 0:1], tpb_ps[:, 0:32], AX.X), deps=[tt])
                        Mc = p.op("v", lambda e: e.tensor_scalar_max(
                            toks4[:, 0:1], toks4[:, 0:1], 1e-5), deps=[rm])
                        R = p.op("v", lambda e: e.reciprocal(
                            toks4[:, 1:2], toks4[:, 0:1]), deps=[Mc])
                        sv = p.op("v", lambda e: e.tensor_scalar_mul(
                            toks4[:, 2:3], toks4[:, 1:2], 127.0), deps=[R])
                        siv = p.op("v", lambda e: e.tensor_scalar_mul(
                            toks4[:, 3:4], toks4[:, 0:1], INV127), deps=[Mc, sv])
                        ts1 = p.op("t", lambda e: e.transpose(
                            tp_ps[0:1, :], toks4[:, 2:3], ident_f[:]),
                            deps=[sv, tt])
                        cs1 = p.op("v", lambda e, k=k: e.tensor_copy(
                            rows4[0:1, k * 128:(k + 1) * 128], tp_ps[0:1, :]),
                            deps=[ts1])
                        ts2 = p.op("t", lambda e: e.transpose(
                            tp_ps[0:1, :], toks4[:, 3:4], ident_f[:]),
                            deps=[siv, cs1])
                        cs2 = p.op("v", lambda e, k=k: e.tensor_copy(
                            rows4[0:1, 512 + k * 128:512 + (k + 1) * 128],
                            tp_ps[0:1, :]), deps=[ts2])
                        tp_war = cs2
                        ktoks.append(cs2)
                    cr = p.op("v", lambda e, blk=blk: e.tensor_tensor(
                        rows4[0:1, 1024:1536], rows4[0:1, 512:1024],
                        statf_row[0:1, blk * 512:(blk + 1) * 512], ALU.mult),
                        deps=ktoks + [rsf2])
                    b1 = p.op("t", lambda e: e.matmul(
                        bc_ps[:], ones_bc[:], rows4[0:1, 0:512], start=True,
                        stop=True, skip_group_check=True), deps=ktoks + [bc_war])
                    c1 = p.op("v", lambda e, blk=blk: e.tensor_copy(
                        cf_bc[:, blk * 512:(blk + 1) * 512], bc_ps[:]), deps=[b1])
                    b2 = p.op("t", lambda e: e.matmul(
                        bc_ps[:], ones_bc[:], rows4[0:1, 1024:1536], start=True,
                        stop=True, skip_group_check=True), deps=[c1, cr])
                    c2 = p.op("v", lambda e, blk=blk: e.tensor_copy(
                        sif_bc[:, blk * 512:(blk + 1) * 512], bc_ps[:]), deps=[b2])
                    bc_war = c2
                    cf_tok[blk] = c2
                p.barrier()
                p.flush()

            with ExitStack() as P5:
                e6 = P5.enter_context
                o5_ps = [e6(nc.psum_tensor(f"o5_ps{i}", [128, 512], F32)) for i in range(2)]
                iqT = e6(nc.sbuf_tensor("iqT", [128, FFC * 512], BF16))
                w2_sb = [e6(nc.sbuf_tensor(f"w2_sb{i}", [128, FFC * 128], BF16))
                         for i in range(2)]
                NCH = 4
                gw_c = [e6(nc.sbuf_tensor(f"gw_c{i}", [128, 512], BF16)) for i in range(NCH)]
                yq5 = [e6(nc.sbuf_tensor(f"yq5_{i}", [128, 512], F32)) for i in range(NCH)]
                i32_5 = [e6(nc.sbuf_tensor(f"i32_5_{i}", [128, 512], I32)) for i in range(2)]
                r0n_5 = [e6(nc.sbuf_tensor(f"r0n_5_{i}", [128, 512], BF16)) for i in range(NCH)]
                sg_5 = [e6(nc.sbuf_tensor(f"sg_5_{i}", [128, 512], BF16)) for i in range(NCH)]
                cb_5 = [e6(nc.sbuf_tensor(f"cb_5_{i}", [128, 512], BF16)) for i in range(NCH)]
                mb_5 = [e6(nc.sbuf_tensor(f"mb_5_{i}", [128, 512], F32)) for i in range(NCH)]
                y1c = [e6(nc.sbuf_tensor(f"y1c{i}", [128, 512], F32)) for i in range(2)]

                gw_ld = [None] * NCH
                q1_last = [None] * NCH
                iq_tok = [None] * FFC
                iq_guard = [None] * FFC
                w2_tok = [None, None]
                w2_war = [None, None]
                o5_war = [None, None]
                y1_war5 = [None, None]
                y2_war = [None, None]
                i32_war = [None, None]
                yq_war = [None] * NCH
                mb_war = [None] * NCH
                sg_war = [None] * NCH
                cb_war = [None] * NCH
                r0_war = [None] * NCH
                for blk in range(MYB):
                    gc0 = blk * 512
                    def head5(kc0, blk):
                        gc0b = blk * 512
                        pair = (kc0, kc0 + 1)
                        hh = {}
                        for kc in pair:
                            gp = kc & 3
                            if blk == 0 and kc < 2:
                                gw_ld[gp] = p.dma("q", f"gwld{gp}", gw_c[gp][:],
                                                  gw_s.ap()[kc, :, gc0b:gc0b + 512],
                                                  deps=[gw_store[kc][blk]])
                        for kc in pair:
                            gp = kc & 3
                            hh[(kc, "q1")] = p.op(
                                "v", lambda e, gp=gp, blk=blk: e.tensor_tensor(
                                    yq5[gp][:], gw_c[gp][:],
                                    cf_bc[:, blk * 512:(blk + 1) * 512], ALU.mult),
                                deps=[gw_ld[gp], cf_tok[blk], yq_war[gp]])
                            q1_last[gp] = hh[(kc, "q1")]
                        for kc in pair:
                            nkc, nblk = kc + 2, blk
                            if nkc >= FFC:
                                nkc, nblk = kc + 2 - FFC, blk + 1
                            gpn = nkc & 3
                            if nblk < MYB:
                                gw_ld[gpn] = p.dma("q", f"gwld{gpn}", gw_c[gpn][:],
                                                   gw_s.ap()[nkc, :, nblk * 512:nblk * 512 + 512],
                                                   deps=[q1_last[gpn], gw_store[nkc][nblk]])
                        for kc in pair:
                            gp = kc & 3
                            ip = kc & 1
                            c1 = p.op("s", lambda e, gp=gp, ip=ip: e.copy(
                                i32_5[ip][:], yq5[gp][:]),
                                deps=[hh[(kc, "q1")], i32_war[ip]])
                            c2 = p.op("s", lambda e, gp=gp, ip=ip: e.activation(
                                r0n_5[gp][:], i32_5[ip][:], AF.Copy, scale=-1.0),
                                deps=[c1, r0_war[gp]])
                            i32_war[ip] = c2
                            hh[(kc, "c2")] = c2
                        for kc in pair:
                            gp = kc & 3
                            hh[(kc, "dd")] = p.op("v", lambda e, gp=gp: e.tensor_tensor(
                                mb_5[gp][:], yq5[gp][:], r0n_5[gp][:], ALU.add),
                                deps=[hh[(kc, "c2")], mb_war[gp]])
                        for kc in pair:
                            gp = kc & 3
                            sg = p.op("s", lambda e, gp=gp: e.activation(
                                sg_5[gp][:], mb_5[gp][:], AF.Sign),
                                deps=[hh[(kc, "dd")], sg_war[gp]])
                            hh[(kc, "sg")] = sg
                            hh[(kc, "ab")] = p.op("s", lambda e, gp=gp: e.activation(
                                mb_5[gp][:], mb_5[gp][:], AF.Abs), deps=[sg])
                        lvh = {kc: hh[(kc, "ab")] for kc in pair}
                        for li, w in enumerate((128.0, 64.0, 32.0, 16.0, 8.0, 4.0)):
                            for kc in pair:
                                gp = kc & 3
                                ddl = [lvh[kc]]
                                if li == 0:
                                    ddl.append(cb_war[gp])
                                lvh[kc] = p.op("v", lambda e, gp=gp, w=w:
                                               e.tensor_scalar(cb_5[gp][:], mb_5[gp][:],
                                                               w, w, ALU.is_gt,
                                                               ALU.mult), deps=ddl)
                            for kc in pair:
                                gp = kc & 3
                                lvh[kc] = p.op("v", lambda e, gp=gp:
                                               e.tensor_tensor(mb_5[gp][:], mb_5[gp][:],
                                                               cb_5[gp][:],
                                                               ALU.subtract),
                                               deps=[lvh[kc]])
                        for kc in pair:
                            hh[(kc, "lv")] = lvh[kc]
                        return hh

                    def tail5(kc0, blk, hh):
                        pair = (kc0, kc0 + 1)
                        cmh = {}
                        for kc in pair:
                            gp = kc & 3
                            cmh[kc] = p.op("v", lambda e, gp=gp: e.tensor_single_scalar(
                                cb_5[gp][:], mb_5[gp][:], 0.5, ALU.is_gt),
                                deps=[hh[(kc, "lv")], cb_war[gp]])
                        for thr in (1.5, 2.5, 3.5):
                            for kc in pair:
                                gp = kc & 3
                                cmh[kc] = p.op("v", lambda e, gp=gp, thr=thr:
                                               e.scalar_tensor_tensor(
                                                   cb_5[gp][:], mb_5[gp][:], thr,
                                                   cb_5[gp][:], ALU.is_gt, ALU.add),
                                               deps=[cmh[kc]])
                        for kc in pair:
                            gp = kc & 3
                            fr = p.op("v", lambda e, gp=gp: e.tensor_tensor(
                                mb_5[gp][:], mb_5[gp][:], cb_5[gp][:], ALU.subtract),
                                deps=[cmh[kc]])
                            cb_war[gp] = fr
                            cmh[kc] = fr
                        for kc in pair:
                            gp = kc & 3
                            sfr = p.op("v", lambda e, gp=gp: e.tensor_tensor(
                                mb_5[gp][:], mb_5[gp][:], sg_5[gp][:], ALU.mult),
                                deps=[cmh[kc]])
                            sg_war[gp] = sfr
                            cmh[kc] = sfr
                        for kc in pair:
                            gp = kc & 3
                            rfin = p.op("v", lambda e, gp=gp: e.tensor_tensor(
                                mb_5[gp][:], yq5[gp][:], mb_5[gp][:], ALU.subtract),
                                deps=[cmh[kc]])
                            yq_war[gp] = rfin
                            iq = p.op("v", lambda e, kc=kc, blk=blk, gp=gp:
                                      e.tensor_tensor(
                                          iqT[:, kc * 512:(kc + 1) * 512], mb_5[gp][:],
                                          sif_bc[:, blk * 512:(blk + 1) * 512],
                                          ALU.mult),
                                      deps=[rfin, cf_tok[blk], iq_guard[kc]])
                            iq_tok[kc] = iq
                            mb_war[gp] = iq
                            r0_war[gp] = hh[(kc, "dd")]

                    prev = None
                    for kc0 in range(0, FFC, 2):
                        hh = head5(kc0, blk)
                        if prev is not None:
                            pk, ph = prev
                            tail5(pk, blk, ph)
                        prev = (kc0, hh)
                    pk, ph = prev
                    tail5(pk, blk, ph)
                    if blk == 0:
                        for dc in (0, 1):
                            w2_tok[dc] = p.dma("g", f"w2ld{dc}",
                                               w2_sb[dc][:].rearrange("p (c f) -> p c f", c=FFC),
                                               w2_h.ap()[dc], deps=[])
                    for dc in range(DCN):
                        wp = dc & 1
                        ypar = dc & 1
                        yld = p.dma("q", f"y1ld{ypar}", y1c[ypar][:],
                                    y1_s.ap()[dc, :, gc0:gc0 + 512],
                                    deps=[y1_war5[ypar]])
                        stop = None
                        for kc in range(FFC):
                            dd2 = [iq_tok[kc], w2_tok[wp]]
                            if kc == 0:
                                dd2.append(o5_war[wp])
                            stop = p.op("t", lambda e, kc=kc, wp=wp: e.matmul(
                                o5_ps[wp][:],
                                w2_sb[wp][:, kc * 128:(kc + 1) * 128],
                                iqT[:, kc * 512:(kc + 1) * 512],
                                start=(kc == 0), stop=(kc == FFC - 1),
                                skip_group_check=True), deps=dd2)
                            if dc == DCN - 1:
                                iq_guard[kc] = stop
                        ndc = dc + 2
                        if not (blk == MYB - 1 and ndc >= DCN):
                            nb = blk if ndc < DCN else blk + 1
                            w2_tok[wp] = p.dma("g", f"w2ld{wp}",
                                               w2_sb[wp][:].rearrange("p (c f) -> p c f", c=FFC),
                                               w2_h.ap()[ndc % DCN], deps=[stop])
                        y2 = p.op("v", lambda e, wp=wp, ypar=ypar: e.tensor_tensor(
                            y1c[ypar][:], o5_ps[wp][:], y1c[ypar][:], ALU.add),
                            deps=[stop, yld, y2_war[ypar]])
                        o5_war[wp] = y2
                        y2_war[ypar] = p.dma("q", f"yst{ypar}",
                                             y_h.ap()[dc * 128:(dc + 1) * 128,
                                                      gc0:gc0 + 512],
                                             y1c[ypar][:], deps=[y2])
                        y1_war5[ypar] = y2_war[ypar]
                p.barrier()
                p.flush()
            SB2.close()

        p.barrier()
        p.op("q", lambda e: e.nop(), deps=[], inc=None)
        p.flush()

    return nc


def _own_idx(T, j):
    MYT = T // 2
    return np.concatenate([np.arange((2 * Z + j) * 512, (2 * Z + j + 1) * 512)
                           for Z in range(MYT // 512)])


def _prep_core(inputs, b, j, T):
    x = np.asarray(inputs["L0_attn_hidden_states"][b])
    MYT = T // 2
    idx = _own_idx(T, j)
    x_own = np.ascontiguousarray(x[idx])
    xT = np.ascontiguousarray(x.T).reshape(DCN, 128, T)
    x_ownT = np.ascontiguousarray(x_own.T).reshape(DCN, 128, MYT)

    jj = np.arange(512)[None, :]
    ii = np.arange(128)[:, None]
    trils = [(jj >= ii + off) for off in (0, 128, 256, 384)]
    if j == 0:
        tiles = trils + [np.zeros((128, 512), bool)] * 4
    else:
        tiles = [np.ones((128, 512), bool)] * 4 + trils
    dmask = np.stack([t.astype(ml_dtypes.bfloat16) for t in tiles])

    qb = np.asarray(inputs["L0_attn_q_b_w"])
    cols = [c for h in range(NH) for c in range(h * 192, h * 192 + 128)] + \
           [c for h in range(NH) for c in range(h * 192 + 128, h * 192 + 192)]
    qbw = np.ascontiguousarray(qb[:, cols])
    kva = np.ascontiguousarray(np.asarray(inputs["L0_attn_kv_a_w"]))
    kvb = np.asarray(inputs["L0_attn_kv_b_w"])
    colsb = [c for h in range(NH) for c in range(h * 256, h * 256 + 128)] + \
            [c for h in range(NH) for c in range(h * 256 + 128, h * 256 + 256)]
    kvbw = np.ascontiguousarray(kvb[:, colsb])

    def tile_in(w, n_in_c):
        d_in, f = w.shape
        return np.ascontiguousarray(w.reshape(n_in_c, 128, f).transpose(1, 0, 2))

    qaw = tile_in(np.asarray(inputs["L0_attn_q_a_w"]), DCN)
    qbw_t = tile_in(qbw, QLC)
    kvaw_t = tile_in(kva, DCN)
    kvbw_t = tile_in(kvbw, KLC)

    ow = np.ascontiguousarray(
        np.asarray(inputs["L0_attn_o_w"]).reshape(DCN, 128, DCN, 128)
        .transpose(2, 1, 0, 3))
    w13 = np.asarray(inputs["L0_ffn_w13"])
    w1w = np.ascontiguousarray(
        w13[:FF].T.reshape(DCN, 128, FFC, 128).transpose(2, 1, 0, 3))
    w3w = np.ascontiguousarray(
        w13[FF:].T.reshape(DCN, 128, FFC, 128).transpose(2, 1, 0, 3))
    w2w = np.ascontiguousarray(
        np.asarray(inputs["L0_ffn_w2"]).T.reshape(FFC, 128, DCN, 128)
        .transpose(2, 1, 0, 3))

    gq = np.ascontiguousarray(np.asarray(inputs["L0_attn_q_a_ln_w"]).reshape(QLC, 128))
    gkv = np.ascontiguousarray(np.asarray(inputs["L0_attn_kv_a_ln_w"]).reshape(KLC, 128))
    gf = np.ascontiguousarray(np.asarray(inputs["L0_ffn_norm_w"]).reshape(FFC, 128))

    return {
        "xT": xT, "x_nat": np.ascontiguousarray(x), "x_own": x_own,
        "x_ownT": x_ownT, "dmask": dmask,
        "qaw": qaw, "qbw": qbw_t, "kvaw": kvaw_t, "kvbw": kvbw_t,
        "ow": ow, "w1w": w1w, "w3w": w3w, "w2w": w2w,
        "gq": gq, "gkv": gkv, "gf": gf,
    }


def kernel(**inputs):
    T = int(np.asarray(inputs["L0_attn_hidden_states"]).shape[1])
    B = int(np.asarray(inputs["L0_attn_hidden_states"]).shape[0])
    n_cores = 8
    nc = build_nc(T=T, n_cores=n_cores)
    in_maps = []
    for c in range(n_cores):
        b, j = c // 2, c % 2
        in_maps.append(_prep_core(inputs, b, j, T))
    res = run_bass_kernel_spmd(nc, in_maps, core_ids=list(range(n_cores)))
    out = np.empty((B, T, D), dtype=np.float32)
    for c in range(n_cores):
        b, j = c // 2, c % 2
        out[b, _own_idx(T, j), :] = res.results[c]["y"].T
    return out
